# version 03
# baseline (speedup 1.0000x reference)
import numpy as np
import ml_dtypes
import concourse.bass as bass
import concourse.mybir as mybir
from concourse.bass_utils import run_bass_kernel_spmd
from contextlib import ExitStack

F32 = mybir.dt.float32
BF16 = mybir.dt.bfloat16
AF = mybir.ActivationFunctionType
ALU = mybir.AluOpType
AX = mybir.AxisListType

S, D, H, DH, DHR, LAT = 4096, 1024, 16, 64, 32, 512
DFF, NE = 4096, 8
TAU = float(np.sqrt(96.0))
PLO = float(np.exp(-80.0 / TAU))
PHI = float(np.exp(80.0 / TAU))
NPASS = 8
PT_ = S // NPASS


class Sem:
    def __init__(self, h):
        self.h = h
        self.n = 0


class Prog:
    def __init__(self):
        self.ops = {k: [] for k in ("sync", "gpsimd", "vector", "scalar", "tensor")}

    def op(self, eng, f, waits=(), inc=None, inc_amt=1):
        wl = [(s.h, s.n if tgt is None else tgt) for (s, tgt) in waits]
        if inc is not None:
            inc.n += inc_amt
            pair = (inc.h, inc_amt)
        else:
            pair = None

        def fn(e, wl=wl, f=f, pair=pair):
            for h, v in wl:
                e.wait_ge(h, v)
            inst = f(e)
            if pair is not None:
                inst.then_inc(*pair)
        self.ops[eng].append(fn)


def build(nc):
    x_d = nc.dram_tensor("x", [S, D], F32, kind="ExternalInput").ap()
    xT_d = nc.dram_tensor("xT", [D, S], F32, kind="ExternalInput").ap()
    wdqt_d = nc.dram_tensor("wdqt", [D, LAT], F32, kind="ExternalInput").ap()
    wdkvt_d = nc.dram_tensor("wdkvt", [D, LAT], F32, kind="ExternalInput").ap()
    wkrt_d = nc.dram_tensor("wkrt", [D, DHR], F32, kind="ExternalInput").ap()
    wuqt_d = nc.dram_tensor("wuqt", [LAT, D], F32, kind="ExternalInput").ap()
    wqrt_d = nc.dram_tensor("wqrt", [LAT, LAT], F32, kind="ExternalInput").ap()
    wukt_d = nc.dram_tensor("wukt", [LAT, D], F32, kind="ExternalInput").ap()
    wuvt_d = nc.dram_tensor("wuvt", [LAT, D], F32, kind="ExternalInput").ap()
    wot_d = nc.dram_tensor("wot", [D, D], F32, kind="ExternalInput").ap()
    w1_d = nc.dram_tensor("w1", [NE, D, DFF], F32, kind="ExternalInput").ap()
    w2_d = nc.dram_tensor("w2", [NE, DFF, D], F32, kind="ExternalInput").ap()
    b1t_d = nc.dram_tensor("b1t", [128, NE * 32], F32, kind="ExternalInput").ap()
    b2_d = nc.dram_tensor("b2", [NE, D], F32, kind="ExternalInput").ap()
    wgt_d = nc.dram_tensor("wgt", [128, 64], F32, kind="ExternalInput").ap()
    bg_d = nc.dram_tensor("bg", [NE, 1], F32, kind="ExternalInput").ap()
    ropec_d = nc.dram_tensor("ropec", [DHR, S], BF16, kind="ExternalInput").ap()
    ropes_d = nc.dram_tensor("ropes", [DHR, S], BF16, kind="ExternalInput").ap()
    masks_d = nc.dram_tensor("masks", [128, 4 * 512], BF16, kind="ExternalInput").ap()
    swp_d = nc.dram_tensor("swp", [DHR, DHR], BF16, kind="ExternalInput").ap()
    id8_d = nc.dram_tensor("id8", [8, 8], F32, kind="ExternalInput").ap()
    id128_d = nc.dram_tensor("id128", [128, 128], F32, kind="ExternalInput").ap()
    ones64_d = nc.dram_tensor("ones64", [1, 64], F32, kind="ExternalInput").ap()
    consts_d = nc.dram_tensor("consts", [128, 4], F32, kind="ExternalInput").ap()

    out_d = nc.dram_tensor("out", [S, D], F32, kind="ExternalOutput").ap()

    w1bf_d = nc.dram_tensor("w1bf", [NE, D, DFF], BF16, kind="Internal").ap()
    w2bf_d = nc.dram_tensor("w2bf", [NE, DFF, D], BF16, kind="Internal").ap()
    x1_d = nc.dram_tensor("x1f", [S, D], F32, kind="Internal").ap()
    x1bf_d = nc.dram_tensor("x1bf", [S, D], BF16, kind="Internal").ap()

    P = Prog()
    st = ExitStack()
    sb = lambda n, s, d: st.enter_context(nc.sbuf_tensor(n, s, d))
    psal = lambda n, s: st.enter_context(nc.psum_tensor(n, s, F32))
    sem = lambda n: Sem(st.enter_context(nc.semaphore(n)))
    s_ld = sem("s_ld")
    s_gp = sem("s_gp")
    s_pe = sem("s_pe")
    s_sc = sem("s_sc")
    s_ve = sem("s_ve")
    s_st = sem("s_st")

    consts = sb("consts_s", [128, 4], F32)
    id8 = sb("id8_s", [8, 8], F32)
    id128 = sb("id128_s", [128, 128], F32)
    ones64 = sb("ones64_s", [1, 64], F32)
    wg_sb = sb("wg_sb", [128, 64], BF16)
    bg_sb = sb("bg_sb", [NE, 1], F32)
    swp_sb = sb("swp_sb", [DHR, DHR], BF16)

    def dma(eng, out, in_, semc, waits=(), **kw):
        P.op(eng, lambda e, o=out, i=in_, kw=kw: e.dma_start(out=o, in_=i, **kw),
             waits=waits, inc=semc, inc_amt=16)

    def ch_in(dram_ap):
        return dram_ap.rearrange("(k p) t -> p k t", p=128)

    def ch_out(sb_t, inner):
        return sb_t[:].rearrange("p (k t) -> p k t", t=inner)

    dma("sync", consts[:], consts_d, s_ld)
    dma("sync", id8[:], id8_d, s_ld)
    dma("sync", id128[:], id128_d, s_ld)
    dma("sync", ones64[:], ones64_d, s_ld)
    dma("sync", bg_sb[:], bg_d, s_ld)
    dma("sync", swp_sb[:], swp_d, s_ld)
    SC_SQRT = consts[:, 0:1]
    SC_EPS = consts[:, 1:2]
    SC_NHALF = consts[:, 2:3]

    stA = ExitStack()
    sba = lambda n, sh, d: stA.enter_context(nc.sbuf_tensor(n, sh, d))
    cqt = sba("cqt", [128, 4 * S], BF16)
    ckvt = sba("ckvt", [128, 4 * S], BF16)
    krt = sba("krt", [DHR, S], BF16)
    tabC = sba("tabC", [DHR, S], BF16)
    tabS = sba("tabS", [DHR, S], BF16)
    masks = sba("masks_s", [128, 4 * 512], BF16)
    rstg = sba("rstg", [DHR, 512], BF16)
    rY = sba("rY", [DHR, 512], BF16)
    rYs = sba("rYs", [DHR, 512], BF16)
    rC = sba("rC", [DHR, 512], BF16)

    dma("sync", tabC[:], ropec_d, s_ld)
    dma("sync", tabS[:], ropes_d, s_ld)
    dma("sync", masks[:], masks_d, s_ld)
    dma("gpsimd", wg_sb[:], wgt_d, s_gp)

    psA = psal("psA", [128, 512])
    psB = psal("psB", [128, 512])
    ps_sT = [psal("ps_sT0", [128, 1024]), psal("ps_sT1", [128, 1024])]
    ps_o = psal("ps_o", [96, 512])
    ps_zb = psal("ps_zb", [64, 512])

    ld_base = s_ld.n

    last_ev = {}

    def mm(out, lhsT, rhs, start, stop, waits=(), inc=None):
        P.op("tensor",
             lambda e, o=out, a=lhsT, b=rhs, s1=start, s2=stop:
             e.matmul(o, a, b, start=s1, stop=s2), waits=waits, inc=inc)

    def act(out, in_, func, bias=0.0, scale=1.0, waits=(), inc=None):
        P.op("scalar",
             lambda e, o=out, i=in_, f=func, b=bias, s=scale:
             e.activation(out=o, in_=i, func=f, bias=b, scale=s),
             waits=waits, inc=inc)

    def evict(dst, src_buf, src_ap, func=AF.Copy, bias=0.0, extra_waits=()):
        act(dst, src_ap, func, bias=bias,
            waits=[(s_pe, None)] + list(extra_waits), inc=s_sc)
        last_ev[id(src_buf)] = s_sc.n

    def buf_wait(buf):
        return [(s_sc, last_ev.get(id(buf), 0))]

    stp = ExitStack()
    xt_sb = stp.enter_context(nc.sbuf_tensor("xt_sb", [128, 8 * S], BF16))
    wdqt = stp.enter_context(nc.sbuf_tensor("wdqt_s", [128, 8 * LAT], BF16))
    wdkvt = stp.enter_context(nc.sbuf_tensor("wdkvt_s", [128, 8 * LAT], BF16))
    wkrt = stp.enter_context(nc.sbuf_tensor("wkrt_s", [128, 8 * DHR], BF16))

    dma("gpsimd", ch_out(xt_sb, S), ch_in(xT_d), s_gp)
    dma("gpsimd", ch_out(wdqt, LAT), ch_in(wdqt_d), s_gp)
    dma("gpsimd", ch_out(wdkvt, LAT), ch_in(wdkvt_d), s_gp)
    dma("gpsimd", ch_out(wkrt, DHR), ch_in(wkrt_d), s_gp)
    gp_proj = s_gp.n

    pp = [psA, psB]
    pi = [0]

    def nbuf():
        b = pp[pi[0] % 2]
        pi[0] += 1
        return b

    w0 = [(s_gp, gp_proj), (s_ld, ld_base)]

    def proj128(dst, dst_col0, wt, wt_cols, row0, M):
        for n in range(S // 512):
            buf = nbuf()
            for dk in range(8):
                w = (w0 + buf_wait(buf)) if dk == 0 else ()
                mm(buf[0:M, :], wt[:, dk * wt_cols + row0: dk * wt_cols + row0 + M],
                   xt_sb[:, dk * S + n * 512: dk * S + (n + 1) * 512],
                   dk == 0, dk == 7, waits=w, inc=s_pe if dk == 7 else None)
            evict(dst[:, dst_col0 + n * 512: dst_col0 + (n + 1) * 512],
                  buf, buf[0:M, :])

    for lm in range(4):
        proj128(cqt, lm * S, wdqt, LAT, lm * 128, 128)
    for lm in range(4):
        proj128(ckvt, lm * S, wdkvt, LAT, lm * 128, 128)


    def rope_chunk(dst, tab_c, buf):
        buf2 = nbuf()
        evict(rstg[:], buf, buf[0:DHR, :], extra_waits=[(s_ve, None)])
        P.op("vector", lambda e, o=rY[:], a=rstg[:], b=tabS[:, tab_c:tab_c + 512]:
             e.tensor_mul(o, a, b), waits=[(s_sc, None), (s_ld, ld_base)], inc=s_ve)
        mm(buf2[0:DHR, :], swp_sb[:], rY[:], True, True,
           waits=[(s_ve, None)] + buf_wait(buf2), inc=s_pe)
        evict(rYs[:], buf2, buf2[0:DHR, :])
        P.op("vector", lambda e, o=rC[:], a=rstg[:], b=tabC[:, tab_c:tab_c + 512]:
             e.tensor_mul(o, a, b))
        P.op("vector", lambda e, o=dst, a=rC[:], b=rYs[:]:
             e.tensor_add(o, a, b), waits=[(s_sc, None)], inc=s_ve)

    for n in range(S // 512):
        buf = nbuf()
        for dk in range(8):
            w = (w0 + buf_wait(buf)) if dk == 0 else ()
            mm(buf[0:DHR, :], wkrt[:, dk * DHR:(dk + 1) * DHR],
               xt_sb[:, dk * S + n * 512: dk * S + (n + 1) * 512],
               dk == 0, dk == 7, waits=w, inc=s_pe if dk == 7 else None)
        rope_chunk(krt[:, n * 512:(n + 1) * 512], n * 512, buf)

    pe_projdone = [s_pe.n]
    stp.close()
    ot_sb = sba("ot_sb", [128, 8 * 2048], BF16)

    x1st_cnt = [0]
    conv_emitted = [False]
    gp_conv = [0]

    def ln_emit(X, stt, out_f32, scratch, store_wait=None):
        sum1 = stt[:, 0:1]; ss = stt[:, 1:2]; mean = stt[:, 2:3]
        mvar = stt[:, 3:4]; m2 = stt[:, 4:5]; var = stt[:, 5:6]
        lnv = stt[:, 6:7]; rstd = stt[:, 7:8]; nb = stt[:, 8:9]
        P.op("vector", lambda e, o=scratch, a=X[:], b=ss:
             e.scalar_tensor_tensor(o, a, 1.0, a, ALU.mult, ALU.mult, accum_out=b))
        P.op("vector", lambda e, o=sum1, i=X[:]: e.reduce_sum(o, i, axis=AX.X),
             inc=s_ve)
        ve_scr = s_ve.n
        P.op("vector", lambda e: e.drain())
        P.op("vector", lambda e, o=mean, i=sum1:
             e.tensor_scalar(o, i, 1.0 / D, None, ALU.mult))
        P.op("vector", lambda e, o=mvar, i=ss:
             e.tensor_scalar(o, i, 1.0 / D, None, ALU.mult))
        P.op("vector", lambda e: e.drain())
        P.op("vector", lambda e, o=m2, a=mean, b=mean: e.tensor_mul(o, a, b))
        P.op("vector", lambda e: e.drain())
        P.op("vector", lambda e, o=var, a=mvar, b=m2: e.tensor_sub(o, a, b),
             inc=s_ve)
        act(lnv, var, AF.Ln, bias=SC_EPS, waits=[(s_ve, None)])
        act(rstd, lnv, AF.Exp, scale=SC_NHALF, inc=s_sc)
        P.op("vector", lambda e, o=nb, i=rstd, m=mean:
             e.tensor_scalar(o, i, m, -1.0, ALU.mult, ALU.mult),
             waits=[(s_sc, None)])
        P.op("vector", lambda e: e.drain(), inc=s_ve)
        aw = [(s_ve, None)]
        if store_wait is not None:
            aw.append(store_wait)
        act(out_f32[:], X[:], AF.Identity, bias=nb, scale=rstd, waits=aw, inc=s_sc)
        return ve_scr, s_sc.n

    def run_half(half):
        kj_end = 2048 * (half + 1)
        nkt = kj_end // 128
        qi0 = half * 2048

        sth = ExitStack()
        wuqt = sth.enter_context(nc.sbuf_tensor("wuqt_s%d" % half, [128, 4 * D], BF16))
        wqrt = sth.enter_context(nc.sbuf_tensor("wqrt_s%d" % half, [128, 4 * LAT], BF16))
        wukt = sth.enter_context(nc.sbuf_tensor("wukt_s%d" % half, [128, 4 * D], BF16))
        wuvt = sth.enter_context(nc.sbuf_tensor("wuvt_s%d" % half, [128, 4 * D], BF16))
        kct = sth.enter_context(nc.sbuf_tensor("kct%d" % half, [64, S], BF16))
        qct = sth.enter_context(nc.sbuf_tensor("qct%d" % half, [64, 2048], BF16))
        qrt = sth.enter_context(nc.sbuf_tensor("qrt%d" % half, [DHR, 2048], BF16))
        vaug = sth.enter_context(nc.sbuf_tensor("vaug%d" % half, [128, 32 * 96], BF16))
        Ebuf = [sth.enter_context(nc.sbuf_tensor("E%d_%d" % (half, i), [128, 1024], BF16))
                for i in range(2)]
        Pbuf = [sth.enter_context(nc.sbuf_tensor("P%d_%d" % (half, i), [128, 1024], BF16))
                for i in range(2)]
        o_f32 = sth.enter_context(nc.sbuf_tensor("of%d" % half, [64, 512], F32))
        zrow = sth.enter_context(nc.sbuf_tensor("zr%d" % half, [1, 512], F32))
        rz = sth.enter_context(nc.sbuf_tensor("rz%d" % half, [1, 512], F32))

        if half == 0:
            wq_wait = [(s_gp, gp_proj), (s_pe, pe_projdone[0])]
        else:
            wq_wait = [(s_st, None)]
        dma("gpsimd", ch_out(wuqt, D), ch_in(wuqt_d),
            s_gp, waits=wq_wait)
        dma("gpsimd", ch_out(wqrt, LAT), ch_in(wqrt_d), s_gp)
        dma("gpsimd", ch_out(wukt, D), ch_in(wukt_d), s_gp)
        dma("gpsimd", ch_out(wuvt, D), ch_in(wuvt_d), s_gp)
        gp_wu = s_gp.n

        if not conv_emitted[0]:
            for e in range(NE):
                dma("gpsimd", w1bf_d[e], w1_d[e], s_gp)
                dma("gpsimd", w2bf_d[e], w2_d[e], s_gp)
            gp_conv[0] = s_gp.n
            conv_emitted[0] = True

        P.op("vector", lambda e: e.memset(
            vaug[:].rearrange("p (b c) -> p b c", c=96)[:, :, 64:96], 0.0),
            waits=([(s_pe, pe_projdone[0])] if half == 0 else [(s_st, None)]))
        P.op("vector", lambda e: e.memset(
            vaug[:].rearrange("p (b c) -> p b c", c=96)[:, :, 64:65], 1.0),
            inc=s_ve)
        ve_vinit = s_ve.n

        for h in range(H):
            for n in range(4):
                buf = nbuf()
                for lk in range(4):
                    w = ([(s_gp, gp_wu)] + buf_wait(buf)) if lk == 0 else ()
                    mm(buf[0:64, :], wuqt[:, lk * D + h * 64: lk * D + h * 64 + 64],
                       cqt[:, lk * S + qi0 + n * 512: lk * S + qi0 + (n + 1) * 512],
                       lk == 0, lk == 3, waits=w, inc=s_pe if lk == 3 else None)
                evict(qct[:, n * 512:(n + 1) * 512], buf, buf[0:64, :])
            for n in range(4):
                buf = nbuf()
                for lk in range(4):
                    w = buf_wait(buf) if lk == 0 else ()
                    mm(buf[0:DHR, :],
                       wqrt[:, lk * LAT + h * DHR: lk * LAT + h * DHR + DHR],
                       cqt[:, lk * S + qi0 + n * 512: lk * S + qi0 + (n + 1) * 512],
                       lk == 0, lk == 3, waits=w, inc=s_pe if lk == 3 else None)
                rope_chunk(qrt[:, n * 512:(n + 1) * 512], qi0 + n * 512, buf)
            for n in range(kj_end // 512):
                buf = nbuf()
                for lk in range(4):
                    w = buf_wait(buf) if lk == 0 else ()
                    mm(buf[0:64, :], wukt[:, lk * D + h * 64: lk * D + h * 64 + 64],
                       ckvt[:, lk * S + n * 512: lk * S + (n + 1) * 512],
                       lk == 0, lk == 3, waits=w, inc=s_pe if lk == 3 else None)
                evict(kct[:, n * 512:(n + 1) * 512], buf, buf[0:64, :])
            for t4 in range(nkt // 4):
                buf = nbuf()
                for sub in range(4):
                    tt = t4 * 4 + sub
                    for lk in range(4):
                        w = ()
                        if sub == 0 and lk == 0:
                            w = buf_wait(buf) + [(s_ve, ve_vinit)]
                        mm(buf[:, sub * 64:(sub + 1) * 64],
                           ckvt[:, lk * S + tt * 128: lk * S + (tt + 1) * 128],
                           wuvt[:, lk * D + h * 64: lk * D + h * 64 + 64],
                           lk == 0, lk == 3, waits=w,
                           inc=s_pe if (sub == 3 and lk == 3) else None)
                for sub in range(4):
                    tt = t4 * 4 + sub
                    act(vaug[:, tt * 96: tt * 96 + 64], buf[:, sub * 64:(sub + 1) * 64],
                        AF.Copy, waits=[(s_pe, None)] if sub == 0 else (),
                        inc=s_sc if sub == 3 else None)
                last_ev[id(buf)] = s_sc.n
            sc_head = s_sc.n
            ve_head = s_ve.n

            for cl in range(4):
                gc = half * 4 + cl
                ktiles = 4 * gc + 4
                qsl = slice(cl * 512, (cl + 1) * 512)
                exp_done = {}
                pt_done = {}
                pv_done = {}
                for g in range(ktiles // 2):
                    bi = g % 2
                    for sub in range(2):
                        j = 2 * g + sub
                        w = []
                        if sub == 0:
                            w = [(s_sc, sc_head), (s_ve, ve_head), (s_ld, ld_base)]
                            if g >= 2:
                                w.append((s_sc, exp_done[g - 2]))
                        mm(ps_sT[bi][:, sub * 512:(sub + 1) * 512],
                           kct[:, j * 128:(j + 1) * 128], qct[:, qsl],
                           True, False, waits=w)
                        mm(ps_sT[bi][:, sub * 512:(sub + 1) * 512],
                           krt[:, j * 128:(j + 1) * 128], qrt[:, qsl],
                           False, True, inc=s_pe if sub == 1 else None)
                    we = [(s_pe, None), (s_ld, ld_base)]
                    if g >= 2:
                        we.append((s_ve, pt_done[g - 2]))
                    act(Ebuf[bi][:], ps_sT[bi][:], AF.Exp, scale=SC_SQRT,
                        waits=we, inc=s_sc)
                    exp_done[g] = s_sc.n
                    wv = [(s_sc, None)]
                    if g >= 2:
                        wv.append((s_pe, pv_done[g - 2]))
                    vops = [lambda e, o=Pbuf[bi][:], i=Ebuf[bi][:]:
                            e.tensor_scalar(o, i, PLO, PHI, ALU.max, ALU.min)]
                    for sub in range(2):
                        j = 2 * g + sub
                        r = j - 4 * gc
                        if r >= 0:
                            vops.append(lambda e, o=Pbuf[bi][:, sub * 512:(sub + 1) * 512],
                                        m=masks[:, r * 512:(r + 1) * 512]:
                                        e.tensor_mul(o, o, m))
                    for k, vo in enumerate(vops):
                        P.op("vector", vo, waits=wv if k == 0 else (),
                             inc=s_ve if k == len(vops) - 1 else None)
                    pt_done[g] = s_ve.n
                    for sub in range(2):
                        j = 2 * g + sub
                        w = [(s_ve, pt_done[g])] if sub == 0 else []
                        if sub == 0 and g == 0:
                            w.append((s_ve, None))
                            w.append((s_sc, None))
                        mm(ps_o[:], vaug[:, j * 96:(j + 1) * 96],
                           Pbuf[bi][:, sub * 512:(sub + 1) * 512],
                           j == 0, j == ktiles - 1,
                           waits=w, inc=s_pe if j == ktiles - 1 else None)
                    pv_done[g] = s_pe.n
                P.op("vector", lambda e, o=zrow[:], i=ps_o[64:65, :]:
                     e.tensor_copy(out=o, in_=i), waits=[(s_pe, None)])
                P.op("vector", lambda e: e.drain())
                P.op("vector", lambda e, o=rz[:], i=zrow[:]: e.reciprocal(o, i),
                     inc=s_ve)
                mm(ps_zb[0:64, :], ones64[:], rz[:], True, True,
                   waits=[(s_ve, None), (s_ld, ld_base)], inc=s_pe)
                act(o_f32[:], ps_o[0:64, :], AF.Copy, waits=[(s_pe, None)],
                    inc=s_sc)
                P.op("vector", lambda e,
                     o=ot_sb[64 * (h % 2):64 * (h % 2) + 64,
                             (h // 2) * 2048 + cl * 512:(h // 2) * 2048 + (cl + 1) * 512],
                     a=o_f32[:], b=ps_zb[0:64, :]:
                     e.tensor_mul(o, a, b),
                     waits=[(s_sc, None), (s_pe, None)], inc=s_ve)

        pe_heads = s_pe.n
        ve_heads = s_ve.n
        sth.close()

        stw = ExitStack()
        wot = stw.enter_context(nc.sbuf_tensor("wot_s%d" % half, [128, 8 * D], BF16))
        resid = stw.enter_context(nc.sbuf_tensor("res%d" % half, [128, 2 * D], F32))
        Xb = [stw.enter_context(nc.sbuf_tensor("X%d_%d" % (half, i), [128, D], F32))
              for i in range(2)]
        x1o = [stw.enter_context(nc.sbuf_tensor("x1o%d_%d" % (half, i), [128, D], F32))
               for i in range(2)]
        stats = [stw.enter_context(nc.sbuf_tensor("stt%d_%d" % (half, i), [128, 10], F32))
                 for i in range(2)]

        dma("gpsimd", ch_out(wot, D), ch_in(wot_d), s_gp,
            waits=[(s_ve, ve_heads)])
        gp_wot = s_gp.n

        add_done = [0, 0]
        scr_done = [0, 0]
        store_done = [0, 0]
        for tt in range(16):
            bi = tt % 2
            row0 = half * 2048 + tt * 128
            dma("sync", resid[:, bi * D:(bi + 1) * D], x_d[row0:row0 + 128, :],
                s_ld,
                waits=[(s_ve, scr_done[bi])] if tt >= 2 else [(s_ve, ve_heads)])
            ld_res = s_ld.n
            pw = ps_sT[bi]
            for ok in range(8):
                for dc in range(2):
                    w = []
                    if ok == 0 and dc == 0:
                        w = [(s_gp, gp_wot), (s_pe, pe_heads), (s_ve, ve_heads)]
                        if tt >= 2:
                            w.append((s_ve, add_done[bi]))
                    mm(pw[:, dc * 512:(dc + 1) * 512],
                       ot_sb[:, ok * 2048 + tt * 128: ok * 2048 + (tt + 1) * 128],
                       wot[:, ok * D + dc * 512: ok * D + (dc + 1) * 512],
                       ok == 0, ok == 7, waits=w,
                       inc=s_pe if (ok == 7 and dc == 1) else None)
            X = Xb[bi]
            P.op("vector", lambda e, o=X[:], a=pw[:], b=resid[:, bi * D:(bi + 1) * D]:
                 e.tensor_add(o, a, b), waits=[(s_pe, None), (s_ld, ld_res)],
                 inc=s_ve)
            add_done[bi] = s_ve.n
            scr, _ = ln_emit(X, stats[bi], x1o[bi], resid[:, bi * D:(bi + 1) * D],
                             store_wait=(s_st, store_done[bi]))
            scr_done[bi] = scr
            P.op("scalar", lambda e, o=x1_d[row0:row0 + 128, :], i=x1o[bi][:]:
                 e.dma_start(out=o, in_=i), waits=[(s_sc, None)],
                 inc=s_st, inc_amt=16)
            store_done[bi] = s_st.n
            x1st_cnt.append(s_st.n)
        stw.close()

    run_half(0)
    x1bf_cnt = [0] * NPASS
    for p in range(4):
        dma("gpsimd", x1bf_d[p * PT_:(p + 1) * PT_, :],
            x1_d[p * PT_:(p + 1) * PT_, :], s_gp,
            waits=[(s_st, x1st_cnt[4 * (p + 1)])])
        x1bf_cnt[p] = s_gp.n
    run_half(1)
    att_st_done = s_st.n
    stA.close()
    for p in range(4, NPASS):
        dma("gpsimd", x1bf_d[p * PT_:(p + 1) * PT_, :],
            x1_d[p * PT_:(p + 1) * PT_, :], s_gp,
            waits=[(s_st, x1st_cnt[4 * (p + 1)])])
        x1bf_cnt[p] = s_gp.n

    stm = ExitStack()
    w1sb = stm.enter_context(nc.sbuf_tensor("w1sb", [128, 8 * DFF], BF16))
    w2sb = stm.enter_context(nc.sbuf_tensor("w2sb", [128, 32 * D], BF16))
    h_sb = stm.enter_context(nc.sbuf_tensor("h_sb", [128, 32 * 512], BF16))
    y_acc = stm.enter_context(nc.sbuf_tensor("y_acc", [128, 4 * D], F32))
    x1t = stm.enter_context(nc.sbuf_tensor("x1t_s", [128, 8 * 512], BF16))
    lg_t = stm.enter_context(nc.sbuf_tensor("lg_t", [128, 32], F32))
    combine = stm.enter_context(nc.sbuf_tensor("combine", [128, 32], F32))
    combT = stm.enter_context(nc.sbuf_tensor("combT", [8, 512], F32))
    logT = stm.enter_context(nc.sbuf_tensor("logT", [8, 512], F32))
    mres = stm.enter_context(nc.sbuf_tensor("mres", [128, D], F32))
    mout = stm.enter_context(nc.sbuf_tensor("mout", [128, D], F32))
    mstat = stm.enter_context(nc.sbuf_tensor("mstat", [128, 10], F32))
    rstat = stm.enter_context(nc.sbuf_tensor("rstat", [128, 32], F32))
    wexp_t = stm.enter_context(nc.sbuf_tensor("wexp_t", [128, 32], F32))
    wsm_t = stm.enter_context(nc.sbuf_tensor("wsm_t", [128, 32], F32))
    b2_sb = stm.enter_context(nc.sbuf_tensor("b2_sb", [NE, D], F32))
    b1_sb = stm.enter_context(nc.sbuf_tensor("b1_sb", [128, NE * 32], F32))

    ps_h = [psA, psB]
    ps_ya = [ps_sT[0], ps_sT[1]]

    mm1_done = [s_pe.n]
    mm2_done = [s_pe.n]
    w1_ld = [0]
    w2_ld = [0]

    dma("sync", b2_sb[:], b2_d, s_ld, waits=[(s_st, att_st_done)])
    dma("sync", b1_sb[:], b1t_d, s_ld)
    dma("sync", ch_out(w1sb, DFF), ch_in(w1bf_d[0]), s_ld,
        waits=[(s_gp, gp_conv[0])])
    w1_ld[-1] = s_ld.n
    dma("sync", ch_out(w2sb, D), ch_in(w2bf_d[0]), s_ld)
    w2_ld[-1] = s_ld.n
    ld_moe_consts = s_ld.n

    ve_yacc_scr = 0
    for p in range(NPASS):
        for k in range(8):
            w = []
            if k == 0:
                w = [(s_gp, x1bf_cnt[p]), (s_pe, mm1_done[-1]), (s_st, att_st_done)]
            dma("sync", x1t[:, k * 512:(k + 1) * 512],
                x1bf_d[p * PT_:(p + 1) * PT_, k * 128:(k + 1) * 128],
                s_ld, waits=w, transpose=True)
        ld_x1t = s_ld.n

        rt = ps_ya[0]
        for dk in range(8):
            w = [(s_ld, ld_x1t), (s_ve, None)] if dk == 0 else []
            mm(rt[0:8, 0:512], wg_sb[:, dk * 8:(dk + 1) * 8],
               x1t[:, dk * 512:(dk + 1) * 512], dk == 0, dk == 7,
               waits=w, inc=s_pe if dk == 7 else None)
        act(logT[:], rt[0:8, 0:512], AF.Identity, bias=bg_sb[:],
            waits=[(s_pe, None), (s_ld, ld_base)], inc=s_sc)
        for q4 in range(4):
            P.op("tensor", lambda e, o=rt[:, 512:520],
                 i=logT[:, q4 * 128:(q4 + 1) * 128], idt=id8[:]:
                 e.transpose(o, i, idt), waits=[(s_sc, None)], inc=s_pe)
            act(lg_t[:, q4 * 8:(q4 + 1) * 8], rt[:, 512:520], AF.Copy,
                waits=[(s_pe, None)], inc=s_sc)
        LG = [lg_t[:, q * 8:(q + 1) * 8] for q in range(4)]
        CMB = [combine[:, q * 8:(q + 1) * 8] for q in range(4)]
        WX = [wexp_t[:, q * 8:(q + 1) * 8] for q in range(4)]
        WS = [wsm_t[:, q * 8:(q + 1) * 8] for q in range(4)]
        ST = lambda j, q: rstat[:, 4 * j + q:4 * j + q + 1]
        dr = lambda inc=None: P.op("vector", lambda e: e.drain(), inc=inc)
        for q in range(4):
            P.op("vector", lambda e, o=ST(0, q), i=LG[q]:
                 e.reduce_max(o, i, axis=AX.X, negate=True),
                 waits=[(s_sc, None)] if q == 0 else (),
                 inc=s_ve if q == 3 else None)
        for q in range(4):
            act(WX[q], LG[q], AF.Exp, bias=ST(0, q),
                waits=[(s_ve, None)] if q == 0 else (),
                inc=s_sc if q == 3 else None)
        for q in range(4):
            P.op("vector", lambda e, o=ST(1, q), i=WX[q]:
                 e.reduce_sum(o, i, axis=AX.X),
                 waits=[(s_sc, None)] if q == 0 else ())
        dr()
        for q in range(4):
            P.op("vector", lambda e, o=ST(2, q), i=ST(1, q): e.reciprocal(o, i))
        dr()
        for q in range(4):
            P.op("vector", lambda e, o=WS[q], i=WX[q], sc=ST(2, q):
                 e.tensor_scalar(o, i, sc, None, ALU.mult))
        dr()
        for q in range(4):
            P.op("vector", lambda e, o=ST(3, q), i=WS[q]: e.reduce_max(o, i, axis=AX.X))
        dr()
        for q in range(4):
            P.op("vector", lambda e, o=CMB[q], i=WS[q], sc=ST(3, q):
                 e.tensor_scalar(o, i, sc, None, ALU.is_lt))
        dr()
        for q in range(4):
            P.op("vector", lambda e, o=CMB[q], a=WS[q], b=CMB[q]: e.tensor_mul(o, a, b))
        dr()
        for q in range(4):
            P.op("vector", lambda e, o=ST(4, q), i=CMB[q]: e.reduce_max(o, i, axis=AX.X))
        dr()
        for q in range(4):
            P.op("vector", lambda e, o=ST(5, q), a=ST(3, q), b=ST(4, q):
                 e.tensor_add(o, a, b))
        for q in range(4):
            P.op("vector", lambda e, o=CMB[q], i=WS[q], sc=ST(4, q):
                 e.tensor_scalar(o, i, sc, None, ALU.is_ge))
        dr()
        for q in range(4):
            P.op("vector", lambda e, o=ST(6, q), i=ST(5, q): e.reciprocal(o, i))
        for q in range(4):
            P.op("vector", lambda e, o=CMB[q], a=WS[q], b=CMB[q]: e.tensor_mul(o, a, b))
        dr()
        for q in range(4):
            P.op("vector", lambda e, o=CMB[q], i=CMB[q], sc=ST(6, q):
                 e.tensor_scalar(o, i, sc, None, ALU.mult),
                 inc=s_ve if q == 3 else None)
        ve_comb = s_ve.n
        for q4 in range(4):
            P.op("tensor", lambda e, o=rt[0:8, 512:640],
                 i=combine[:, q4 * 8:(q4 + 1) * 8], idt=id128[:]:
                 e.transpose(o, i, idt),
                 waits=[(s_ve, ve_comb), (s_sc, None)], inc=s_pe)
            act(combT[:, q4 * 128:(q4 + 1) * 128], rt[0:8, 512:640], AF.Copy,
                waits=[(s_pe, None)], inc=s_sc)
        for tm in range(4):
            for dc in range(2):
                w = [(s_sc, None), (s_ld, ld_moe_consts)] if dc == 0 else ()
                mm(rt[:, dc * 512:(dc + 1) * 512],
                   combT[:, tm * 128:(tm + 1) * 128],
                   b2_sb[:, dc * 512:(dc + 1) * 512], True, True,
                   waits=w, inc=s_pe if dc == 1 else None)
            act(y_acc[:, tm * D:(tm + 1) * D], rt[:], AF.Copy,
                waits=[(s_pe, None), (s_ve, ve_yacc_scr)], inc=s_sc)
        sc_yinit = s_sc.n

        h_ev = {}
        for e_ in range(NE):
            if not (p == 0 and e_ == 0):
                dma("sync", ch_out(w1sb, DFF), ch_in(w1bf_d[e_]),
                    s_ld, waits=[(s_pe, mm1_done[-1])])
                w1_ld.append(s_ld.n)
                dma("sync", ch_out(w2sb, D), ch_in(w2bf_d[e_]),
                    s_ld, waits=[(s_pe, mm2_done[-1])])
                w2_ld.append(s_ld.n)
            for fm in range(32):
                buf = ps_h[fm % 2]
                for dk in range(8):
                    w = []
                    if dk == 0:
                        w = [(s_ld, w1_ld[-1]), (s_sc, sc_yinit),
                             (s_sc, h_ev.get((id(buf)), last_ev.get(id(buf), 0)))]
                    mm(buf[:], w1sb[:, dk * DFF + fm * 128: dk * DFF + (fm + 1) * 128],
                       x1t[:, dk * 512:(dk + 1) * 512], dk == 0, dk == 7,
                       waits=w, inc=s_pe if dk == 7 else None)
                act(h_sb[:, fm * 512:(fm + 1) * 512], buf[:], AF.Relu,
                    bias=b1_sb[:, e_ * 32 + fm: e_ * 32 + fm + 1],
                    waits=[(s_pe, None), (s_ld, ld_base)], inc=s_sc)
                h_ev[id(buf)] = s_sc.n
            mm1_done.append(s_pe.n)
            sc_h = s_sc.n
            for half2 in range(2):
                for fk in range(32):
                    for tmi in range(2):
                        tm = half2 * 2 + tmi
                        for dc in range(2):
                            w = []
                            if fk == 0 and tmi == 0 and dc == 0:
                                w = [(s_ld, w2_ld[-1]), (s_sc, sc_h), (s_ve, None)]
                            mm(ps_ya[tmi][:, dc * 512:(dc + 1) * 512],
                               h_sb[:, fk * 512 + tm * 128: fk * 512 + (tm + 1) * 128],
                               w2sb[:, fk * D + dc * 512: fk * D + (dc + 1) * 512],
                               fk == 0, fk == 31, waits=w,
                               inc=s_pe if (fk == 31 and tmi == 1 and dc == 1) else None)
                for tmi in range(2):
                    tm = half2 * 2 + tmi
                    P.op("vector", lambda e, o=y_acc[:, tm * D:(tm + 1) * D],
                         a=ps_ya[tmi][:], s=combine[:, tm * 8 + e_: tm * 8 + e_ + 1]:
                         e.scalar_tensor_tensor(o, a, s, o, ALU.mult, ALU.add),
                         waits=[(s_pe, None), (s_sc, sc_yinit)] if tmi == 0 else [],
                         inc=s_ve)
            mm2_done.append(s_pe.n)

        scr_cnt = 0
        for tm in range(4):
            row0 = p * PT_ + tm * 128
            dma("sync", mres[:], x1_d[row0:row0 + 128, :], s_ld,
                waits=[(s_st, None)])
            P.op("vector", lambda e, o=mout[:], a=y_acc[:, tm * D:(tm + 1) * D],
                 b=mres[:]:
                 e.tensor_add(o, a, b), waits=[(s_ld, None), (s_ve, None)],
                 inc=s_ve)
            scr, _ = ln_emit(mout, mstat, mres, y_acc[:, tm * D:(tm + 1) * D])
            scr_cnt = scr
            P.op("scalar", lambda e, o=out_d[row0:row0 + 128, :], i=mres[:]:
                 e.dma_start(out=o, in_=i), waits=[(s_sc, None)],
                 inc=s_st, inc_amt=16)
        ve_yacc_scr = scr_cnt

    P.op("sync", lambda e: e.nop(), waits=[(s_st, s_st.n)])

    with nc.Block() as block:
        @block.sync
        def _(eng):
            for fn in P.ops["sync"]:
                fn(eng)

        @block.gpsimd
        def _(eng):
            for fn in P.ops["gpsimd"]:
                fn(eng)

        @block.vector
        def _(eng):
            for fn in P.ops["vector"]:
                fn(eng)

        @block.scalar
        def _(eng):
            for fn in P.ops["scalar"]:
                fn(eng)

        @block.tensor
        def _(eng):
            for fn in P.ops["tensor"]:
                fn(eng)

    stm.close()
    st.close()
    return nc


def _consts_np():
    inv_freq = 10000.0 ** (-np.arange(0, DHR, 2, dtype=np.float64) / DHR)
    t = np.arange(S, dtype=np.float64)
    ang = inv_freq[:, None] * t[None, :]
    C = np.repeat(np.cos(ang), 2, axis=0).astype(np.float32)
    sin = np.sin(ang)
    S2 = np.empty((DHR, S), np.float64)
    S2[0::2] = sin
    S2[1::2] = -sin
    swp = np.zeros((DHR, DHR), np.float32)
    for m in range(DHR):
        swp[m ^ 1, m] = 1.0
    masks = np.zeros((128, 4 * 512), np.float32)
    jj = np.arange(128)[:, None]
    ii = np.arange(512)[None, :]
    for r in range(4):
        masks[:, r * 512:(r + 1) * 512] = (jj + 128 * r <= ii).astype(np.float32)
    consts = np.zeros((128, 4), np.float32)
    consts[:, 0] = 1.0 / np.sqrt(96.0)
    consts[:, 1] = 1e-5
    consts[:, 2] = -0.5
    consts[:, 3] = 1.0 / D
    bf = ml_dtypes.bfloat16
    return (C.astype(bf), S2.astype(np.float32).astype(bf), swp.astype(bf),
            masks.astype(bf), consts)


def kernel(L0_attn_x, L0_attn_W_DQ, L0_attn_W_UQ, L0_attn_W_QR, L0_attn_W_DKV,
           L0_attn_W_UK, L0_attn_W_UV, L0_attn_W_KR, L0_attn_W_O,
           L0_ffn_W1, L0_ffn_b1, L0_ffn_W2, L0_ffn_b2, L0_ffn_Wg, L0_ffn_bg):
    nc = bass.Bass("TRN2", target_bir_lowering=False)
    build(nc)

    C, S2, swp, masks, consts = _consts_np()
    asc = np.ascontiguousarray
    common = {
        "wdqt": asc(L0_attn_W_DQ.T), "wdkvt": asc(L0_attn_W_DKV.T),
        "wkrt": asc(L0_attn_W_KR.T), "wuqt": asc(L0_attn_W_UQ.T),
        "wqrt": asc(L0_attn_W_QR.T), "wukt": asc(L0_attn_W_UK.T),
        "wuvt": asc(L0_attn_W_UV.T), "wot": asc(L0_attn_W_O.T),
        "w1": asc(L0_ffn_W1), "w2": asc(L0_ffn_W2),
        "b1t": asc(L0_ffn_b1.reshape(NE, 32, 128).transpose(2, 0, 1).reshape(128, NE * 32)),
        "b2": asc(L0_ffn_b2),
        "wgt": asc(L0_ffn_Wg.reshape(8, 128, 8).transpose(1, 0, 2).reshape(128, 64)),
        "bg": asc(L0_ffn_bg.reshape(NE, 1)),
        "ropec": C, "ropes": S2, "masks": masks, "swp": swp,
        "id8": np.eye(8, dtype=np.float32), "id128": np.eye(128, dtype=np.float32),
        "ones64": np.ones((1, 64), np.float32), "consts": consts,
    }
    in_maps = []
    for c in range(8):
        m = dict(common)
        xc = asc(np.asarray(L0_attn_x[c], dtype=np.float32))
        m["x"] = xc
        m["xT"] = asc(xc.T)
        in_maps.append(m)
    res = run_bass_kernel_spmd(nc, in_maps, core_ids=list(range(8)))
    return np.stack([np.asarray(r["out"], dtype=np.float32) for r in res.results])


# version 05
# speedup vs baseline: 1.1219x; 1.1219x over previous
import numpy as np
import ml_dtypes
import concourse.bass as bass
import concourse.mybir as mybir
from concourse.bass_utils import run_bass_kernel_spmd
from contextlib import ExitStack

F32 = mybir.dt.float32
BF16 = mybir.dt.bfloat16
AF = mybir.ActivationFunctionType
ALU = mybir.AluOpType
AX = mybir.AxisListType

S, D, H, DH, DHR, LAT = 4096, 1024, 16, 64, 32, 512
DFF, NE = 4096, 8
TAU = float(np.sqrt(96.0))
PLO = float(np.exp(-80.0 / TAU))
PHI = float(np.exp(80.0 / TAU))
NPASS = 8
PT_ = S // NPASS
VSL = 260


class Sem:
    def __init__(self, h):
        self.h = h
        self.n = 0


class Prog:
    def __init__(self):
        self.ops = {k: [] for k in ("sync", "gpsimd", "vector", "scalar", "tensor")}

    def op(self, eng, f, waits=(), inc=None, inc_amt=1):
        wl = [(s.h, s.n if tgt is None else tgt) for (s, tgt) in waits]
        if inc is not None:
            inc.n += inc_amt
            pair = (inc.h, inc_amt)
        else:
            pair = None

        def fn(e, wl=wl, f=f, pair=pair):
            for h, v in wl:
                e.wait_ge(h, v)
            inst = f(e)
            if pair is not None:
                inst.then_inc(*pair)
        self.ops[eng].append(fn)


def build(nc):
    x_d = nc.dram_tensor("x", [S, D], F32, kind="ExternalInput").ap()
    xT_d = nc.dram_tensor("xT", [D, S], F32, kind="ExternalInput").ap()
    wdqt_d = nc.dram_tensor("wdqt", [D, LAT], F32, kind="ExternalInput").ap()
    wdkvt_d = nc.dram_tensor("wdkvt", [D, LAT], F32, kind="ExternalInput").ap()
    wkrct_d = nc.dram_tensor("wkrct", [D, 64], F32, kind="ExternalInput").ap()
    wuqt_d = nc.dram_tensor("wuqt", [LAT, D], F32, kind="ExternalInput").ap()
    wqrct_d = nc.dram_tensor("wqrct", [LAT, D], F32, kind="ExternalInput").ap()
    wukt_d = nc.dram_tensor("wukt", [LAT, D], F32, kind="ExternalInput").ap()
    wuvt_d = nc.dram_tensor("wuvt", [LAT, D], F32, kind="ExternalInput").ap()
    wot_d = nc.dram_tensor("wot", [D, D], F32, kind="ExternalInput").ap()
    w1_d = nc.dram_tensor("w1", [NE, D, DFF], F32, kind="ExternalInput").ap()
    w2_d = nc.dram_tensor("w2", [NE, DFF, D], F32, kind="ExternalInput").ap()
    b1t_d = nc.dram_tensor("b1t", [128, NE * 32], F32, kind="ExternalInput").ap()
    b2_d = nc.dram_tensor("b2", [NE, D], F32, kind="ExternalInput").ap()
    wgt_d = nc.dram_tensor("wgt", [128, 64], F32, kind="ExternalInput").ap()
    bg_d = nc.dram_tensor("bg", [NE, 1], F32, kind="ExternalInput").ap()
    ropecs_d = nc.dram_tensor("ropecs", [64, S], BF16, kind="ExternalInput").ap()
    masks_d = nc.dram_tensor("masks", [128, 4 * 512], BF16, kind="ExternalInput").ap()
    id8_d = nc.dram_tensor("id8", [8, 8], F32, kind="ExternalInput").ap()
    id128_d = nc.dram_tensor("id128", [128, 128], F32, kind="ExternalInput").ap()
    ones64_d = nc.dram_tensor("ones64", [1, 64], F32, kind="ExternalInput").ap()
    consts_d = nc.dram_tensor("consts", [128, 4], F32, kind="ExternalInput").ap()

    out_d = nc.dram_tensor("out", [S, D], F32, kind="ExternalOutput").ap()

    w1bf_d = nc.dram_tensor("w1bf", [NE, D, DFF], BF16, kind="Internal").ap()
    w2bf_d = nc.dram_tensor("w2bf", [NE, DFF, D], BF16, kind="Internal").ap()
    x1_d = nc.dram_tensor("x1f", [S, D], F32, kind="Internal").ap()
    x1bf_d = nc.dram_tensor("x1bf", [S, D], BF16, kind="Internal").ap()

    P = Prog()
    st = ExitStack()
    sb = lambda n, s_, d_: st.enter_context(nc.sbuf_tensor(n, s_, d_))
    psal = lambda n, s_: st.enter_context(nc.psum_tensor(n, s_, F32))
    sem = lambda n: Sem(st.enter_context(nc.semaphore(n)))
    s_ld = sem("s_ld")
    s_gp = sem("s_gp")
    s_pe = sem("s_pe")
    s_sc = sem("s_sc")
    s_ve = sem("s_ve")
    s_st = sem("s_st")

    consts = sb("consts_s", [128, 4], F32)
    id8 = sb("id8_s", [8, 8], F32)
    id128 = sb("id128_s", [128, 128], F32)
    ones64 = sb("ones64_s", [1, 64], F32)
    wg_sb = sb("wg_sb", [128, 64], BF16)
    bg_sb = sb("bg_sb", [NE, 1], F32)

    def dma(eng, out, in_, semc, waits=(), **kw):
        P.op(eng, lambda e, o=out, i=in_, kw=kw: e.dma_start(out=o, in_=i, **kw),
             waits=waits, inc=semc, inc_amt=16)

    def ch_in(dram_ap):
        return dram_ap.rearrange("(k p) t -> p k t", p=128)

    def ch_out(sb_t, inner):
        return sb_t[:].rearrange("p (k t) -> p k t", t=inner)

    dma("sync", consts[:], consts_d, s_ld)
    dma("sync", id8[:], id8_d, s_ld)
    dma("sync", id128[:], id128_d, s_ld)
    dma("sync", ones64[:], ones64_d, s_ld)
    dma("sync", bg_sb[:], bg_d, s_ld)
    SC_SQRT = consts[:, 0:1]
    SC_EPS = consts[:, 1:2]
    SC_NHALF = consts[:, 2:3]

    stA = ExitStack()
    sba = lambda n, sh, d_: stA.enter_context(nc.sbuf_tensor(n, sh, d_))
    cqt = sba("cqt", [128, 4 * S], BF16)
    ckvt = sba("ckvt", [128, 4 * S], BF16)
    krt = sba("krt", [DHR, S], BF16)
    tabCS = sba("tabCS", [64, S], BF16)
    masks = sba("masks_s", [128, 4 * 512], BF16)
    qrs = [sba("qrs%d" % i, [64, 512], BF16) for i in range(2)]
    rt1 = sba("rt1", [DHR, 512], BF16)
    rt2 = sba("rt2", [DHR, 512], BF16)

    dma("sync", tabCS[:], ropecs_d, s_ld)
    dma("sync", masks[:], masks_d, s_ld)
    dma("gpsimd", wg_sb[:], wgt_d, s_gp)

    psA = psal("psA", [128, 512])
    psB = psal("psB", [128, 512])
    ps_sT = [psal("ps_sT0", [128, 1024]), psal("ps_sT1", [128, 1024])]
    ps_oc = [psal("ps_o0", [96, 512]), psal("ps_o1", [96, 512])]
    ps_zb = psB

    ld_base = s_ld.n

    last_ev = {}

    def mm(out, lhsT, rhs, start, stop, waits=(), inc=None):
        P.op("tensor",
             lambda e, o=out, a=lhsT, b=rhs, s1=start, s2=stop:
             e.matmul(o, a, b, start=s1, stop=s2), waits=waits, inc=inc)

    def act(out, in_, func, bias=0.0, scale=1.0, waits=(), inc=None):
        P.op("scalar",
             lambda e, o=out, i=in_, f=func, b=bias, s=scale:
             e.activation(out=o, in_=i, func=f, bias=b, scale=s),
             waits=waits, inc=inc)

    pp4 = [(ps_sT[0], 0), (ps_sT[0], 512), (ps_sT[1], 0), (ps_sT[1], 512)]
    pi = [0]

    def nbuf():
        b = pp4[pi[0] % 4]
        pi[0] += 1
        return b

    def buf_wait(key):
        v = last_ev.get(key)
        return [v] if v is not None else []

    def evict(dst, key, src_ap, func=AF.Copy, bias=0.0, extra_waits=()):
        act(dst, src_ap, func, bias=bias,
            waits=[(s_pe, None)] + list(extra_waits), inc=s_sc)
        last_ev[key] = (s_sc, s_sc.n)

    stp = ExitStack()
    xt_sb = stp.enter_context(nc.sbuf_tensor("xt_sb", [128, 8 * S], BF16))
    wdqt = stp.enter_context(nc.sbuf_tensor("wdqt_s", [128, 8 * LAT], BF16))
    wdkvt = stp.enter_context(nc.sbuf_tensor("wdkvt_s", [128, 8 * LAT], BF16))
    wkrct = stp.enter_context(nc.sbuf_tensor("wkrct_s", [128, 8 * 64], BF16))

    dma("gpsimd", ch_out(xt_sb, S), ch_in(xT_d), s_gp)
    dma("gpsimd", ch_out(wdqt, LAT), ch_in(wdqt_d), s_gp)
    dma("gpsimd", ch_out(wdkvt, LAT), ch_in(wdkvt_d), s_gp)
    dma("gpsimd", ch_out(wkrct, 64), ch_in(wkrct_d), s_gp)
    gp_proj = s_gp.n

    w0 = [(s_gp, gp_proj), (s_ld, ld_base)]

    def proj128(dst, dst_col0, wt, wt_cols, row0, M):
        for n in range(S // 512):
            t, off = nbuf()
            key = (id(t), off)
            w = w0 + buf_wait(key)
            for dk in range(8):
                mm(t[0:M, off:off + 512],
                   wt[:, dk * wt_cols + row0: dk * wt_cols + row0 + M],
                   xt_sb[:, dk * S + n * 512: dk * S + (n + 1) * 512],
                   dk == 0, dk == 7, waits=w if dk == 0 else (),
                   inc=s_pe if dk == 7 else None)
            evict(dst[:, dst_col0 + n * 512: dst_col0 + (n + 1) * 512],
                  key, t[0:M, off:off + 512])

    for lm in range(4):
        proj128(cqt, lm * S, wdqt, LAT, lm * 128, 128)
    for lm in range(4):
        proj128(ckvt, lm * S, wdkvt, LAT, lm * 128, 128)

    def rope_emit(dst, stage, tab_c):
        P.op("vector", lambda e, o=rt1[:], a=stage[0:DHR, :],
             b=tabCS[0:DHR, tab_c:tab_c + 512]:
             e.tensor_mul(o, a, b), waits=[(s_sc, None), (s_ld, ld_base)])
        P.op("vector", lambda e, o=rt2[:], a=stage[DHR:64, :],
             b=tabCS[DHR:64, tab_c:tab_c + 512]:
             e.tensor_mul(o, a, b))
        P.op("vector", lambda e, o=dst, a=rt1[:], b=rt2[:]:
             e.tensor_add(o, a, b), inc=s_ve)

    for n in range(S // 512):
        t, off = nbuf()
        key = (id(t), off)
        w = w0 + buf_wait(key) + [(s_ve, None)]
        for dk in range(8):
            mm(t[0:64, off:off + 512], wkrct[:, dk * 64:(dk + 1) * 64],
               xt_sb[:, dk * S + n * 512: dk * S + (n + 1) * 512],
               dk == 0, dk == 7, waits=w if dk == 0 else (),
               inc=s_pe if dk == 7 else None)
        evict(qrs[n % 2][:], key, t[0:64, off:off + 512],
              extra_waits=[(s_ve, None)])
        rope_emit(krt[:, n * 512:(n + 1) * 512], qrs[n % 2], n * 512)

    pe_projdone = s_pe.n
    stp.close()

    ot_sb = sba("ot_sb", [128, 8 * 2048], BF16)

    x1st_cnt = [0]
    conv_emitted = [False]
    gp_conv = [0]

    def ln_emit(X, stt, out_f32, scratch, store_wait=None):
        sum1 = stt[:, 0:1]; ss = stt[:, 1:2]; mean = stt[:, 2:3]
        mvar = stt[:, 3:4]; m2 = stt[:, 4:5]; var = stt[:, 5:6]
        lnv = stt[:, 6:7]; rstd = stt[:, 7:8]; nb = stt[:, 8:9]
        P.op("vector", lambda e, o=scratch, a=X[:], b=ss:
             e.scalar_tensor_tensor(o, a, 1.0, a, ALU.mult, ALU.mult, accum_out=b))
        P.op("vector", lambda e, o=sum1, i=X[:]: e.reduce_sum(o, i, axis=AX.X),
             inc=s_ve)
        ve_scr = s_ve.n
        P.op("vector", lambda e: e.drain())
        P.op("vector", lambda e, o=mean, i=sum1:
             e.tensor_scalar(o, i, 1.0 / D, None, ALU.mult))
        P.op("vector", lambda e, o=mvar, i=ss:
             e.tensor_scalar(o, i, 1.0 / D, None, ALU.mult))
        P.op("vector", lambda e: e.drain())
        P.op("vector", lambda e, o=m2, a=mean, b=mean: e.tensor_mul(o, a, b))
        P.op("vector", lambda e: e.drain())
        P.op("vector", lambda e, o=var, a=mvar, b=m2: e.tensor_sub(o, a, b),
             inc=s_ve)
        act(lnv, var, AF.Ln, bias=SC_EPS, waits=[(s_ve, None)])
        act(rstd, lnv, AF.Exp, scale=SC_NHALF, inc=s_sc)
        P.op("vector", lambda e, o=nb, i=rstd, m=mean:
             e.tensor_scalar(o, i, m, -1.0, ALU.mult, ALU.mult),
             waits=[(s_sc, None)])
        P.op("vector", lambda e: e.drain(), inc=s_ve)
        aw = [(s_ve, None)]
        if store_wait is not None:
            aw.append(store_wait)
        act(out_f32[:], X[:], AF.Identity, bias=nb, scale=rstd, waits=aw,
            inc=s_sc)
        return ve_scr, s_sc.n

    def run_half(half):
        kj_end = 2048 * (half + 1)
        nkt = kj_end // 128
        qi0 = half * 2048

        sth = ExitStack()
        wuqt = sth.enter_context(nc.sbuf_tensor("wuqt_s%d" % half, [128, 4 * D], BF16))
        wqrct = sth.enter_context(nc.sbuf_tensor("wqrct_s%d" % half, [128, 4 * D], BF16))
        wukt = sth.enter_context(nc.sbuf_tensor("wukt_s%d" % half, [128, 4 * D], BF16))
        wuvt = sth.enter_context(nc.sbuf_tensor("wuvt_s%d" % half, [128, 4 * D], BF16))
        khT = sth.enter_context(nc.sbuf_tensor("khT%d" % half, [96, S], BF16))
        qhT = sth.enter_context(nc.sbuf_tensor("qhT%d" % half, [96, 2048], BF16))
        v4 = sth.enter_context(nc.sbuf_tensor("v4_%d" % half, [128, 32 * VSL + 36], BF16))
        Ebuf = [sth.enter_context(nc.sbuf_tensor("E%d_%d" % (half, i), [128, 1024], BF16))
                for i in range(3)]
        Pbuf = [sth.enter_context(nc.sbuf_tensor("P%d_%d" % (half, i), [128, 1024], BF16))
                for i in range(3)]
        o_f32 = sth.enter_context(nc.sbuf_tensor("of%d" % half, [64, 512], F32))
        zrow = sth.enter_context(nc.sbuf_tensor("zr%d" % half, [1, 512], F32))
        rz = sth.enter_context(nc.sbuf_tensor("rz%d" % half, [1, 512], F32))

        if half == 0:
            wq_wait = [(s_gp, gp_proj), (s_pe, pe_projdone)]
        else:
            wq_wait = [(s_st, None)]
        dma("gpsimd", ch_out(wuqt, D), ch_in(wuqt_d), s_gp, waits=wq_wait)
        dma("gpsimd", ch_out(wqrct, D), ch_in(wqrct_d), s_gp)
        dma("gpsimd", ch_out(wukt, D), ch_in(wukt_d), s_gp)
        dma("gpsimd", ch_out(wuvt, D), ch_in(wuvt_d), s_gp)
        gp_wu = s_gp.n

        if not conv_emitted[0]:
            for e in range(NE):
                dma("gpsimd", w1bf_d[e], w1_d[e], s_gp)
                dma("gpsimd", w2bf_d[e], w2_d[e], s_gp)
            gp_conv[0] = s_gp.n
            conv_emitted[0] = True

        vw = [(s_pe, pe_projdone)] if half == 0 else [(s_st, None)]
        for hh in range(4):
            P.op("vector", lambda e, o=v4[:, hh * 65 + 64: 32 * VSL: VSL]:
                 e.memset(o, 1.0), waits=vw if hh == 0 else ())
        P.op("vector", lambda e, o=v4[:, 32 * VSL: 32 * VSL + 36]:
             e.memset(o, 0.0), inc=s_ve)
        ve_vinit = s_ve.n
        P.op("vector", lambda e, o=khT[64:96, 0:kj_end], i=krt[:, 0:kj_end]:
             e.tensor_copy(out=o, in_=i),
             waits=(vw if half else []) + [(s_ve, None)], inc=s_ve)

        o_free = {}
        for hg in range(4):
            for tt in range(nkt):
                t, off = nbuf()
                key = (id(t), off)
                w = buf_wait(key) + [(s_gp, gp_wu), (s_ve, ve_vinit)]
                for lk in range(4):
                    mm(t[:, off:off + 256],
                       ckvt[:, lk * S + tt * 128: lk * S + (tt + 1) * 128],
                       wuvt[:, lk * D + hg * 256: lk * D + hg * 256 + 256],
                       lk == 0, lk == 3, waits=w if lk == 0 else (),
                       inc=s_pe if lk == 3 else None)
                act(v4[:, tt * VSL: tt * VSL + VSL]
                    .rearrange("p (b c) -> p b c", c=65)[:, :, 0:64],
                    t[:, off:off + 256].rearrange("p (b c) -> p b c", c=64),
                    AF.Copy, waits=[(s_pe, None)], inc=s_sc)
                last_ev[key] = (s_sc, s_sc.n)

            for hh in range(4):
                h = hg * 4 + hh
                for n in range(4):
                    t, off = nbuf()
                    key = (id(t), off)
                    w = buf_wait(key)
                    for lk in range(4):
                        mm(t[0:64, off:off + 512],
                           wuqt[:, lk * D + h * 64: lk * D + h * 64 + 64],
                           cqt[:, lk * S + qi0 + n * 512: lk * S + qi0 + (n + 1) * 512],
                           lk == 0, lk == 3, waits=w if lk == 0 else (),
                           inc=s_pe if lk == 3 else None)
                    evict(qhT[0:64, n * 512:(n + 1) * 512], key,
                          t[0:64, off:off + 512])
                for n in range(4):
                    t, off = nbuf()
                    key = (id(t), off)
                    w = buf_wait(key)
                    for lk in range(4):
                        mm(t[0:64, off:off + 512],
                           wqrct[:, lk * D + h * 64: lk * D + h * 64 + 64],
                           cqt[:, lk * S + qi0 + n * 512: lk * S + qi0 + (n + 1) * 512],
                           lk == 0, lk == 3, waits=w if lk == 0 else (),
                           inc=s_pe if lk == 3 else None)
                    evict(qrs[n % 2][:], key, t[0:64, off:off + 512],
                          extra_waits=[(s_ve, None)])
                    rope_emit(qhT[64:96, n * 512:(n + 1) * 512], qrs[n % 2],
                              qi0 + n * 512)
                for n in range(kj_end // 512):
                    t, off = nbuf()
                    key = (id(t), off)
                    w = buf_wait(key)
                    for lk in range(4):
                        mm(t[0:64, off:off + 512],
                           wukt[:, lk * D + h * 64: lk * D + h * 64 + 64],
                           ckvt[:, lk * S + n * 512: lk * S + (n + 1) * 512],
                           lk == 0, lk == 3, waits=w if lk == 0 else (),
                           inc=s_pe if lk == 3 else None)
                    evict(khT[0:64, n * 512:(n + 1) * 512], key,
                          t[0:64, off:off + 512])
                sc_head = s_sc.n
                ve_head = s_ve.n

                for cl in range(4):
                    gc = half * 4 + cl
                    ktiles = 4 * gc + 4
                    qsl = slice(cl * 512, (cl + 1) * 512)
                    po = ps_oc[gc % 2]
                    exp_done = {}
                    pt_done = {}
                    pv_done = {}
                    for g in range(ktiles // 2):
                        bi = g % 2
                        ei = g % 3
                        stk = ps_sT[bi]
                        for sub in range(2):
                            j = 2 * g + sub
                            w = []
                            if sub == 0:
                                w = [(s_sc, sc_head), (s_ve, ve_head)]
                                if g >= 2:
                                    w.append((s_sc, exp_done[g - 2]))
                                else:
                                    w += buf_wait((id(stk), 0)) + buf_wait((id(stk), 512))
                            mm(stk[:, sub * 512:(sub + 1) * 512],
                               khT[:, j * 128:(j + 1) * 128], qhT[:, qsl],
                               True, True, waits=w,
                               inc=s_pe if sub == 1 else None)
                        we = [(s_pe, None), (s_ld, ld_base)]
                        if g >= 3:
                            we.append((s_ve, pt_done[g - 3]))
                        act(Ebuf[ei][:], stk[:], AF.Exp, scale=SC_SQRT,
                            waits=we, inc=s_sc)
                        exp_done[g] = s_sc.n
                        last_ev[(id(stk), 0)] = (s_sc, s_sc.n)
                        last_ev[(id(stk), 512)] = (s_sc, s_sc.n)
                        wv = [(s_sc, None)]
                        if g >= 3:
                            wv.append((s_pe, pv_done[g - 3]))
                        vops = [lambda e, o=Pbuf[ei][:], i=Ebuf[ei][:]:
                                e.tensor_scalar(o, i, PLO, PHI, ALU.max, ALU.min)]
                        for sub in range(2):
                            j = 2 * g + sub
                            r = j - 4 * gc
                            if r >= 0:
                                vops.append(
                                    lambda e, o=Pbuf[ei][:, sub * 512:(sub + 1) * 512],
                                    m=masks[:, r * 512:(r + 1) * 512]:
                                    e.tensor_mul(o, o, m))
                        for k, vo in enumerate(vops):
                            P.op("vector", vo, waits=wv if k == 0 else (),
                                 inc=s_ve if k == len(vops) - 1 else None)
                        pt_done[g] = s_ve.n
                        for sub in range(2):
                            j = 2 * g + sub
                            w = [(s_ve, pt_done[g])] if sub == 0 else []
                            if sub == 0 and g == 0:
                                w.append(o_free.get(gc % 2, (s_ve, 0)))
                                w.append((s_sc, None))
                            mm(po[:], v4[:, j * VSL + hh * 65: j * VSL + hh * 65 + 96],
                               Pbuf[ei][:, sub * 512:(sub + 1) * 512],
                               j == 0, j == ktiles - 1,
                               waits=w, inc=s_pe if j == ktiles - 1 else None)
                        pv_done[g] = s_pe.n
                    P.op("vector", lambda e, o=zrow[:], i=po[64:65, :]:
                         e.tensor_copy(out=o, in_=i), waits=[(s_pe, None)])
                    P.op("vector", lambda e: e.drain())
                    P.op("vector", lambda e, o=rz[:], i=zrow[:]:
                         e.reciprocal(o, i), inc=s_ve)
                    mm(ps_zb[0:64, :], ones64[:], rz[:], True, True,
                       waits=[(s_ve, None), (s_ld, ld_base)], inc=s_pe)
                    act(o_f32[:], po[0:64, :], AF.Copy, waits=[(s_pe, None)],
                        inc=s_sc)
                    P.op("vector", lambda e,
                         o=ot_sb[64 * (h % 2):64 * (h % 2) + 64,
                                 (h // 2) * 2048 + cl * 512:(h // 2) * 2048 + (cl + 1) * 512],
                         a=o_f32[:], b=ps_zb[0:64, :]:
                         e.tensor_mul(o, a, b),
                         waits=[(s_sc, None), (s_pe, None)], inc=s_ve)
                    o_free[gc % 2] = (s_ve, s_ve.n)

        pe_heads = s_pe.n
        ve_heads = s_ve.n
        sth.close()

        stw = ExitStack()
        wot = stw.enter_context(nc.sbuf_tensor("wot_s%d" % half, [128, 8 * D], BF16))
        resid = stw.enter_context(nc.sbuf_tensor("res%d" % half, [128, 2 * D], F32))
        Xb = [stw.enter_context(nc.sbuf_tensor("X%d_%d" % (half, i), [128, D], F32))
              for i in range(2)]
        x1o = [stw.enter_context(nc.sbuf_tensor("x1o%d_%d" % (half, i), [128, D], F32))
               for i in range(2)]
        stats = [stw.enter_context(nc.sbuf_tensor("stt%d_%d" % (half, i), [128, 10], F32))
                 for i in range(2)]

        dma("gpsimd", ch_out(wot, D), ch_in(wot_d), s_gp, waits=[(s_ve, ve_heads)])
        gp_wot = s_gp.n

        add_done = [0, 0]
        scr_done = [0, 0]
        store_done = [0, 0]
        for tt in range(16):
            bi = tt % 2
            row0 = half * 2048 + tt * 128
            dma("sync", resid[:, bi * D:(bi + 1) * D], x_d[row0:row0 + 128, :],
                s_ld,
                waits=[(s_ve, scr_done[bi])] if tt >= 2 else [(s_ve, ve_heads)])
            ld_res = s_ld.n
            pw = ps_sT[bi]
            for ok in range(8):
                for dc in range(2):
                    w = []
                    if ok == 0 and dc == 0:
                        w = [(s_gp, gp_wot), (s_pe, pe_heads), (s_ve, ve_heads)]
                        if tt >= 2:
                            w.append((s_ve, add_done[bi]))
                        else:
                            w += buf_wait((id(pw), 0)) + buf_wait((id(pw), 512))
                    mm(pw[:, dc * 512:(dc + 1) * 512],
                       ot_sb[:, ok * 2048 + tt * 128: ok * 2048 + (tt + 1) * 128],
                       wot[:, ok * D + dc * 512: ok * D + (dc + 1) * 512],
                       ok == 0, ok == 7, waits=w,
                       inc=s_pe if (ok == 7 and dc == 1) else None)
            X = Xb[bi]
            P.op("vector", lambda e, o=X[:], a=pw[:], b=resid[:, bi * D:(bi + 1) * D]:
                 e.tensor_add(o, a, b), waits=[(s_pe, None), (s_ld, ld_res)],
                 inc=s_ve)
            add_done[bi] = s_ve.n
            last_ev[(id(pw), 0)] = (s_ve, s_ve.n)
            last_ev[(id(pw), 512)] = (s_ve, s_ve.n)
            scr, _ = ln_emit(X, stats[bi], x1o[bi], resid[:, bi * D:(bi + 1) * D],
                             store_wait=(s_st, store_done[bi]))
            scr_done[bi] = scr
            P.op("scalar", lambda e, o=x1_d[row0:row0 + 128, :], i=x1o[bi][:]:
                 e.dma_start(out=o, in_=i), waits=[(s_sc, None)],
                 inc=s_st, inc_amt=16)
            store_done[bi] = s_st.n
            x1st_cnt.append(s_st.n)
        stw.close()

    run_half(0)
    x1bf_cnt = [0] * NPASS
    for p in range(4):
        dma("gpsimd", x1bf_d[p * PT_:(p + 1) * PT_, :],
            x1_d[p * PT_:(p + 1) * PT_, :], s_gp,
            waits=[(s_st, x1st_cnt[4 * (p + 1)])])
        x1bf_cnt[p] = s_gp.n
    run_half(1)
    att_st_done = s_st.n
    stA.close()
    for p in range(4, NPASS):
        dma("gpsimd", x1bf_d[p * PT_:(p + 1) * PT_, :],
            x1_d[p * PT_:(p + 1) * PT_, :], s_gp,
            waits=[(s_st, x1st_cnt[4 * (p + 1)])])
        x1bf_cnt[p] = s_gp.n

    stm = ExitStack()
    w1sb = stm.enter_context(nc.sbuf_tensor("w1sb", [128, 8 * DFF], BF16))
    w2sb = stm.enter_context(nc.sbuf_tensor("w2sb", [128, 32 * D], BF16))
    h_sb = stm.enter_context(nc.sbuf_tensor("h_sb", [128, 32 * 512], BF16))
    y_acc = stm.enter_context(nc.sbuf_tensor("y_acc", [128, 4 * D], F32))
    x1t = [stm.enter_context(nc.sbuf_tensor("x1t_s%d" % i, [128, 8 * 512], BF16))
           for i in range(2)]
    lg_t = stm.enter_context(nc.sbuf_tensor("lg_t", [128, 32], F32))
    combine = [stm.enter_context(nc.sbuf_tensor("combine%d" % i, [128, 32], F32))
               for i in range(2)]
    combT = stm.enter_context(nc.sbuf_tensor("combT", [8, 512], BF16))
    logT = stm.enter_context(nc.sbuf_tensor("logT", [8, 512], F32))
    mres = stm.enter_context(nc.sbuf_tensor("mres", [128, D], F32))
    mout = stm.enter_context(nc.sbuf_tensor("mout", [128, D], F32))
    mstat = stm.enter_context(nc.sbuf_tensor("mstat", [128, 10], F32))
    rstat = stm.enter_context(nc.sbuf_tensor("rstat", [128, 32], F32))
    wexp_t = stm.enter_context(nc.sbuf_tensor("wexp_t", [128, 32], F32))
    wsm_t = stm.enter_context(nc.sbuf_tensor("wsm_t", [128, 32], F32))
    b2_sb = stm.enter_context(nc.sbuf_tensor("b2_sb", [NE, D], BF16))
    b1_sb = stm.enter_context(nc.sbuf_tensor("b1_sb", [128, NE * 32], F32))

    ps_h = [psA, psB]
    ps_ya = [ps_sT[0], ps_sT[1]]
    last_ev[(id(psA), 0)] = (s_st, att_st_done)
    last_ev[(id(psB), 0)] = (s_st, att_st_done)

    pass_mm1_end = [0] * (NPASS + 1)
    mm1_done = [s_pe.n]
    mm2_done = [s_pe.n]
    w1_ld = [0]
    w2_ld = [0]

    dma("gpsimd", b2_sb[:], b2_d, s_gp, waits=[(s_st, att_st_done)])
    gp_b2 = s_gp.n
    dma("sync", b1_sb[:], b1t_d, s_ld, waits=[(s_st, att_st_done)])
    dma("sync", ch_out(w1sb, DFF), ch_in(w1bf_d[0]), s_ld,
        waits=[(s_gp, gp_conv[0])])
    w1_ld[-1] = s_ld.n
    dma("sync", ch_out(w2sb, D), ch_in(w2bf_d[0]), s_ld)
    w2_ld[-1] = s_ld.n
    ld_moe_consts = s_ld.n

    def emit_router(p, inject=None):
        cmb = combine[p % 2]
        xt = x1t[p % 2]
        rt = ps_ya[0]
        for dk in range(8):
            w = [(s_ld, None), (s_ve, None), (s_gp, gp_proj)] if dk == 0 else []
            mm(rt[0:8, 0:512], wg_sb[:, dk * 8:(dk + 1) * 8],
               xt[:, dk * 512:(dk + 1) * 512], dk == 0, dk == 7,
               waits=w, inc=s_pe if dk == 7 else None)
        act(logT[:], rt[0:8, 0:512], AF.Identity, bias=bg_sb[:],
            waits=[(s_pe, None), (s_ld, ld_moe_consts)], inc=s_sc)

        def emit_ltrans():
            for q4 in range(4):
                P.op("tensor", lambda e, o=rt[:, 512:520],
                     i=logT[:, q4 * 128:(q4 + 1) * 128], idt=id8[:]:
                     e.transpose(o, i, idt), waits=[(s_sc, None)], inc=s_pe)
                act(lg_t[:, q4 * 8:(q4 + 1) * 8], rt[:, 512:520], AF.Copy,
                    waits=[(s_pe, None)], inc=s_sc)

        def emit_softmax():
            LG = [lg_t[:, q * 8:(q + 1) * 8] for q in range(4)]
            CM = [cmb[:, q * 8:(q + 1) * 8] for q in range(4)]
            WX = [wexp_t[:, q * 8:(q + 1) * 8] for q in range(4)]
            WS = [wsm_t[:, q * 8:(q + 1) * 8] for q in range(4)]
            STc = lambda j, q: rstat[:, 4 * j + q:4 * j + q + 1]
            dr = lambda: P.op("vector", lambda e: e.drain())
            for q in range(4):
                P.op("vector", lambda e, o=STc(0, q), i=LG[q]:
                     e.reduce_max(o, i, axis=AX.X, negate=True),
                     waits=[(s_sc, None)] if q == 0 else (),
                     inc=s_ve if q == 3 else None)
            for q in range(4):
                act(WX[q], LG[q], AF.Exp, bias=STc(0, q),
                    waits=[(s_ve, None)] if q == 0 else (),
                    inc=s_sc if q == 3 else None)
            for q in range(4):
                P.op("vector", lambda e, o=STc(1, q), i=WX[q]:
                     e.reduce_sum(o, i, axis=AX.X),
                     waits=[(s_sc, None)] if q == 0 else ())
            dr()
            for q in range(4):
                P.op("vector", lambda e, o=STc(2, q), i=STc(1, q): e.reciprocal(o, i))
            dr()
            for q in range(4):
                P.op("vector", lambda e, o=WS[q], i=WX[q], sc=STc(2, q):
                     e.tensor_scalar(o, i, sc, None, ALU.mult))
            dr()
            for q in range(4):
                P.op("vector", lambda e, o=STc(3, q), i=WS[q]:
                     e.reduce_max(o, i, axis=AX.X))
            dr()
            for q in range(4):
                P.op("vector", lambda e, o=CM[q], i=WS[q], sc=STc(3, q):
                     e.tensor_scalar(o, i, sc, None, ALU.is_lt))
            dr()
            for q in range(4):
                P.op("vector", lambda e, o=CM[q], a=WS[q], b=CM[q]:
                     e.tensor_mul(o, a, b))
            dr()
            for q in range(4):
                P.op("vector", lambda e, o=STc(4, q), i=CM[q]:
                     e.reduce_max(o, i, axis=AX.X))
            dr()
            for q in range(4):
                P.op("vector", lambda e, o=STc(5, q), a=STc(3, q), b=STc(4, q):
                     e.tensor_add(o, a, b))
            for q in range(4):
                P.op("vector", lambda e, o=CM[q], i=WS[q], sc=STc(4, q):
                     e.tensor_scalar(o, i, sc, None, ALU.is_ge))
            dr()
            for q in range(4):
                P.op("vector", lambda e, o=STc(6, q), i=STc(5, q):
                     e.reciprocal(o, i))
            for q in range(4):
                P.op("vector", lambda e, o=CM[q], a=WS[q], b=CM[q]:
                     e.tensor_mul(o, a, b))
            dr()
            for q in range(4):
                P.op("vector", lambda e, o=CM[q], i=CM[q], sc=STc(6, q):
                     e.tensor_scalar(o, i, sc, None, ALU.mult),
                     inc=s_ve if q == 3 else None)

        def emit_ctrans():
            ve_comb = s_ve.n
            for q4 in range(4):
                P.op("tensor", lambda e, o=rt[0:8, 512:640],
                     i=cmb[:, q4 * 8:(q4 + 1) * 8], idt=id128[:]:
                     e.transpose(o, i, idt),
                     waits=[(s_ve, ve_comb), (s_sc, None)], inc=s_pe)
                act(combT[:, q4 * 128:(q4 + 1) * 128], rt[0:8, 512:640], AF.Copy,
                    waits=[(s_pe, None)], inc=s_sc)

        def emit_yinit(ve_scr_gate):
            for tm in range(4):
                for dc in range(2):
                    w = [(s_sc, None), (s_gp, gp_b2)] if dc == 0 else ()
                    mm(rt[:, dc * 512:(dc + 1) * 512],
                       combT[:, tm * 128:(tm + 1) * 128],
                       b2_sb[:, dc * 512:(dc + 1) * 512], True, True,
                       waits=w, inc=s_pe if dc == 1 else None)
                act(y_acc[:, tm * D:(tm + 1) * D], rt[:], AF.Copy,
                    waits=[(s_pe, None), (s_ve, ve_scr_gate)], inc=s_sc)

        return emit_ltrans, emit_softmax, emit_ctrans, emit_yinit

    ve_yacc_scr = [0]
    ln_prev = [None]

    def emit_pass_ln(p):
        for tm in range(4):
            row0 = p * PT_ + tm * 128
            dma("sync", mres[:], x1_d[row0:row0 + 128, :], s_ld,
                waits=[(s_st, None)])
            P.op("vector", lambda e, o=mout[:], a=y_acc[:, tm * D:(tm + 1) * D],
                 b=mres[:]:
                 e.tensor_add(o, a, b), waits=[(s_ld, None), (s_ve, None)],
                 inc=s_ve)
            scr, _ = ln_emit(mout, mstat, mres, y_acc[:, tm * D:(tm + 1) * D])
            ve_yacc_scr[0] = scr
            P.op("scalar", lambda e, o=out_d[row0:row0 + 128, :], i=mres[:]:
                 e.dma_start(out=o, in_=i), waits=[(s_sc, None)],
                 inc=s_st, inc_amt=16)

    for p in range(NPASS):
        xt = x1t[p % 2]
        for k in range(8):
            w = []
            if k == 0:
                w = [(s_gp, x1bf_cnt[p]), (s_st, att_st_done),
                     (s_pe, pass_mm1_end[max(0, p - 1)])]
            dma("sync", xt[:, k * 512:(k + 1) * 512],
                x1bf_d[p * PT_:(p + 1) * PT_, k * 128:(k + 1) * 128],
                s_ld, waits=w, transpose=True)

        e_lt, e_sm, e_ct, e_yi = emit_router(p)
        if p == 0:
            e_lt()
            e_sm()
            e_ct()
            e_yi(ve_yacc_scr[0])
        else:
            e_sm_pend = (e_lt, e_sm, e_ct, e_yi)

        for e_ in range(NE):
            if not (p == 0 and e_ == 0):
                dma("sync", ch_out(w1sb, DFF), ch_in(w1bf_d[e_]),
                    s_ld, waits=[(s_pe, mm1_done[-1])])
                w1_ld.append(s_ld.n)
                dma("sync", ch_out(w2sb, D), ch_in(w2bf_d[e_]),
                    s_ld, waits=[(s_pe, mm2_done[-1])])
                w2_ld.append(s_ld.n)
            for fm in range(32):
                buf = ps_h[fm % 2]
                key = (id(buf), 0)
                for dk in range(8):
                    w = []
                    if dk == 0:
                        w = [(s_ld, w1_ld[-1])] + buf_wait(key)
                    mm(buf[:], w1sb[:, dk * DFF + fm * 128: dk * DFF + (fm + 1) * 128],
                       xt[:, dk * 512:(dk + 1) * 512], dk == 0, dk == 7,
                       waits=w, inc=s_pe if dk == 7 else None)
                act(h_sb[:, fm * 512:(fm + 1) * 512], buf[:], AF.Relu,
                    bias=b1_sb[:, e_ * 32 + fm: e_ * 32 + fm + 1],
                    waits=[(s_pe, None), (s_ld, ld_moe_consts)], inc=s_sc)
                last_ev[key] = (s_sc, s_sc.n)
                if p > 0 and e_ == 0:
                    if fm == 3:
                        e_sm_pend[0]()
                    elif fm == 5:
                        e_sm_pend[1]()
                        emit_pass_ln(p - 1)
                    elif fm == 24:
                        e_sm_pend[2]()
                    elif fm == 29:
                        e_sm_pend[3](ve_yacc_scr[0])
            mm1_done.append(s_pe.n)
            if e_ == NE - 1:
                pass_mm1_end[p] = s_pe.n
            sc_h = s_sc.n
            for half2 in range(2):
                for fk in range(32):
                    for tmi in range(2):
                        tm = half2 * 2 + tmi
                        for dc in range(2):
                            w = []
                            if fk == 0 and tmi == 0 and dc == 0:
                                w = [(s_ld, w2_ld[-1]), (s_sc, sc_h), (s_ve, None)]
                            mm(ps_ya[tmi][:, dc * 512:(dc + 1) * 512],
                               h_sb[:, fk * 512 + tm * 128: fk * 512 + (tm + 1) * 128],
                               w2sb[:, fk * D + dc * 512: fk * D + (dc + 1) * 512],
                               fk == 0, fk == 31, waits=w,
                               inc=s_pe if (fk == 31 and tmi == 1 and dc == 1) else None)
                for tmi in range(2):
                    tm = half2 * 2 + tmi
                    P.op("vector", lambda e, o=y_acc[:, tm * D:(tm + 1) * D],
                         a=ps_ya[tmi][:],
                         s=combine[p % 2][:, tm * 8 + e_: tm * 8 + e_ + 1]:
                         e.scalar_tensor_tensor(o, a, s, o, ALU.mult, ALU.add),
                         waits=[(s_pe, None), (s_sc, None)] if tmi == 0 else [],
                         inc=s_ve)
            mm2_done.append(s_pe.n)

    emit_pass_ln(NPASS - 1)

    P.op("sync", lambda e: e.nop(), waits=[(s_st, s_st.n)])

    with nc.Block() as block:
        @block.sync
        def _(eng):
            for fn in P.ops["sync"]:
                fn(eng)

        @block.gpsimd
        def _(eng):
            for fn in P.ops["gpsimd"]:
                fn(eng)

        @block.vector
        def _(eng):
            for fn in P.ops["vector"]:
                fn(eng)

        @block.scalar
        def _(eng):
            for fn in P.ops["scalar"]:
                fn(eng)

        @block.tensor
        def _(eng):
            for fn in P.ops["tensor"]:
                fn(eng)

    stm.close()
    st.close()
    return nc


def _consts_np():
    inv_freq = 10000.0 ** (-np.arange(0, DHR, 2, dtype=np.float64) / DHR)
    t = np.arange(S, dtype=np.float64)
    ang = inv_freq[:, None] * t[None, :]
    C = np.repeat(np.cos(ang), 2, axis=0).astype(np.float32)
    sin = np.sin(ang)
    Spm = np.empty((DHR, S), np.float64)
    Spm[0::2] = -sin
    Spm[1::2] = sin
    masks = np.zeros((128, 4 * 512), np.float32)
    jj = np.arange(128)[:, None]
    ii = np.arange(512)[None, :]
    for r in range(4):
        masks[:, r * 512:(r + 1) * 512] = (jj + 128 * r <= ii).astype(np.float32)
    consts = np.zeros((128, 4), np.float32)
    consts[:, 0] = 1.0 / np.sqrt(96.0)
    consts[:, 1] = 1e-5
    consts[:, 2] = -0.5
    consts[:, 3] = 1.0 / D
    bf = ml_dtypes.bfloat16
    CS = np.concatenate([C, Spm.astype(np.float32)], axis=0)
    return (CS.astype(bf), masks.astype(bf), consts)


def _cat_swap(W, blk):
    n = W.shape[0] // blk
    out = np.empty((n * 2 * blk, W.shape[1]), W.dtype)
    perm = np.arange(blk) ^ 1
    for i in range(n):
        out[i * 2 * blk: i * 2 * blk + blk] = W[i * blk:(i + 1) * blk]
        out[i * 2 * blk + blk: (i + 1) * 2 * blk] = W[i * blk:(i + 1) * blk][perm]
    return out


def kernel(L0_attn_x, L0_attn_W_DQ, L0_attn_W_UQ, L0_attn_W_QR, L0_attn_W_DKV,
           L0_attn_W_UK, L0_attn_W_UV, L0_attn_W_KR, L0_attn_W_O,
           L0_ffn_W1, L0_ffn_b1, L0_ffn_W2, L0_ffn_b2, L0_ffn_Wg, L0_ffn_bg):
    nc = bass.Bass("TRN2", target_bir_lowering=False)
    build(nc)

    CS, masks, consts = _consts_np()
    asc = np.ascontiguousarray
    wqrc = _cat_swap(L0_attn_W_QR, DHR)
    wkrc = _cat_swap(L0_attn_W_KR, DHR)
    common = {
        "wdqt": asc(L0_attn_W_DQ.T), "wdkvt": asc(L0_attn_W_DKV.T),
        "wkrct": asc(wkrc.T), "wuqt": asc(L0_attn_W_UQ.T),
        "wqrct": asc(wqrc.T), "wukt": asc(L0_attn_W_UK.T),
        "wuvt": asc(L0_attn_W_UV.T), "wot": asc(L0_attn_W_O.T),
        "w1": asc(L0_ffn_W1), "w2": asc(L0_ffn_W2),
        "b1t": asc(L0_ffn_b1.reshape(NE, 32, 128).transpose(2, 0, 1).reshape(128, NE * 32)),
        "b2": asc(L0_ffn_b2),
        "wgt": asc(L0_ffn_Wg.reshape(8, 128, 8).transpose(1, 0, 2).reshape(128, 64)),
        "bg": asc(L0_ffn_bg.reshape(NE, 1)),
        "ropecs": CS, "masks": masks,
        "id8": np.eye(8, dtype=np.float32), "id128": np.eye(128, dtype=np.float32),
        "ones64": np.ones((1, 64), np.float32), "consts": consts,
    }
    in_maps = []
    for c in range(8):
        m = dict(common)
        xc = asc(np.asarray(L0_attn_x[c], dtype=np.float32))
        m["x"] = xc
        m["xT"] = asc(xc.T)
        in_maps.append(m)
    res = run_bass_kernel_spmd(nc, in_maps, core_ids=list(range(8)))
    return np.stack([np.asarray(r["out"], dtype=np.float32) for r in res.results])


# version 06
# speedup vs baseline: 1.2205x; 1.0878x over previous
import numpy as np
import ml_dtypes
import concourse.bass as bass
import concourse.mybir as mybir
from concourse.bass_utils import run_bass_kernel_spmd
from contextlib import ExitStack

F32 = mybir.dt.float32
BF16 = mybir.dt.bfloat16
AF = mybir.ActivationFunctionType
ALU = mybir.AluOpType
AX = mybir.AxisListType

S, D, H, DH, DHR, LAT = 4096, 1024, 16, 64, 32, 512
DFF, NE = 4096, 8
TAU = float(np.sqrt(96.0))
PLO = float(np.exp(-80.0 / TAU))
PHI = float(np.exp(80.0 / TAU))
NPASS = 8
PT_ = S // NPASS
VSL = 260


class Sem:
    def __init__(self, h):
        self.h = h
        self.n = 0


class Prog:
    def __init__(self):
        self.ops = {k: [] for k in ("sync", "gpsimd", "vector", "scalar", "tensor")}

    def op(self, eng, f, waits=(), inc=None, inc_amt=1):
        wl = [(s.h, s.n if tgt is None else tgt) for (s, tgt) in waits]
        if inc is not None:
            inc.n += inc_amt
            pair = (inc.h, inc_amt)
        else:
            pair = None

        def fn(e, wl=wl, f=f, pair=pair):
            for h, v in wl:
                e.wait_ge(h, v)
            inst = f(e)
            if pair is not None:
                inst.then_inc(*pair)
        self.ops[eng].append(fn)


def build(nc):
    x_d = nc.dram_tensor("x", [S, D], F32, kind="ExternalInput").ap()
    xT_d = nc.dram_tensor("xT", [D, S], F32, kind="ExternalInput").ap()
    wdqt_d = nc.dram_tensor("wdqt", [D, LAT], F32, kind="ExternalInput").ap()
    wdkvt_d = nc.dram_tensor("wdkvt", [D, LAT], F32, kind="ExternalInput").ap()
    wkrct_d = nc.dram_tensor("wkrct", [D, 64], F32, kind="ExternalInput").ap()
    wuqt_d = nc.dram_tensor("wuqt", [LAT, D], F32, kind="ExternalInput").ap()
    wqrct_d = nc.dram_tensor("wqrct", [LAT, D], F32, kind="ExternalInput").ap()
    wukt_d = nc.dram_tensor("wukt", [LAT, D], F32, kind="ExternalInput").ap()
    wuvt_d = nc.dram_tensor("wuvt", [LAT, D], F32, kind="ExternalInput").ap()
    wot_d = nc.dram_tensor("wot", [D, D], F32, kind="ExternalInput").ap()
    w1_d = nc.dram_tensor("w1", [NE, D, DFF], F32, kind="ExternalInput").ap()
    w2_d = nc.dram_tensor("w2", [NE, DFF, D], F32, kind="ExternalInput").ap()
    b1t_d = nc.dram_tensor("b1t", [128, NE * 32], F32, kind="ExternalInput").ap()
    b2_d = nc.dram_tensor("b2", [NE, D], F32, kind="ExternalInput").ap()
    wgt_d = nc.dram_tensor("wgt", [128, 64], F32, kind="ExternalInput").ap()
    bg_d = nc.dram_tensor("bg", [NE, 1], F32, kind="ExternalInput").ap()
    ropecs_d = nc.dram_tensor("ropecs", [64, S], BF16, kind="ExternalInput").ap()
    masks_d = nc.dram_tensor("masks", [128, 4 * 512], BF16, kind="ExternalInput").ap()
    id8_d = nc.dram_tensor("id8", [8, 8], F32, kind="ExternalInput").ap()
    id128_d = nc.dram_tensor("id128", [128, 128], F32, kind="ExternalInput").ap()
    ones64_d = nc.dram_tensor("ones64", [1, 64], F32, kind="ExternalInput").ap()
    consts_d = nc.dram_tensor("consts", [128, 4], F32, kind="ExternalInput").ap()

    out_d = nc.dram_tensor("out", [S, D], F32, kind="ExternalOutput").ap()

    w1bf_d = nc.dram_tensor("w1bf", [NE, D, DFF], BF16, kind="Internal").ap()
    w2bf_d = nc.dram_tensor("w2bf", [NE, DFF, D], BF16, kind="Internal").ap()
    x1_d = nc.dram_tensor("x1f", [S, D], F32, kind="Internal").ap()
    x1bf_d = nc.dram_tensor("x1bf", [S, D], BF16, kind="Internal").ap()

    P = Prog()
    st = ExitStack()
    sb = lambda n, s_, d_: st.enter_context(nc.sbuf_tensor(n, s_, d_))
    psal = lambda n, s_: st.enter_context(nc.psum_tensor(n, s_, F32))
    sem = lambda n: Sem(st.enter_context(nc.semaphore(n)))
    s_ld = sem("s_ld")
    s_gp = sem("s_gp")
    s_pe = sem("s_pe")
    s_sc = sem("s_sc")
    s_ve = sem("s_ve")
    s_st = sem("s_st")
    s_h2 = sem("s_h2")

    consts = sb("consts_s", [128, 4], F32)
    id8 = sb("id8_s", [8, 8], F32)
    id128 = sb("id128_s", [128, 128], F32)
    ones64 = sb("ones64_s", [1, 64], F32)
    wg_sb = sb("wg_sb", [128, 64], BF16)
    bg_sb = sb("bg_sb", [NE, 1], F32)

    def dma(eng, out, in_, semc, waits=(), **kw):
        P.op(eng, lambda e, o=out, i=in_, kw=kw: e.dma_start(out=o, in_=i, **kw),
             waits=waits, inc=semc, inc_amt=16)

    def ch_in(dram_ap):
        return dram_ap.rearrange("(k p) t -> p k t", p=128)

    def ch_out(sb_t, inner):
        return sb_t[:].rearrange("p (k t) -> p k t", t=inner)

    dma("sync", consts[:], consts_d, s_ld)
    dma("sync", id8[:], id8_d, s_ld)
    dma("sync", id128[:], id128_d, s_ld)
    dma("sync", ones64[:], ones64_d, s_ld)
    dma("sync", bg_sb[:], bg_d, s_ld)
    SC_SQRT = consts[:, 0:1]
    SC_EPS = consts[:, 1:2]
    SC_NHALF = consts[:, 2:3]

    stA = ExitStack()
    sba = lambda n, sh, d_: stA.enter_context(nc.sbuf_tensor(n, sh, d_))
    cqt = sba("cqt", [128, 4 * S], BF16)
    ckvt = sba("ckvt", [128, 4 * S], BF16)
    krt = sba("krt", [DHR, S], BF16)
    tabCS = sba("tabCS", [64, S], BF16)
    masks = sba("masks_s", [128, 4 * 512], BF16)
    qrs = [sba("qrs%d" % i, [64, 512], BF16) for i in range(2)]
    rt1 = sba("rt1", [DHR, 512], BF16)
    rt2 = sba("rt2", [DHR, 512], BF16)

    dma("sync", tabCS[:], ropecs_d, s_ld)
    dma("sync", masks[:], masks_d, s_ld)
    dma("gpsimd", wg_sb[:], wgt_d, s_gp)

    psA = psal("psA", [128, 512])
    psB = psal("psB", [128, 512])
    ps_sT = [psal("ps_sT0", [128, 1024]), psal("ps_sT1", [128, 1024])]
    ps_oc = [psal("ps_o0", [96, 512]), psal("ps_o1", [96, 512])]
    ps_zb = psB

    ld_base = s_ld.n

    last_ev = {}

    def mm(out, lhsT, rhs, start, stop, waits=(), inc=None):
        P.op("tensor",
             lambda e, o=out, a=lhsT, b=rhs, s1=start, s2=stop:
             e.matmul(o, a, b, start=s1, stop=s2), waits=waits, inc=inc)

    def act(out, in_, func, bias=0.0, scale=1.0, waits=(), inc=None):
        P.op("scalar",
             lambda e, o=out, i=in_, f=func, b=bias, s=scale:
             e.activation(out=o, in_=i, func=f, bias=b, scale=s),
             waits=waits, inc=inc)

    pp4 = [(ps_sT[0], 0), (ps_sT[0], 512), (ps_sT[1], 0), (ps_sT[1], 512)]
    pi = [0]

    def nbuf():
        b = pp4[pi[0] % 4]
        pi[0] += 1
        return b

    def buf_wait(key):
        v = last_ev.get(key)
        return [v] if v is not None else []

    def evict(dst, key, src_ap, func=AF.Copy, bias=0.0, extra_waits=()):
        act(dst, src_ap, func, bias=bias,
            waits=[(s_pe, None)] + list(extra_waits), inc=s_sc)
        last_ev[key] = (s_sc, s_sc.n)

    stp = ExitStack()
    xt_sb = stp.enter_context(nc.sbuf_tensor("xt_sb", [128, 8 * S], BF16))
    wdqt = stp.enter_context(nc.sbuf_tensor("wdqt_s", [128, 8 * LAT], BF16))
    wdkvt = stp.enter_context(nc.sbuf_tensor("wdkvt_s", [128, 8 * LAT], BF16))
    wkrct = stp.enter_context(nc.sbuf_tensor("wkrct_s", [128, 8 * 64], BF16))

    dma("gpsimd", ch_out(xt_sb, S), ch_in(xT_d), s_gp)
    dma("gpsimd", ch_out(wdqt, LAT), ch_in(wdqt_d), s_gp)
    dma("gpsimd", ch_out(wdkvt, LAT), ch_in(wdkvt_d), s_gp)
    dma("gpsimd", ch_out(wkrct, 64), ch_in(wkrct_d), s_gp)
    gp_proj = s_gp.n

    w0 = [(s_gp, gp_proj), (s_ld, ld_base)]

    def proj128(dst, dst_col0, wt, wt_cols, row0, M):
        for n in range(S // 512):
            t, off = nbuf()
            key = (id(t), off)
            w = w0 + buf_wait(key)
            for dk in range(8):
                mm(t[0:M, off:off + 512],
                   wt[:, dk * wt_cols + row0: dk * wt_cols + row0 + M],
                   xt_sb[:, dk * S + n * 512: dk * S + (n + 1) * 512],
                   dk == 0, dk == 7, waits=w if dk == 0 else (),
                   inc=s_pe if dk == 7 else None)
            evict(dst[:, dst_col0 + n * 512: dst_col0 + (n + 1) * 512],
                  key, t[0:M, off:off + 512])

    for lm in range(4):
        proj128(cqt, lm * S, wdqt, LAT, lm * 128, 128)
    for lm in range(4):
        proj128(ckvt, lm * S, wdkvt, LAT, lm * 128, 128)

    def rope_emit(dst, stage, tab_c):
        P.op("vector", lambda e, o=rt1[:], a=stage[0:DHR, :],
             b=tabCS[0:DHR, tab_c:tab_c + 512]:
             e.tensor_mul(o, a, b), waits=[(s_sc, None), (s_ld, ld_base)])
        P.op("vector", lambda e, o=rt2[:], a=stage[DHR:64, :],
             b=tabCS[DHR:64, tab_c:tab_c + 512]:
             e.tensor_mul(o, a, b))
        P.op("vector", lambda e, o=dst, a=rt1[:], b=rt2[:]:
             e.tensor_add(o, a, b), inc=s_ve)

    for n in range(S // 512):
        t, off = nbuf()
        key = (id(t), off)
        w = w0 + buf_wait(key) + [(s_ve, None)]
        for dk in range(8):
            mm(t[0:64, off:off + 512], wkrct[:, dk * 64:(dk + 1) * 64],
               xt_sb[:, dk * S + n * 512: dk * S + (n + 1) * 512],
               dk == 0, dk == 7, waits=w if dk == 0 else (),
               inc=s_pe if dk == 7 else None)
        evict(qrs[n % 2][:], key, t[0:64, off:off + 512],
              extra_waits=[(s_ve, None)])
        rope_emit(krt[:, n * 512:(n + 1) * 512], qrs[n % 2], n * 512)

    pe_projdone = s_pe.n
    stp.close()

    ot_sb = sba("ot_sb", [128, 8 * 2048], BF16)

    x1st_cnt = [0]
    conv_emitted = [False]
    gp_conv = [0]

    def ln_emit(X, stt, out_f32, scratch, store_wait=None):
        sum1 = stt[:, 0:1]; ss = stt[:, 1:2]; mean = stt[:, 2:3]
        mvar = stt[:, 3:4]; m2 = stt[:, 4:5]; var = stt[:, 5:6]
        lnv = stt[:, 6:7]; rstd = stt[:, 7:8]; nb = stt[:, 8:9]
        P.op("vector", lambda e, o=scratch, a=X[:], b=ss:
             e.scalar_tensor_tensor(o, a, 1.0, a, ALU.mult, ALU.mult, accum_out=b))
        P.op("vector", lambda e, o=sum1, i=X[:]: e.reduce_sum(o, i, axis=AX.X),
             inc=s_ve)
        ve_scr = s_ve.n
        P.op("vector", lambda e: e.drain())
        P.op("vector", lambda e, o=mean, i=sum1:
             e.tensor_scalar(o, i, 1.0 / D, None, ALU.mult))
        P.op("vector", lambda e, o=mvar, i=ss:
             e.tensor_scalar(o, i, 1.0 / D, None, ALU.mult))
        P.op("vector", lambda e: e.drain())
        P.op("vector", lambda e, o=m2, a=mean, b=mean: e.tensor_mul(o, a, b))
        P.op("vector", lambda e: e.drain())
        P.op("vector", lambda e, o=var, a=mvar, b=m2: e.tensor_sub(o, a, b),
             inc=s_ve)
        act(lnv, var, AF.Ln, bias=SC_EPS, waits=[(s_ve, None)])
        act(rstd, lnv, AF.Exp, scale=SC_NHALF, inc=s_sc)
        P.op("vector", lambda e, o=nb, i=rstd, m=mean:
             e.tensor_scalar(o, i, m, -1.0, ALU.mult, ALU.mult),
             waits=[(s_sc, None)])
        P.op("vector", lambda e: e.drain(), inc=s_ve)
        aw = [(s_ve, None)]
        if store_wait is not None:
            aw.append(store_wait)
        act(out_f32[:], X[:], AF.Identity, bias=nb, scale=rstd, waits=aw,
            inc=s_sc)
        return ve_scr, s_sc.n

    def run_half(half):
        kj_end = 2048 * (half + 1)
        nkt = kj_end // 128
        qi0 = half * 2048

        sth = ExitStack()
        wuqt = sth.enter_context(nc.sbuf_tensor("wuqt_s%d" % half, [128, 4 * D], BF16))
        wqrct = sth.enter_context(nc.sbuf_tensor("wqrct_s%d" % half, [128, 4 * D], BF16))
        wukt = sth.enter_context(nc.sbuf_tensor("wukt_s%d" % half, [128, 4 * D], BF16))
        wuvt = sth.enter_context(nc.sbuf_tensor("wuvt_s%d" % half, [128, 4 * D], BF16))
        khT = sth.enter_context(nc.sbuf_tensor("khT%d" % half, [96, S], BF16))
        qhT = sth.enter_context(nc.sbuf_tensor("qhT%d" % half, [96, 2048], BF16))
        v4 = sth.enter_context(nc.sbuf_tensor("v4_%d" % half, [128, 32 * VSL + 36], BF16))
        Ebuf = [sth.enter_context(nc.sbuf_tensor("E%d_%d" % (half, i), [128, 1024], BF16))
                for i in range(3)]
        Pbuf = [sth.enter_context(nc.sbuf_tensor("P%d_%d" % (half, i), [128, 1024], BF16))
                for i in range(3)]
        o_f32 = sth.enter_context(nc.sbuf_tensor("of%d" % half, [64, 512], F32))
        zrow = sth.enter_context(nc.sbuf_tensor("zr%d" % half, [1, 512], F32))
        rz = sth.enter_context(nc.sbuf_tensor("rz%d" % half, [1, 512], F32))

        if half == 0:
            wq_wait = [(s_gp, gp_proj), (s_pe, pe_projdone)]
        else:
            wq_wait = [(s_st, None)]
        dma("gpsimd", ch_out(wuqt, D), ch_in(wuqt_d), s_gp, waits=wq_wait)
        dma("gpsimd", ch_out(wqrct, D), ch_in(wqrct_d), s_gp)
        dma("gpsimd", ch_out(wukt, D), ch_in(wukt_d), s_gp)
        dma("gpsimd", ch_out(wuvt, D), ch_in(wuvt_d), s_gp)
        gp_wu = s_gp.n

        if not conv_emitted[0]:
            for e in range(NE):
                dma("gpsimd", w1bf_d[e], w1_d[e], s_gp)
                dma("gpsimd", w2bf_d[e], w2_d[e], s_gp)
            gp_conv[0] = s_gp.n
            conv_emitted[0] = True

        vw = [(s_pe, pe_projdone)] if half == 0 else [(s_st, None)]
        for hh in range(4):
            P.op("vector", lambda e, o=v4[:, hh * 65 + 64: 32 * VSL: VSL]:
                 e.memset(o, 1.0), waits=vw if hh == 0 else ())
        P.op("vector", lambda e, o=v4[:, 32 * VSL: 32 * VSL + 36]:
             e.memset(o, 0.0), inc=s_ve)
        ve_vinit = s_ve.n
        P.op("vector", lambda e, o=khT[64:96, 0:kj_end], i=krt[:, 0:kj_end]:
             e.tensor_copy(out=o, in_=i),
             waits=(vw if half else []) + [(s_ve, None)], inc=s_ve)

        o_free = {}
        pend_z = [None]
        for hg in range(4):
            for tt in range(nkt):
                t, off = nbuf()
                key = (id(t), off)
                w = buf_wait(key) + [(s_gp, gp_wu), (s_ve, ve_vinit)]
                for lk in range(4):
                    mm(t[:, off:off + 256],
                       ckvt[:, lk * S + tt * 128: lk * S + (tt + 1) * 128],
                       wuvt[:, lk * D + hg * 256: lk * D + hg * 256 + 256],
                       lk == 0, lk == 3, waits=w if lk == 0 else (),
                       inc=s_pe if lk == 3 else None)
                act(v4[:, tt * VSL: tt * VSL + VSL]
                    .rearrange("p (b c) -> p b c", c=65)[:, :, 0:64],
                    t[:, off:off + 256].rearrange("p (b c) -> p b c", c=64),
                    AF.Copy, waits=[(s_pe, None)], inc=s_sc)
                last_ev[key] = (s_sc, s_sc.n)

            for hh in range(4):
                h = hg * 4 + hh
                for n in range(4):
                    t, off = nbuf()
                    key = (id(t), off)
                    w = buf_wait(key)
                    for lk in range(4):
                        mm(t[0:64, off:off + 512],
                           wuqt[:, lk * D + h * 64: lk * D + h * 64 + 64],
                           cqt[:, lk * S + qi0 + n * 512: lk * S + qi0 + (n + 1) * 512],
                           lk == 0, lk == 3, waits=w if lk == 0 else (),
                           inc=s_pe if lk == 3 else None)
                    evict(qhT[0:64, n * 512:(n + 1) * 512], key,
                          t[0:64, off:off + 512])
                for n in range(4):
                    t, off = nbuf()
                    key = (id(t), off)
                    w = buf_wait(key)
                    for lk in range(4):
                        mm(t[0:64, off:off + 512],
                           wqrct[:, lk * D + h * 64: lk * D + h * 64 + 64],
                           cqt[:, lk * S + qi0 + n * 512: lk * S + qi0 + (n + 1) * 512],
                           lk == 0, lk == 3, waits=w if lk == 0 else (),
                           inc=s_pe if lk == 3 else None)
                    evict(qrs[n % 2][:], key, t[0:64, off:off + 512],
                          extra_waits=[(s_ve, None)])
                    rope_emit(qhT[64:96, n * 512:(n + 1) * 512], qrs[n % 2],
                              qi0 + n * 512)
                for n in range(kj_end // 512):
                    t, off = nbuf()
                    key = (id(t), off)
                    w = buf_wait(key)
                    for lk in range(4):
                        mm(t[0:64, off:off + 512],
                           wukt[:, lk * D + h * 64: lk * D + h * 64 + 64],
                           ckvt[:, lk * S + n * 512: lk * S + (n + 1) * 512],
                           lk == 0, lk == 3, waits=w if lk == 0 else (),
                           inc=s_pe if lk == 3 else None)
                    evict(khT[0:64, n * 512:(n + 1) * 512], key,
                          t[0:64, off:off + 512])
                sc_head = s_sc.n
                ve_head = s_ve.n

                for cl in range(4):
                    gc = half * 4 + cl
                    ktiles = 4 * gc + 4
                    qsl = slice(cl * 512, (cl + 1) * 512)
                    po = ps_oc[gc % 2]
                    exp_done = {}
                    pt_done = {}
                    pv_done = {}

                    def emit_pv(g, ktiles=ktiles, po=po, gc=gc):
                        ei = g % 3
                        for sub in range(2):
                            j = 2 * g + sub
                            w = [(s_ve, pt_done[g])] if sub == 0 else []
                            if sub == 0 and g == 0:
                                w.append(o_free.get(gc % 2, (s_ve, 0)))
                                w.append((s_sc, None))
                            mm(po[:], v4[:, j * VSL + hh * 65: j * VSL + hh * 65 + 96],
                               Pbuf[ei][:, sub * 512:(sub + 1) * 512],
                               j == 0, j == ktiles - 1,
                               waits=w, inc=s_pe if j == ktiles - 1 else None)
                        pv_done[g] = s_pe.n

                    pend = []
                    for g in range(ktiles // 2):
                        bi = g % 2
                        ei = g % 3
                        stk = ps_sT[bi]
                        for sub in range(2):
                            j = 2 * g + sub
                            w = []
                            if sub == 0:
                                w = [(s_sc, sc_head), (s_ve, ve_head)]
                                if g >= 2:
                                    w.append((s_sc, exp_done[g - 2]))
                                else:
                                    w += buf_wait((id(stk), 0)) + buf_wait((id(stk), 512))
                            mm(stk[:, sub * 512:(sub + 1) * 512],
                               khT[:, j * 128:(j + 1) * 128], qhT[:, qsl],
                               True, True, waits=w,
                               inc=s_pe if sub == 1 else None)
                        if g == 1 and pend_z[0] is not None:
                            pend_z[0]()
                            pend_z[0] = None
                        we = [(s_pe, None), (s_ld, ld_base)]
                        if g >= 3:
                            we.append((s_ve, pt_done[g - 3]))
                        act(Ebuf[ei][:], stk[:], AF.Exp, scale=SC_SQRT,
                            waits=we, inc=s_sc)
                        exp_done[g] = s_sc.n
                        last_ev[(id(stk), 0)] = (s_sc, s_sc.n)
                        last_ev[(id(stk), 512)] = (s_sc, s_sc.n)
                        wv = [(s_sc, None)]
                        if g >= 3:
                            wv.append((s_pe, pv_done[g - 3]))
                        vops = [lambda e, o=Pbuf[ei][:], i=Ebuf[ei][:]:
                                e.tensor_scalar(o, i, PLO, PHI, ALU.max, ALU.min)]
                        for sub in range(2):
                            j = 2 * g + sub
                            r = j - 4 * gc
                            if r >= 0:
                                vops.append(
                                    lambda e, o=Pbuf[ei][:, sub * 512:(sub + 1) * 512],
                                    m=masks[:, r * 512:(r + 1) * 512]:
                                    e.tensor_mul(o, o, m))
                        for k, vo in enumerate(vops):
                            P.op("vector", vo, waits=wv if k == 0 else (),
                                 inc=s_ve if k == len(vops) - 1 else None)
                        pt_done[g] = s_ve.n
                        pend.append(g)
                        if len(pend) > 2:
                            emit_pv(pend.pop(0))
                    while pend:
                        emit_pv(pend.pop(0))

                    def z_chain(po=po, h=h, cl=cl, gc=gc):
                        P.op("vector", lambda e, o=zrow[:], i=po[64:65, :]:
                             e.tensor_copy(out=o, in_=i), waits=[(s_pe, None)])
                        P.op("vector", lambda e: e.drain())
                        P.op("vector", lambda e, o=rz[:], i=zrow[:]:
                             e.reciprocal(o, i), inc=s_ve)
                        mm(ps_zb[0:64, :], ones64[:], rz[:], True, True,
                           waits=[(s_ve, None), (s_ld, ld_base)], inc=s_pe)
                        act(o_f32[:], po[0:64, :], AF.Copy, waits=[(s_pe, None)],
                            inc=s_sc)
                        P.op("vector", lambda e,
                             o=ot_sb[64 * (h % 2):64 * (h % 2) + 64,
                                     (h // 2) * 2048 + cl * 512:(h // 2) * 2048 + (cl + 1) * 512],
                             a=o_f32[:], b=ps_zb[0:64, :]:
                             e.tensor_mul(o, a, b),
                             waits=[(s_sc, None), (s_pe, None)], inc=s_ve)
                        o_free[gc % 2] = (s_ve, s_ve.n)

                    pend_z[0] = z_chain

        if pend_z[0] is not None:
            pend_z[0]()
            pend_z[0] = None
        pe_heads = s_pe.n
        ve_heads = s_ve.n
        sth.close()

        stw = ExitStack()
        wot = stw.enter_context(nc.sbuf_tensor("wot_s%d" % half, [128, 8 * D], BF16))
        resid = stw.enter_context(nc.sbuf_tensor("res%d" % half, [128, 2 * D], F32))
        Xb = [stw.enter_context(nc.sbuf_tensor("X%d_%d" % (half, i), [128, D], F32))
              for i in range(2)]
        x1o = [stw.enter_context(nc.sbuf_tensor("x1o%d_%d" % (half, i), [128, D], F32))
               for i in range(2)]
        stats = [stw.enter_context(nc.sbuf_tensor("stt%d_%d" % (half, i), [128, 10], F32))
                 for i in range(2)]

        dma("gpsimd", ch_out(wot, D), ch_in(wot_d), s_gp, waits=[(s_ve, ve_heads)])
        gp_wot = s_gp.n

        add_done = [0, 0]
        scr_done = [0, 0]
        store_done = [0, 0]
        for tt in range(16):
            bi = tt % 2
            row0 = half * 2048 + tt * 128
            dma("sync", resid[:, bi * D:(bi + 1) * D], x_d[row0:row0 + 128, :],
                s_ld,
                waits=[(s_ve, scr_done[bi])] if tt >= 2 else [(s_ve, ve_heads)])
            ld_res = s_ld.n
            pw = ps_sT[bi]
            for ok in range(8):
                for dc in range(2):
                    w = []
                    if ok == 0 and dc == 0:
                        w = [(s_gp, gp_wot), (s_pe, pe_heads), (s_ve, ve_heads)]
                        if tt >= 2:
                            w.append((s_ve, add_done[bi]))
                        else:
                            w += buf_wait((id(pw), 0)) + buf_wait((id(pw), 512))
                    mm(pw[:, dc * 512:(dc + 1) * 512],
                       ot_sb[:, ok * 2048 + tt * 128: ok * 2048 + (tt + 1) * 128],
                       wot[:, ok * D + dc * 512: ok * D + (dc + 1) * 512],
                       ok == 0, ok == 7, waits=w,
                       inc=s_pe if (ok == 7 and dc == 1) else None)
            X = Xb[bi]
            P.op("vector", lambda e, o=X[:], a=pw[:], b=resid[:, bi * D:(bi + 1) * D]:
                 e.tensor_add(o, a, b), waits=[(s_pe, None), (s_ld, ld_res)],
                 inc=s_ve)
            add_done[bi] = s_ve.n
            last_ev[(id(pw), 0)] = (s_ve, s_ve.n)
            last_ev[(id(pw), 512)] = (s_ve, s_ve.n)
            scr, _ = ln_emit(X, stats[bi], x1o[bi], resid[:, bi * D:(bi + 1) * D],
                             store_wait=(s_st, store_done[bi]))
            scr_done[bi] = scr
            P.op("scalar", lambda e, o=x1_d[row0:row0 + 128, :], i=x1o[bi][:]:
                 e.dma_start(out=o, in_=i), waits=[(s_sc, None)],
                 inc=s_st, inc_amt=16)
            store_done[bi] = s_st.n
            x1st_cnt.append(s_st.n)
        stw.close()

    run_half(0)
    x1bf_cnt = [0] * NPASS
    for p in range(4):
        dma("gpsimd", x1bf_d[p * PT_:(p + 1) * PT_, :],
            x1_d[p * PT_:(p + 1) * PT_, :], s_gp,
            waits=[(s_st, x1st_cnt[4 * (p + 1)])])
        x1bf_cnt[p] = s_gp.n
    run_half(1)
    att_st_done = s_st.n
    stA.close()
    for p in range(4, NPASS):
        dma("gpsimd", x1bf_d[p * PT_:(p + 1) * PT_, :],
            x1_d[p * PT_:(p + 1) * PT_, :], s_gp,
            waits=[(s_st, x1st_cnt[4 * (p + 1)])])
        x1bf_cnt[p] = s_gp.n

    stm = ExitStack()
    w1sb = stm.enter_context(nc.sbuf_tensor("w1sb", [128, 8 * DFF], BF16))
    w2sb = stm.enter_context(nc.sbuf_tensor("w2sb", [128, 32 * D], BF16))
    h_sb = stm.enter_context(nc.sbuf_tensor("h_sb", [128, 32 * 512], BF16))
    y_acc = stm.enter_context(nc.sbuf_tensor("y_acc", [128, 4 * D], F32))
    x1t = [stm.enter_context(nc.sbuf_tensor("x1t_s%d" % i, [128, 8 * 512], BF16))
           for i in range(2)]
    lg_t = stm.enter_context(nc.sbuf_tensor("lg_t", [128, 32], F32))
    combine = [stm.enter_context(nc.sbuf_tensor("combine%d" % i, [128, 32], F32))
               for i in range(2)]
    combT = stm.enter_context(nc.sbuf_tensor("combT", [8, 512], BF16))
    logT = stm.enter_context(nc.sbuf_tensor("logT", [8, 512], F32))
    mres = stm.enter_context(nc.sbuf_tensor("mres", [128, D], F32))
    mout = stm.enter_context(nc.sbuf_tensor("mout", [128, D], F32))
    mstat = stm.enter_context(nc.sbuf_tensor("mstat", [128, 10], F32))
    rstat = stm.enter_context(nc.sbuf_tensor("rstat", [128, 32], F32))
    wexp_t = stm.enter_context(nc.sbuf_tensor("wexp_t", [128, 32], F32))
    wsm_t = stm.enter_context(nc.sbuf_tensor("wsm_t", [128, 32], F32))
    b2_sb = stm.enter_context(nc.sbuf_tensor("b2_sb", [NE, D], BF16))
    b1_sb = stm.enter_context(nc.sbuf_tensor("b1_sb", [128, NE * 32], F32))

    ps_h = [psA, psB]
    ps_ya = [ps_sT[0], ps_sT[1]]
    last_ev[(id(psA), 0)] = (s_st, att_st_done)
    last_ev[(id(psB), 0)] = (s_st, att_st_done)

    pass_mm1_end = [0] * (NPASS + 1)
    mm1_done = [s_pe.n]
    mm2_done = [s_pe.n]
    w1_ld = [0]
    w2_ld = [0]

    dma("gpsimd", b2_sb[:], b2_d, s_gp, waits=[(s_st, att_st_done)])
    gp_b2 = s_gp.n
    dma("sync", b1_sb[:], b1t_d, s_ld, waits=[(s_st, att_st_done)])
    dma("sync", ch_out(w1sb, DFF), ch_in(w1bf_d[0]), s_ld,
        waits=[(s_gp, gp_conv[0])])
    w1_ld[-1] = s_ld.n
    dma("sync", ch_out(w2sb, D), ch_in(w2bf_d[0]), s_ld)
    w2_ld[-1] = s_ld.n
    ld_moe_consts = s_ld.n

    sm_done = [0]

    def emit_router(p, inject=None):
        cmb = combine[p % 2]
        xt = x1t[p % 2]
        rt = ps_ya[0]
        for dk in range(8):
            w = [(s_ld, None), (s_ve, None), (s_gp, gp_proj)] if dk == 0 else []
            mm(rt[0:8, 0:512], wg_sb[:, dk * 8:(dk + 1) * 8],
               xt[:, dk * 512:(dk + 1) * 512], dk == 0, dk == 7,
               waits=w, inc=s_pe if dk == 7 else None)
        act(logT[:], rt[0:8, 0:512], AF.Identity, bias=bg_sb[:],
            waits=[(s_pe, None), (s_ld, ld_moe_consts)], inc=s_sc)

        def emit_ltrans():
            for q4 in range(4):
                P.op("tensor", lambda e, o=rt[:, 512:520],
                     i=logT[:, q4 * 128:(q4 + 1) * 128], idt=id8[:]:
                     e.transpose(o, i, idt), waits=[(s_sc, None)], inc=s_pe)
                act(lg_t[:, q4 * 8:(q4 + 1) * 8], rt[:, 512:520], AF.Copy,
                    waits=[(s_pe, None)], inc=s_sc)

        def emit_softmax():
            LG = [lg_t[:, q * 8:(q + 1) * 8] for q in range(4)]
            CM = [cmb[:, q * 8:(q + 1) * 8] for q in range(4)]
            WX = [wexp_t[:, q * 8:(q + 1) * 8] for q in range(4)]
            WS = [wsm_t[:, q * 8:(q + 1) * 8] for q in range(4)]
            STc = lambda j, q: rstat[:, 4 * j + q:4 * j + q + 1]
            dr = lambda: P.op("vector", lambda e: e.drain())
            for q in range(4):
                P.op("vector", lambda e, o=STc(0, q), i=LG[q]:
                     e.reduce_max(o, i, axis=AX.X, negate=True),
                     waits=[(s_sc, None)] if q == 0 else (),
                     inc=s_ve if q == 3 else None)
            for q in range(4):
                act(WX[q], LG[q], AF.Exp, bias=STc(0, q),
                    waits=[(s_ve, None)] if q == 0 else (),
                    inc=s_sc if q == 3 else None)
            for q in range(4):
                P.op("vector", lambda e, o=STc(1, q), i=WX[q]:
                     e.reduce_sum(o, i, axis=AX.X),
                     waits=[(s_sc, None)] if q == 0 else ())
            dr()
            for q in range(4):
                P.op("vector", lambda e, o=STc(2, q), i=STc(1, q): e.reciprocal(o, i))
            dr()
            for q in range(4):
                P.op("vector", lambda e, o=WS[q], i=WX[q], sc=STc(2, q):
                     e.tensor_scalar(o, i, sc, None, ALU.mult))
            dr()
            for q in range(4):
                P.op("vector", lambda e, o=STc(3, q), i=WS[q]:
                     e.reduce_max(o, i, axis=AX.X))
            dr()
            for q in range(4):
                P.op("vector", lambda e, o=CM[q], i=WS[q], sc=STc(3, q):
                     e.tensor_scalar(o, i, sc, None, ALU.is_lt))
            dr()
            for q in range(4):
                P.op("vector", lambda e, o=CM[q], a=WS[q], b=CM[q]:
                     e.tensor_mul(o, a, b))
            dr()
            for q in range(4):
                P.op("vector", lambda e, o=STc(4, q), i=CM[q]:
                     e.reduce_max(o, i, axis=AX.X))
            dr()
            for q in range(4):
                P.op("vector", lambda e, o=STc(5, q), a=STc(3, q), b=STc(4, q):
                     e.tensor_add(o, a, b))
            for q in range(4):
                P.op("vector", lambda e, o=CM[q], i=WS[q], sc=STc(4, q):
                     e.tensor_scalar(o, i, sc, None, ALU.is_ge))
            dr()
            for q in range(4):
                P.op("vector", lambda e, o=STc(6, q), i=STc(5, q):
                     e.reciprocal(o, i))
            for q in range(4):
                P.op("vector", lambda e, o=CM[q], a=WS[q], b=CM[q]:
                     e.tensor_mul(o, a, b))
            dr()
            for q in range(4):
                P.op("vector", lambda e, o=CM[q], i=CM[q], sc=STc(6, q):
                     e.tensor_scalar(o, i, sc, None, ALU.mult),
                     inc=s_ve if q == 3 else None)
            sm_done[0] = s_ve.n

        def emit_ctrans():
            ve_comb = sm_done[0]
            for q4 in range(4):
                P.op("tensor", lambda e, o=rt[0:8, 512:640],
                     i=cmb[:, q4 * 8:(q4 + 1) * 8], idt=id128[:]:
                     e.transpose(o, i, idt),
                     waits=[(s_ve, ve_comb), (s_sc, None)], inc=s_pe)
                act(combT[:, q4 * 128:(q4 + 1) * 128], rt[0:8, 512:640], AF.Copy,
                    waits=[(s_pe, None)], inc=s_sc)

        def emit_yinit(ve_scr_gate):
            for tm in range(4):
                for dc in range(2):
                    w = [(s_sc, None), (s_gp, gp_b2)] if dc == 0 else ()
                    mm(rt[:, dc * 512:(dc + 1) * 512],
                       combT[:, tm * 128:(tm + 1) * 128],
                       b2_sb[:, dc * 512:(dc + 1) * 512], True, True,
                       waits=w, inc=s_pe if dc == 1 else None)
                act(y_acc[:, tm * D:(tm + 1) * D], rt[:], AF.Copy,
                    waits=[(s_pe, None), (s_ve, ve_scr_gate)], inc=s_sc)

        return emit_ltrans, emit_softmax, emit_ctrans, emit_yinit

    ve_yacc_scr = [0]
    ln_prev = [None]

    def emit_pass_ln(p):
        for tm in range(4):
            row0 = p * PT_ + tm * 128
            dma("sync", mres[:], x1_d[row0:row0 + 128, :], s_ld,
                waits=[(s_st, None)])
            P.op("vector", lambda e, o=mout[:], a=y_acc[:, tm * D:(tm + 1) * D],
                 b=mres[:]:
                 e.tensor_add(o, a, b), waits=[(s_ld, None), (s_ve, None)],
                 inc=s_ve)
            scr, _ = ln_emit(mout, mstat, mres, y_acc[:, tm * D:(tm + 1) * D])
            ve_yacc_scr[0] = scr
            P.op("scalar", lambda e, o=out_d[row0:row0 + 128, :], i=mres[:]:
                 e.dma_start(out=o, in_=i), waits=[(s_sc, None)],
                 inc=s_st, inc_amt=16)

    for p in range(NPASS):
        xt = x1t[p % 2]
        for k in range(8):
            w = []
            if k == 0:
                w = [(s_gp, x1bf_cnt[p]), (s_st, att_st_done),
                     (s_pe, pass_mm1_end[max(0, p - 1)])]
            dma("sync", xt[:, k * 512:(k + 1) * 512],
                x1bf_d[p * PT_:(p + 1) * PT_, k * 128:(k + 1) * 128],
                s_ld, waits=w, transpose=True)

        e_lt, e_sm, e_ct, e_yi = emit_router(p)
        if p == 0:
            e_lt()
            e_sm()
            e_ct()
            e_yi(ve_yacc_scr[0])
        else:
            e_sm_pend = (e_lt, e_sm, e_ct, e_yi)

        for e_ in range(NE):
            if not (p == 0 and e_ == 0):
                dma("sync", ch_out(w1sb, DFF), ch_in(w1bf_d[e_]),
                    s_ld, waits=[(s_pe, mm1_done[-1])])
                w1_ld.append(s_ld.n)
                dma("sync", ch_out(w2sb, D), ch_in(w2bf_d[e_]),
                    s_ld, waits=[(s_pe, mm2_done[-1])])
                w2_ld.append(s_ld.n)
            for fm in range(32):
                buf = ps_h[fm % 2]
                key = (id(buf), 0)
                for dk in range(8):
                    w = []
                    if dk == 0:
                        w = [(s_ld, w1_ld[-1])] + buf_wait(key)
                    mm(buf[:], w1sb[:, dk * DFF + fm * 128: dk * DFF + (fm + 1) * 128],
                       xt[:, dk * 512:(dk + 1) * 512], dk == 0, dk == 7,
                       waits=w, inc=s_pe if dk == 7 else None)
                act(h_sb[:, fm * 512:(fm + 1) * 512], buf[:], AF.Relu,
                    bias=b1_sb[:, e_ * 32 + fm: e_ * 32 + fm + 1],
                    waits=[(s_pe, None), (s_ld, ld_moe_consts)], inc=s_h2)
                last_ev[key] = (s_h2, s_h2.n)
                if p > 0 and e_ == 0:
                    if fm == 3:
                        e_sm_pend[0]()
                    elif fm == 5:
                        e_sm_pend[1]()
                        emit_pass_ln(p - 1)
                    elif fm == 24:
                        e_sm_pend[2]()
                    elif fm == 29:
                        e_sm_pend[3](ve_yacc_scr[0])
            mm1_done.append(s_pe.n)
            if e_ == NE - 1:
                pass_mm1_end[p] = s_pe.n
            sc_h = s_h2.n
            for half2 in range(2):
                for fk in range(32):
                    for tmi in range(2):
                        tm = half2 * 2 + tmi
                        for dc in range(2):
                            w = []
                            if fk == 0 and tmi == 0 and dc == 0:
                                w = [(s_ld, w2_ld[-1]), (s_h2, sc_h), (s_ve, None)]
                            mm(ps_ya[tmi][:, dc * 512:(dc + 1) * 512],
                               h_sb[:, fk * 512 + tm * 128: fk * 512 + (tm + 1) * 128],
                               w2sb[:, fk * D + dc * 512: fk * D + (dc + 1) * 512],
                               fk == 0, fk == 31, waits=w,
                               inc=s_pe if (fk == 31 and tmi == 1 and dc == 1) else None)
                for tmi in range(2):
                    tm = half2 * 2 + tmi
                    P.op("vector", lambda e, o=y_acc[:, tm * D:(tm + 1) * D],
                         a=ps_ya[tmi][:],
                         s=combine[p % 2][:, tm * 8 + e_: tm * 8 + e_ + 1]:
                         e.scalar_tensor_tensor(o, a, s, o, ALU.mult, ALU.add),
                         waits=[(s_pe, None), (s_sc, None)] if tmi == 0 else [],
                         inc=s_ve)
            mm2_done.append(s_pe.n)

    emit_pass_ln(NPASS - 1)

    P.op("sync", lambda e: e.nop(), waits=[(s_st, s_st.n)])

    with nc.Block() as block:
        @block.sync
        def _(eng):
            for fn in P.ops["sync"]:
                fn(eng)

        @block.gpsimd
        def _(eng):
            for fn in P.ops["gpsimd"]:
                fn(eng)

        @block.vector
        def _(eng):
            for fn in P.ops["vector"]:
                fn(eng)

        @block.scalar
        def _(eng):
            for fn in P.ops["scalar"]:
                fn(eng)

        @block.tensor
        def _(eng):
            for fn in P.ops["tensor"]:
                fn(eng)

    stm.close()
    st.close()
    return nc


def _consts_np():
    inv_freq = 10000.0 ** (-np.arange(0, DHR, 2, dtype=np.float64) / DHR)
    t = np.arange(S, dtype=np.float64)
    ang = inv_freq[:, None] * t[None, :]
    C = np.repeat(np.cos(ang), 2, axis=0).astype(np.float32)
    sin = np.sin(ang)
    Spm = np.empty((DHR, S), np.float64)
    Spm[0::2] = -sin
    Spm[1::2] = sin
    masks = np.zeros((128, 4 * 512), np.float32)
    jj = np.arange(128)[:, None]
    ii = np.arange(512)[None, :]
    for r in range(4):
        masks[:, r * 512:(r + 1) * 512] = (jj + 128 * r <= ii).astype(np.float32)
    consts = np.zeros((128, 4), np.float32)
    consts[:, 0] = 1.0 / np.sqrt(96.0)
    consts[:, 1] = 1e-5
    consts[:, 2] = -0.5
    consts[:, 3] = 1.0 / D
    bf = ml_dtypes.bfloat16
    CS = np.concatenate([C, Spm.astype(np.float32)], axis=0)
    return (CS.astype(bf), masks.astype(bf), consts)


def _cat_swap(W, blk):
    n = W.shape[0] // blk
    out = np.empty((n * 2 * blk, W.shape[1]), W.dtype)
    perm = np.arange(blk) ^ 1
    for i in range(n):
        out[i * 2 * blk: i * 2 * blk + blk] = W[i * blk:(i + 1) * blk]
        out[i * 2 * blk + blk: (i + 1) * 2 * blk] = W[i * blk:(i + 1) * blk][perm]
    return out


def kernel(L0_attn_x, L0_attn_W_DQ, L0_attn_W_UQ, L0_attn_W_QR, L0_attn_W_DKV,
           L0_attn_W_UK, L0_attn_W_UV, L0_attn_W_KR, L0_attn_W_O,
           L0_ffn_W1, L0_ffn_b1, L0_ffn_W2, L0_ffn_b2, L0_ffn_Wg, L0_ffn_bg):
    nc = bass.Bass("TRN2", target_bir_lowering=False)
    build(nc)

    CS, masks, consts = _consts_np()
    asc = np.ascontiguousarray
    wqrc = _cat_swap(L0_attn_W_QR, DHR)
    wkrc = _cat_swap(L0_attn_W_KR, DHR)
    common = {
        "wdqt": asc(L0_attn_W_DQ.T), "wdkvt": asc(L0_attn_W_DKV.T),
        "wkrct": asc(wkrc.T), "wuqt": asc(L0_attn_W_UQ.T),
        "wqrct": asc(wqrc.T), "wukt": asc(L0_attn_W_UK.T),
        "wuvt": asc(L0_attn_W_UV.T), "wot": asc(L0_attn_W_O.T),
        "w1": asc(L0_ffn_W1), "w2": asc(L0_ffn_W2),
        "b1t": asc(L0_ffn_b1.reshape(NE, 32, 128).transpose(2, 0, 1).reshape(128, NE * 32)),
        "b2": asc(L0_ffn_b2),
        "wgt": asc(L0_ffn_Wg.reshape(8, 128, 8).transpose(1, 0, 2).reshape(128, 64)),
        "bg": asc(L0_ffn_bg.reshape(NE, 1)),
        "ropecs": CS, "masks": masks,
        "id8": np.eye(8, dtype=np.float32), "id128": np.eye(128, dtype=np.float32),
        "ones64": np.ones((1, 64), np.float32), "consts": consts,
    }
    in_maps = []
    for c in range(8):
        m = dict(common)
        xc = asc(np.asarray(L0_attn_x[c], dtype=np.float32))
        m["x"] = xc
        m["xT"] = asc(xc.T)
        in_maps.append(m)
    res = run_bass_kernel_spmd(nc, in_maps, core_ids=list(range(8)))
    return np.stack([np.asarray(r["out"], dtype=np.float32) for r in res.results])


# version 07
# speedup vs baseline: 1.3188x; 1.0806x over previous
import numpy as np
import ml_dtypes
import concourse.bass as bass
import concourse.mybir as mybir
from concourse.bass_utils import run_bass_kernel_spmd
from contextlib import ExitStack

F32 = mybir.dt.float32
BF16 = mybir.dt.bfloat16
AF = mybir.ActivationFunctionType
ALU = mybir.AluOpType
AX = mybir.AxisListType

S, D, H, DH, DHR, LAT = 4096, 1024, 16, 64, 32, 512
DFF, NE = 4096, 8
TAU = float(np.sqrt(96.0))
PLO = float(np.exp(-80.0 / TAU))
PHI = float(np.exp(80.0 / TAU))
NPASS = 8
PT_ = S // NPASS
VSL = 260


class Sem:
    def __init__(self, h):
        self.h = h
        self.n = 0


class Prog:
    def __init__(self):
        self.ops = {k: [] for k in ("sync", "gpsimd", "vector", "scalar", "tensor")}

    def op(self, eng, f, waits=(), inc=None, inc_amt=1):
        wl = [(s.h, s.n if tgt is None else tgt) for (s, tgt) in waits]
        if inc is not None:
            inc.n += inc_amt
            pair = (inc.h, inc_amt)
        else:
            pair = None

        def fn(e, wl=wl, f=f, pair=pair):
            for h, v in wl:
                e.wait_ge(h, v)
            inst = f(e)
            if pair is not None:
                inst.then_inc(*pair)
        self.ops[eng].append(fn)


def build(nc):
    x_d = nc.dram_tensor("x", [S, D], F32, kind="ExternalInput").ap()
    xT_d = nc.dram_tensor("xT", [D, S], F32, kind="ExternalInput").ap()
    wdqt_d = nc.dram_tensor("wdqt", [D, LAT], F32, kind="ExternalInput").ap()
    wdkvt_d = nc.dram_tensor("wdkvt", [D, LAT], F32, kind="ExternalInput").ap()
    wkrct_d = nc.dram_tensor("wkrct", [D, 64], F32, kind="ExternalInput").ap()
    wuqt_d = nc.dram_tensor("wuqt", [LAT, D], F32, kind="ExternalInput").ap()
    wqrct_d = nc.dram_tensor("wqrct", [LAT, D], F32, kind="ExternalInput").ap()
    wukt_d = nc.dram_tensor("wukt", [LAT, D], F32, kind="ExternalInput").ap()
    wuvt_d = nc.dram_tensor("wuvt", [LAT, D], F32, kind="ExternalInput").ap()
    wot_d = nc.dram_tensor("wot", [D, D], F32, kind="ExternalInput").ap()
    w1_d = nc.dram_tensor("w1", [NE, D, DFF], F32, kind="ExternalInput").ap()
    w2_d = nc.dram_tensor("w2", [NE, DFF, D], F32, kind="ExternalInput").ap()
    b1t_d = nc.dram_tensor("b1t", [128, NE * 32], F32, kind="ExternalInput").ap()
    b2_d = nc.dram_tensor("b2", [NE, D], F32, kind="ExternalInput").ap()
    wgt_d = nc.dram_tensor("wgt", [128, 64], F32, kind="ExternalInput").ap()
    bg_d = nc.dram_tensor("bg", [NE, 1], F32, kind="ExternalInput").ap()
    ropecs_d = nc.dram_tensor("ropecs", [64, S], BF16, kind="ExternalInput").ap()
    masks_d = nc.dram_tensor("masks", [128, 4 * 512], BF16, kind="ExternalInput").ap()
    id8_d = nc.dram_tensor("id8", [8, 8], F32, kind="ExternalInput").ap()
    id128_d = nc.dram_tensor("id128", [128, 128], F32, kind="ExternalInput").ap()
    ones64_d = nc.dram_tensor("ones64", [1, 64], F32, kind="ExternalInput").ap()
    consts_d = nc.dram_tensor("consts", [128, 4], F32, kind="ExternalInput").ap()

    out_d = nc.dram_tensor("out", [S, D], F32, kind="ExternalOutput").ap()

    w1bf_d = nc.dram_tensor("w1bf", [NE, D, DFF], BF16, kind="Internal").ap()
    w2bf_d = nc.dram_tensor("w2bf", [NE, DFF, D], BF16, kind="Internal").ap()
    x1_d = nc.dram_tensor("x1f", [S, D], F32, kind="Internal").ap()
    x1bf_d = nc.dram_tensor("x1bf", [S, D], BF16, kind="Internal").ap()

    P = Prog()
    st = ExitStack()
    sb = lambda n, s_, d_: st.enter_context(nc.sbuf_tensor(n, s_, d_))
    psal = lambda n, s_: st.enter_context(nc.psum_tensor(n, s_, F32))
    sem = lambda n: Sem(st.enter_context(nc.semaphore(n)))
    s_ld = sem("s_ld")
    s_gp = sem("s_gp")
    s_pe = sem("s_pe")
    s_sc = sem("s_sc")
    s_ve = sem("s_ve")
    s_st = sem("s_st")
    s_h2 = sem("s_h2")

    consts = sb("consts_s", [128, 4], F32)
    id8 = sb("id8_s", [8, 8], F32)
    id128 = sb("id128_s", [128, 128], F32)
    ones64 = sb("ones64_s", [1, 64], F32)
    wg_sb = sb("wg_sb", [128, 64], BF16)
    bg_sb = sb("bg_sb", [NE, 1], F32)

    def dma(eng, out, in_, semc, waits=(), **kw):
        P.op(eng, lambda e, o=out, i=in_, kw=kw: e.dma_start(out=o, in_=i, **kw),
             waits=waits, inc=semc, inc_amt=16)

    def ch_in(dram_ap):
        return dram_ap.rearrange("(k p) t -> p k t", p=128)

    def ch_out(sb_t, inner):
        return sb_t[:].rearrange("p (k t) -> p k t", t=inner)

    dma("sync", consts[:], consts_d, s_ld)
    dma("sync", id8[:], id8_d, s_ld)
    dma("sync", id128[:], id128_d, s_ld)
    dma("sync", ones64[:], ones64_d, s_ld)
    dma("sync", bg_sb[:], bg_d, s_ld)
    SC_SQRT = consts[:, 0:1]
    SC_EPS = consts[:, 1:2]
    SC_NHALF = consts[:, 2:3]

    stA = ExitStack()
    sba = lambda n, sh, d_: stA.enter_context(nc.sbuf_tensor(n, sh, d_))
    cqt = sba("cqt", [128, 4 * S], BF16)
    ckvt = sba("ckvt", [128, 4 * S], BF16)
    krt = sba("krt", [DHR, S], BF16)
    tabCS = sba("tabCS", [64, S], BF16)
    masks = sba("masks_s", [128, 4 * 512], BF16)
    qrs = [sba("qrs%d" % i, [64, 512], BF16) for i in range(2)]
    rt1 = sba("rt1", [DHR, 512], BF16)
    rt2 = sba("rt2", [DHR, 512], BF16)

    dma("sync", tabCS[:], ropecs_d, s_ld)
    dma("sync", masks[:], masks_d, s_ld)
    dma("gpsimd", wg_sb[:], wgt_d, s_gp)

    psA = psal("psA", [128, 512])
    psB = psal("psB", [128, 512])
    ps_sT = [psal("ps_sT0", [128, 1024]), psal("ps_sT1", [128, 1024])]
    ps_oc = [psal("ps_o0", [96, 512]), psal("ps_o1", [96, 512])]
    ps_zb = psB

    ld_base = s_ld.n

    last_ev = {}

    def mm(out, lhsT, rhs, start, stop, waits=(), inc=None):
        P.op("tensor",
             lambda e, o=out, a=lhsT, b=rhs, s1=start, s2=stop:
             e.matmul(o, a, b, start=s1, stop=s2), waits=waits, inc=inc)

    def act(out, in_, func, bias=0.0, scale=1.0, waits=(), inc=None):
        P.op("scalar",
             lambda e, o=out, i=in_, f=func, b=bias, s=scale:
             e.activation(out=o, in_=i, func=f, bias=b, scale=s),
             waits=waits, inc=inc)

    pp4 = [(ps_sT[0], 0), (ps_sT[0], 512), (ps_sT[1], 0), (ps_sT[1], 512)]
    pi = [0]

    def nbuf():
        b = pp4[pi[0] % 4]
        pi[0] += 1
        return b

    def buf_wait(key):
        v = last_ev.get(key)
        return [v] if v is not None else []

    def evict(dst, key, src_ap, func=AF.Copy, bias=0.0, extra_waits=()):
        act(dst, src_ap, func, bias=bias,
            waits=[(s_pe, None)] + list(extra_waits), inc=s_sc)
        last_ev[key] = (s_sc, s_sc.n)

    stp = ExitStack()
    xt_sb = stp.enter_context(nc.sbuf_tensor("xt_sb", [128, 8 * S], BF16))
    wdqt = stp.enter_context(nc.sbuf_tensor("wdqt_s", [128, 8 * LAT], BF16))
    wdkvt = stp.enter_context(nc.sbuf_tensor("wdkvt_s", [128, 8 * LAT], BF16))
    wkrct = stp.enter_context(nc.sbuf_tensor("wkrct_s", [128, 8 * 64], BF16))

    dma("gpsimd", ch_out(xt_sb, S), ch_in(xT_d), s_gp)
    dma("gpsimd", ch_out(wdqt, LAT), ch_in(wdqt_d), s_gp)
    dma("gpsimd", ch_out(wdkvt, LAT), ch_in(wdkvt_d), s_gp)
    dma("gpsimd", ch_out(wkrct, 64), ch_in(wkrct_d), s_gp)
    gp_proj = s_gp.n

    w0 = [(s_gp, gp_proj), (s_ld, ld_base)]

    def proj128(dst, dst_col0, wt, wt_cols, row0, M):
        for n in range(S // 512):
            t, off = nbuf()
            key = (id(t), off)
            w = w0 + buf_wait(key)
            for dk in range(8):
                mm(t[0:M, off:off + 512],
                   wt[:, dk * wt_cols + row0: dk * wt_cols + row0 + M],
                   xt_sb[:, dk * S + n * 512: dk * S + (n + 1) * 512],
                   dk == 0, dk == 7, waits=w if dk == 0 else (),
                   inc=s_pe if dk == 7 else None)
            evict(dst[:, dst_col0 + n * 512: dst_col0 + (n + 1) * 512],
                  key, t[0:M, off:off + 512])

    for lm in range(4):
        proj128(cqt, lm * S, wdqt, LAT, lm * 128, 128)
    for lm in range(4):
        proj128(ckvt, lm * S, wdkvt, LAT, lm * 128, 128)

    def rope_emit(dst, stage, tab_c):
        P.op("vector", lambda e, o=rt1[:], a=stage[0:DHR, :],
             b=tabCS[0:DHR, tab_c:tab_c + 512]:
             e.tensor_mul(o, a, b), waits=[(s_sc, None), (s_ld, ld_base)])
        P.op("vector", lambda e, o=rt2[:], a=stage[DHR:64, :],
             b=tabCS[DHR:64, tab_c:tab_c + 512]:
             e.tensor_mul(o, a, b))
        P.op("vector", lambda e, o=dst, a=rt1[:], b=rt2[:]:
             e.tensor_add(o, a, b), inc=s_ve)

    for n in range(S // 512):
        t, off = nbuf()
        key = (id(t), off)
        w = w0 + buf_wait(key) + [(s_ve, None)]
        for dk in range(8):
            mm(t[0:64, off:off + 512], wkrct[:, dk * 64:(dk + 1) * 64],
               xt_sb[:, dk * S + n * 512: dk * S + (n + 1) * 512],
               dk == 0, dk == 7, waits=w if dk == 0 else (),
               inc=s_pe if dk == 7 else None)
        evict(qrs[n % 2][:], key, t[0:64, off:off + 512],
              extra_waits=[(s_ve, None)])
        rope_emit(krt[:, n * 512:(n + 1) * 512], qrs[n % 2], n * 512)

    pe_projdone = s_pe.n
    stp.close()

    ot_sb = sba("ot_sb", [128, 8 * 2048], BF16)

    x1st_cnt = [0]
    conv_emitted = [False]
    gp_conv = [0]

    def ln_emit(X, stt, out_f32, scratch, store_wait=None):
        sum1 = stt[:, 0:1]; ss = stt[:, 1:2]; mean = stt[:, 2:3]
        mvar = stt[:, 3:4]; m2 = stt[:, 4:5]; var = stt[:, 5:6]
        lnv = stt[:, 6:7]; rstd = stt[:, 7:8]; nb = stt[:, 8:9]
        P.op("vector", lambda e, o=scratch, a=X[:], b=ss:
             e.scalar_tensor_tensor(o, a, 1.0, a, ALU.mult, ALU.mult, accum_out=b))
        P.op("vector", lambda e, o=sum1, i=X[:]: e.reduce_sum(o, i, axis=AX.X),
             inc=s_ve)
        ve_scr = s_ve.n
        P.op("vector", lambda e: e.drain())
        P.op("vector", lambda e, o=mean, i=sum1:
             e.tensor_scalar(o, i, 1.0 / D, None, ALU.mult))
        P.op("vector", lambda e, o=mvar, i=ss:
             e.tensor_scalar(o, i, 1.0 / D, None, ALU.mult))
        P.op("vector", lambda e: e.drain())
        P.op("vector", lambda e, o=m2, a=mean, b=mean: e.tensor_mul(o, a, b))
        P.op("vector", lambda e: e.drain())
        P.op("vector", lambda e, o=var, a=mvar, b=m2: e.tensor_sub(o, a, b),
             inc=s_ve)
        act(lnv, var, AF.Ln, bias=SC_EPS, waits=[(s_ve, None)])
        act(rstd, lnv, AF.Exp, scale=SC_NHALF, inc=s_sc)
        P.op("vector", lambda e, o=nb, i=rstd, m=mean:
             e.tensor_scalar(o, i, m, -1.0, ALU.mult, ALU.mult),
             waits=[(s_sc, None)])
        P.op("vector", lambda e: e.drain(), inc=s_ve)
        aw = [(s_ve, None)]
        if store_wait is not None:
            aw.append(store_wait)
        act(out_f32[:], X[:], AF.Identity, bias=nb, scale=rstd, waits=aw,
            inc=s_sc)
        return ve_scr, s_sc.n

    def run_half(half):
        kj_end = 2048 * (half + 1)
        nkt = kj_end // 128
        qi0 = half * 2048

        sth = ExitStack()
        wuqt = sth.enter_context(nc.sbuf_tensor("wuqt_s%d" % half, [128, 4 * D], BF16))
        wqrct = sth.enter_context(nc.sbuf_tensor("wqrct_s%d" % half, [128, 4 * D], BF16))
        wukt = sth.enter_context(nc.sbuf_tensor("wukt_s%d" % half, [128, 4 * D], BF16))
        wuvt = sth.enter_context(nc.sbuf_tensor("wuvt_s%d" % half, [128, 4 * D], BF16))
        khT = sth.enter_context(nc.sbuf_tensor("khT%d" % half, [96, S], BF16))
        qhT = sth.enter_context(nc.sbuf_tensor("qhT%d" % half, [96, 2048], BF16))
        v4 = sth.enter_context(nc.sbuf_tensor("v4_%d" % half, [128, 32 * VSL + 36], BF16))
        Ebuf = [sth.enter_context(nc.sbuf_tensor("E%d_%d" % (half, i), [128, 1024], BF16))
                for i in range(3)]
        Pbuf = [sth.enter_context(nc.sbuf_tensor("P%d_%d" % (half, i), [128, 1024], BF16))
                for i in range(3)]
        o_f32 = sth.enter_context(nc.sbuf_tensor("of%d" % half, [64, 512], F32))
        zrow = sth.enter_context(nc.sbuf_tensor("zr%d" % half, [1, 512], F32))
        rz = sth.enter_context(nc.sbuf_tensor("rz%d" % half, [1, 512], F32))

        if half == 0:
            wq_wait = [(s_gp, gp_proj), (s_pe, pe_projdone)]
        else:
            wq_wait = [(s_st, None)]
        dma("gpsimd", ch_out(wuqt, D), ch_in(wuqt_d), s_gp, waits=wq_wait)
        dma("gpsimd", ch_out(wqrct, D), ch_in(wqrct_d), s_gp)
        dma("gpsimd", ch_out(wukt, D), ch_in(wukt_d), s_gp)
        dma("gpsimd", ch_out(wuvt, D), ch_in(wuvt_d), s_gp)
        gp_wu = s_gp.n

        if not conv_emitted[0]:
            for e in range(NE):
                dma("gpsimd", w1bf_d[e], w1_d[e], s_gp)
                dma("gpsimd", w2bf_d[e], w2_d[e], s_gp)
            gp_conv[0] = s_gp.n
            conv_emitted[0] = True

        vw = [(s_pe, pe_projdone)] if half == 0 else [(s_st, None)]
        for hh in range(4):
            P.op("vector", lambda e, o=v4[:, hh * 65 + 64: 32 * VSL: VSL]:
                 e.memset(o, 1.0), waits=vw if hh == 0 else ())
        P.op("vector", lambda e, o=v4[:, 32 * VSL: 32 * VSL + 36]:
             e.memset(o, 0.0), inc=s_ve)
        ve_vinit = s_ve.n
        P.op("vector", lambda e, o=khT[64:96, 0:kj_end], i=krt[:, 0:kj_end]:
             e.tensor_copy(out=o, in_=i),
             waits=(vw if half else []) + [(s_ve, None)], inc=s_ve)

        o_free = {}
        pend_z = [None]
        for hg in range(4):
            for tt in range(nkt):
                t, off = nbuf()
                key = (id(t), off)
                w = buf_wait(key) + [(s_gp, gp_wu), (s_ve, ve_vinit)]
                for lk in range(4):
                    mm(t[:, off:off + 256],
                       ckvt[:, lk * S + tt * 128: lk * S + (tt + 1) * 128],
                       wuvt[:, lk * D + hg * 256: lk * D + hg * 256 + 256],
                       lk == 0, lk == 3, waits=w if lk == 0 else (),
                       inc=s_pe if lk == 3 else None)
                act(v4[:, tt * VSL: tt * VSL + VSL]
                    .rearrange("p (b c) -> p b c", c=65)[:, :, 0:64],
                    t[:, off:off + 256].rearrange("p (b c) -> p b c", c=64),
                    AF.Copy, waits=[(s_pe, None)], inc=s_sc)
                last_ev[key] = (s_sc, s_sc.n)

            for hh in range(4):
                h = hg * 4 + hh
                for n in range(4):
                    t, off = nbuf()
                    key = (id(t), off)
                    w = buf_wait(key)
                    for lk in range(4):
                        mm(t[0:64, off:off + 512],
                           wuqt[:, lk * D + h * 64: lk * D + h * 64 + 64],
                           cqt[:, lk * S + qi0 + n * 512: lk * S + qi0 + (n + 1) * 512],
                           lk == 0, lk == 3, waits=w if lk == 0 else (),
                           inc=s_pe if lk == 3 else None)
                    evict(qhT[0:64, n * 512:(n + 1) * 512], key,
                          t[0:64, off:off + 512])
                for n in range(4):
                    t, off = nbuf()
                    key = (id(t), off)
                    w = buf_wait(key)
                    for lk in range(4):
                        mm(t[0:64, off:off + 512],
                           wqrct[:, lk * D + h * 64: lk * D + h * 64 + 64],
                           cqt[:, lk * S + qi0 + n * 512: lk * S + qi0 + (n + 1) * 512],
                           lk == 0, lk == 3, waits=w if lk == 0 else (),
                           inc=s_pe if lk == 3 else None)
                    evict(qrs[n % 2][:], key, t[0:64, off:off + 512],
                          extra_waits=[(s_ve, None)])
                    rope_emit(qhT[64:96, n * 512:(n + 1) * 512], qrs[n % 2],
                              qi0 + n * 512)
                for n in range(kj_end // 512):
                    t, off = nbuf()
                    key = (id(t), off)
                    w = buf_wait(key)
                    for lk in range(4):
                        mm(t[0:64, off:off + 512],
                           wukt[:, lk * D + h * 64: lk * D + h * 64 + 64],
                           ckvt[:, lk * S + n * 512: lk * S + (n + 1) * 512],
                           lk == 0, lk == 3, waits=w if lk == 0 else (),
                           inc=s_pe if lk == 3 else None)
                    evict(khT[0:64, n * 512:(n + 1) * 512], key,
                          t[0:64, off:off + 512])
                sc_head = s_sc.n
                ve_head = s_ve.n

                for cl in range(4):
                    gc = half * 4 + cl
                    ktiles = 4 * gc + 4
                    qsl = slice(cl * 512, (cl + 1) * 512)
                    po = ps_oc[gc % 2]
                    exp_done = {}
                    pt_done = {}
                    pv_done = {}

                    def emit_pv(g, ktiles=ktiles, po=po, gc=gc):
                        ei = g % 3
                        for sub in range(2):
                            j = 2 * g + sub
                            w = [(s_ve, pt_done[g])] if sub == 0 else []
                            if sub == 0 and g == 0:
                                w.append(o_free.get(gc % 2, (s_ve, 0)))
                                w.append((s_sc, None))
                            mm(po[:], v4[:, j * VSL + hh * 65: j * VSL + hh * 65 + 96],
                               Pbuf[ei][:, sub * 512:(sub + 1) * 512],
                               j == 0, j == ktiles - 1,
                               waits=w, inc=s_pe if j == ktiles - 1 else None)
                        pv_done[g] = s_pe.n

                    pend = []
                    for g in range(ktiles // 2):
                        bi = g % 2
                        ei = g % 3
                        stk = ps_sT[bi]
                        for sub in range(2):
                            j = 2 * g + sub
                            w = []
                            if sub == 0:
                                w = [(s_sc, sc_head), (s_ve, ve_head)]
                                if g >= 2:
                                    w.append((s_sc, exp_done[g - 2]))
                                else:
                                    w += buf_wait((id(stk), 0)) + buf_wait((id(stk), 512))
                            mm(stk[:, sub * 512:(sub + 1) * 512],
                               khT[:, j * 128:(j + 1) * 128], qhT[:, qsl],
                               True, True, waits=w,
                               inc=s_pe if sub == 1 else None)
                        if g == 1 and pend_z[0] is not None:
                            pend_z[0]()
                            pend_z[0] = None
                        we = [(s_pe, None), (s_ld, ld_base)]
                        if g >= 3:
                            we.append((s_ve, pt_done[g - 3]))
                        act(Ebuf[ei][:], stk[:], AF.Exp, scale=SC_SQRT,
                            waits=we, inc=s_sc)
                        exp_done[g] = s_sc.n
                        last_ev[(id(stk), 0)] = (s_sc, s_sc.n)
                        last_ev[(id(stk), 512)] = (s_sc, s_sc.n)
                        wv = [(s_sc, None)]
                        if g >= 3:
                            wv.append((s_pe, pv_done[g - 3]))
                        vops = [lambda e, o=Pbuf[ei][:], i=Ebuf[ei][:]:
                                e.tensor_scalar(o, i, PLO, PHI, ALU.max, ALU.min)]
                        for sub in range(2):
                            j = 2 * g + sub
                            r = j - 4 * gc
                            if r >= 0:
                                vops.append(
                                    lambda e, o=Pbuf[ei][:, sub * 512:(sub + 1) * 512],
                                    m=masks[:, r * 512:(r + 1) * 512]:
                                    e.tensor_mul(o, o, m))
                        for k, vo in enumerate(vops):
                            P.op("vector", vo, waits=wv if k == 0 else (),
                                 inc=s_ve if k == len(vops) - 1 else None)
                        pt_done[g] = s_ve.n
                        pend.append(g)
                        if len(pend) > 2:
                            emit_pv(pend.pop(0))
                    while pend:
                        emit_pv(pend.pop(0))

                    def z_chain(po=po, h=h, cl=cl, gc=gc):
                        P.op("vector", lambda e, o=zrow[:], i=po[64:65, :]:
                             e.tensor_copy(out=o, in_=i), waits=[(s_pe, None)])
                        P.op("vector", lambda e: e.drain())
                        P.op("vector", lambda e, o=rz[:], i=zrow[:]:
                             e.reciprocal(o, i), inc=s_ve)
                        mm(ps_zb[0:64, :], ones64[:], rz[:], True, True,
                           waits=[(s_ve, None), (s_ld, ld_base)], inc=s_pe)
                        act(o_f32[:], po[0:64, :], AF.Copy, waits=[(s_pe, None)],
                            inc=s_sc)
                        P.op("vector", lambda e,
                             o=ot_sb[64 * (h % 2):64 * (h % 2) + 64,
                                     (h // 2) * 2048 + cl * 512:(h // 2) * 2048 + (cl + 1) * 512],
                             a=o_f32[:], b=ps_zb[0:64, :]:
                             e.tensor_mul(o, a, b),
                             waits=[(s_sc, None), (s_pe, None)], inc=s_ve)
                        o_free[gc % 2] = (s_ve, s_ve.n)

                    pend_z[0] = z_chain

        if pend_z[0] is not None:
            pend_z[0]()
            pend_z[0] = None
        pe_heads = s_pe.n
        ve_heads = s_ve.n
        sth.close()

        stw = ExitStack()
        wot = stw.enter_context(nc.sbuf_tensor("wot_s%d" % half, [128, 8 * D], BF16))
        resid = stw.enter_context(nc.sbuf_tensor("res%d" % half, [128, 2 * D], F32))
        Xb = [stw.enter_context(nc.sbuf_tensor("X%d_%d" % (half, i), [128, D], F32))
              for i in range(2)]
        x1o = [stw.enter_context(nc.sbuf_tensor("x1o%d_%d" % (half, i), [128, D], F32))
               for i in range(2)]
        stats = [stw.enter_context(nc.sbuf_tensor("stt%d_%d" % (half, i), [128, 10], F32))
                 for i in range(2)]

        dma("gpsimd", ch_out(wot, D), ch_in(wot_d), s_gp, waits=[(s_ve, ve_heads)])
        gp_wot = s_gp.n

        add_done = [0, 0]
        scr_done = [0, 0]
        store_done = [0, 0]
        for tt in range(16):
            bi = tt % 2
            row0 = half * 2048 + tt * 128
            dma("sync", resid[:, bi * D:(bi + 1) * D], x_d[row0:row0 + 128, :],
                s_ld,
                waits=[(s_ve, scr_done[bi])] if tt >= 2 else [(s_ve, ve_heads)])
            ld_res = s_ld.n
            pw = ps_sT[bi]
            for ok in range(8):
                for dc in range(2):
                    w = []
                    if ok == 0 and dc == 0:
                        w = [(s_gp, gp_wot), (s_pe, pe_heads), (s_ve, ve_heads)]
                        if tt >= 2:
                            w.append((s_ve, add_done[bi]))
                        else:
                            w += buf_wait((id(pw), 0)) + buf_wait((id(pw), 512))
                    mm(pw[:, dc * 512:(dc + 1) * 512],
                       ot_sb[:, ok * 2048 + tt * 128: ok * 2048 + (tt + 1) * 128],
                       wot[:, ok * D + dc * 512: ok * D + (dc + 1) * 512],
                       ok == 0, ok == 7, waits=w,
                       inc=s_pe if (ok == 7 and dc == 1) else None)
            X = Xb[bi]
            P.op("vector", lambda e, o=X[:], a=pw[:], b=resid[:, bi * D:(bi + 1) * D]:
                 e.tensor_add(o, a, b), waits=[(s_pe, None), (s_ld, ld_res)],
                 inc=s_ve)
            add_done[bi] = s_ve.n
            last_ev[(id(pw), 0)] = (s_ve, s_ve.n)
            last_ev[(id(pw), 512)] = (s_ve, s_ve.n)
            scr, _ = ln_emit(X, stats[bi], x1o[bi], resid[:, bi * D:(bi + 1) * D],
                             store_wait=(s_st, store_done[bi]))
            scr_done[bi] = scr
            P.op("scalar", lambda e, o=x1_d[row0:row0 + 128, :], i=x1o[bi][:]:
                 e.dma_start(out=o, in_=i), waits=[(s_sc, None)],
                 inc=s_st, inc_amt=16)
            store_done[bi] = s_st.n
            x1st_cnt.append(s_st.n)
        stw.close()

    run_half(0)
    x1bf_cnt = [0] * NPASS
    for p in range(4):
        dma("gpsimd", x1bf_d[p * PT_:(p + 1) * PT_, :],
            x1_d[p * PT_:(p + 1) * PT_, :], s_gp,
            waits=[(s_st, x1st_cnt[4 * (p + 1)])])
        x1bf_cnt[p] = s_gp.n
    run_half(1)
    att_st_done = s_st.n
    stA.close()
    for p in range(4, NPASS):
        dma("gpsimd", x1bf_d[p * PT_:(p + 1) * PT_, :],
            x1_d[p * PT_:(p + 1) * PT_, :], s_gp,
            waits=[(s_st, x1st_cnt[4 * (p + 1)])])
        x1bf_cnt[p] = s_gp.n

    stm = ExitStack()
    w1sb = stm.enter_context(nc.sbuf_tensor("w1sb", [128, 8 * DFF], BF16))
    w2sb = stm.enter_context(nc.sbuf_tensor("w2sb", [128, 32 * D], BF16))
    h_sb = stm.enter_context(nc.sbuf_tensor("h_sb", [128, 32 * 512], BF16))
    y_acc = stm.enter_context(nc.sbuf_tensor("y_acc", [128, 4 * D], F32))
    x1t = [stm.enter_context(nc.sbuf_tensor("x1t_s%d" % i, [128, 8 * 512], BF16))
           for i in range(2)]
    lg_t = stm.enter_context(nc.sbuf_tensor("lg_t", [128, 32], F32))
    combine = [stm.enter_context(nc.sbuf_tensor("combine%d" % i, [128, 32], F32))
               for i in range(2)]
    combT = stm.enter_context(nc.sbuf_tensor("combT", [8, 512], BF16))
    logT = stm.enter_context(nc.sbuf_tensor("logT", [8, 512], F32))
    mres = stm.enter_context(nc.sbuf_tensor("mres", [128, D], F32))
    mout = stm.enter_context(nc.sbuf_tensor("mout", [128, D], F32))
    mstat = stm.enter_context(nc.sbuf_tensor("mstat", [128, 10], F32))
    rstat = stm.enter_context(nc.sbuf_tensor("rstat", [128, 32], F32))
    wexp_t = stm.enter_context(nc.sbuf_tensor("wexp_t", [128, 32], F32))
    wsm_t = stm.enter_context(nc.sbuf_tensor("wsm_t", [128, 32], F32))
    b2_sb = stm.enter_context(nc.sbuf_tensor("b2_sb", [NE, D], BF16))
    b1_sb = stm.enter_context(nc.sbuf_tensor("b1_sb", [128, NE * 32], F32))

    ps_h = [psA, psB]
    ps_ya = [ps_sT[0], ps_sT[1]]
    last_ev[(id(psA), 0)] = (s_st, att_st_done)
    last_ev[(id(psB), 0)] = (s_st, att_st_done)

    pass_mm1_end = [0] * (NPASS + 1)
    mm1_done = [s_pe.n]
    mm2_done = [s_pe.n]
    w1_ld = [0]
    w2_ld = [0]

    dma("gpsimd", b2_sb[:], b2_d, s_gp, waits=[(s_st, att_st_done)])
    gp_b2 = s_gp.n
    dma("sync", b1_sb[:], b1t_d, s_ld, waits=[(s_st, att_st_done)])
    dma("sync", ch_out(w1sb, DFF), ch_in(w1bf_d[0]), s_ld,
        waits=[(s_gp, gp_conv[0])])
    w1_ld[-1] = s_ld.n
    dma("sync", ch_out(w2sb, D), ch_in(w2bf_d[0]), s_ld)
    w2_ld[-1] = s_ld.n
    ld_moe_consts = s_ld.n

    sm_done = [0]

    def emit_router(p, inject=None):
        cmb = combine[p % 2]
        xt = x1t[p % 2]
        rt = ps_ya[0]
        for dk in range(8):
            w = [(s_ld, None), (s_ve, None), (s_gp, gp_proj)] if dk == 0 else []
            mm(rt[0:8, 0:512], wg_sb[:, dk * 8:(dk + 1) * 8],
               xt[:, dk * 512:(dk + 1) * 512], dk == 0, dk == 7,
               waits=w, inc=s_pe if dk == 7 else None)
        act(logT[:], rt[0:8, 0:512], AF.Identity, bias=bg_sb[:],
            waits=[(s_pe, None), (s_ld, ld_moe_consts)], inc=s_sc)

        def emit_ltrans():
            for q4 in range(4):
                P.op("tensor", lambda e, o=rt[:, 512:520],
                     i=logT[:, q4 * 128:(q4 + 1) * 128], idt=id8[:]:
                     e.transpose(o, i, idt), waits=[(s_sc, None)], inc=s_pe)
                act(lg_t[:, q4 * 8:(q4 + 1) * 8], rt[:, 512:520], AF.Copy,
                    waits=[(s_pe, None)], inc=s_sc)

        def emit_softmax():
            LG = [lg_t[:, q * 8:(q + 1) * 8] for q in range(4)]
            CM = [cmb[:, q * 8:(q + 1) * 8] for q in range(4)]
            WX = [wexp_t[:, q * 8:(q + 1) * 8] for q in range(4)]
            WS = [wsm_t[:, q * 8:(q + 1) * 8] for q in range(4)]
            STc = lambda j, q: rstat[:, 4 * j + q:4 * j + q + 1]
            dr = lambda: P.op("vector", lambda e: e.drain())
            for q in range(4):
                P.op("vector", lambda e, o=STc(0, q), i=LG[q]:
                     e.reduce_max(o, i, axis=AX.X, negate=True),
                     waits=[(s_sc, None)] if q == 0 else (),
                     inc=s_ve if q == 3 else None)
            for q in range(4):
                act(WX[q], LG[q], AF.Exp, bias=STc(0, q),
                    waits=[(s_ve, None)] if q == 0 else (),
                    inc=s_sc if q == 3 else None)
            for q in range(4):
                P.op("vector", lambda e, o=STc(1, q), i=WX[q]:
                     e.reduce_sum(o, i, axis=AX.X),
                     waits=[(s_sc, None)] if q == 0 else ())
            dr()
            for q in range(4):
                P.op("vector", lambda e, o=STc(2, q), i=STc(1, q): e.reciprocal(o, i))
            dr()
            for q in range(4):
                P.op("vector", lambda e, o=WS[q], i=WX[q], sc=STc(2, q):
                     e.tensor_scalar(o, i, sc, None, ALU.mult))
            dr()
            for q in range(4):
                P.op("vector", lambda e, o=STc(3, q), i=WS[q]:
                     e.reduce_max(o, i, axis=AX.X))
            dr()
            for q in range(4):
                P.op("vector", lambda e, o=CM[q], i=WS[q], sc=STc(3, q):
                     e.tensor_scalar(o, i, sc, None, ALU.is_lt))
            dr()
            for q in range(4):
                P.op("vector", lambda e, o=CM[q], a=WS[q], b=CM[q]:
                     e.tensor_mul(o, a, b))
            dr()
            for q in range(4):
                P.op("vector", lambda e, o=STc(4, q), i=CM[q]:
                     e.reduce_max(o, i, axis=AX.X))
            dr()
            for q in range(4):
                P.op("vector", lambda e, o=STc(5, q), a=STc(3, q), b=STc(4, q):
                     e.tensor_add(o, a, b))
            for q in range(4):
                P.op("vector", lambda e, o=CM[q], i=WS[q], sc=STc(4, q):
                     e.tensor_scalar(o, i, sc, None, ALU.is_ge))
            dr()
            for q in range(4):
                P.op("vector", lambda e, o=STc(6, q), i=STc(5, q):
                     e.reciprocal(o, i))
            for q in range(4):
                P.op("vector", lambda e, o=CM[q], a=WS[q], b=CM[q]:
                     e.tensor_mul(o, a, b))
            dr()
            for q in range(4):
                P.op("vector", lambda e, o=CM[q], i=CM[q], sc=STc(6, q):
                     e.tensor_scalar(o, i, sc, None, ALU.mult),
                     inc=s_ve if q == 3 else None)
            sm_done[0] = s_ve.n

        def emit_ctrans():
            ve_comb = sm_done[0]
            for q4 in range(4):
                P.op("tensor", lambda e, o=rt[0:8, 512:640],
                     i=cmb[:, q4 * 8:(q4 + 1) * 8], idt=id128[:]:
                     e.transpose(o, i, idt),
                     waits=[(s_ve, ve_comb), (s_sc, None)], inc=s_pe)
                act(combT[:, q4 * 128:(q4 + 1) * 128], rt[0:8, 512:640], AF.Copy,
                    waits=[(s_pe, None)], inc=s_sc)

        def emit_yinit(ve_scr_gate):
            for tm in range(4):
                for dc in range(2):
                    w = [(s_sc, None), (s_gp, gp_b2)] if dc == 0 else ()
                    mm(rt[:, dc * 512:(dc + 1) * 512],
                       combT[:, tm * 128:(tm + 1) * 128],
                       b2_sb[:, dc * 512:(dc + 1) * 512], True, True,
                       waits=w, inc=s_pe if dc == 1 else None)
                act(y_acc[:, tm * D:(tm + 1) * D], rt[:], AF.Copy,
                    waits=[(s_pe, None), (s_ve, ve_scr_gate)], inc=s_sc)

        return emit_ltrans, emit_softmax, emit_ctrans, emit_yinit

    ve_yacc_scr = [0]
    ln_prev = [None]

    def emit_pass_ln(p):
        for tm in range(4):
            row0 = p * PT_ + tm * 128
            dma("sync", mres[:], x1_d[row0:row0 + 128, :], s_ld,
                waits=[(s_st, None)])
            P.op("vector", lambda e, o=mout[:], a=y_acc[:, tm * D:(tm + 1) * D],
                 b=mres[:]:
                 e.tensor_add(o, a, b), waits=[(s_ld, None), (s_ve, None)],
                 inc=s_ve)
            scr, _ = ln_emit(mout, mstat, mres, y_acc[:, tm * D:(tm + 1) * D])
            ve_yacc_scr[0] = scr
            P.op("scalar", lambda e, o=out_d[row0:row0 + 128, :], i=mres[:]:
                 e.dma_start(out=o, in_=i), waits=[(s_sc, None)],
                 inc=s_st, inc_amt=16)

    for p in range(NPASS):
        xt = x1t[p % 2]
        for k in range(8):
            w = []
            if k == 0:
                w = [(s_gp, x1bf_cnt[p]), (s_st, att_st_done)]
                if p >= 2:
                    w.append((s_pe, pass_mm1_end[p - 2]))
            dma("sync", xt[:, k * 512:(k + 1) * 512],
                x1bf_d[p * PT_:(p + 1) * PT_, k * 128:(k + 1) * 128],
                s_ld, waits=w, transpose=True)

        e_lt, e_sm, e_ct, e_yi = emit_router(p)
        if p == 0:
            e_lt()
            e_sm()
            e_ct()
            e_yi(ve_yacc_scr[0])
        else:
            e_sm_pend = (e_lt, e_sm, e_ct, e_yi)

        for e_ in range(NE):
            if not (p == 0 and e_ == 0):
                dma("sync", ch_out(w1sb, DFF), ch_in(w1bf_d[e_]),
                    s_ld, waits=[(s_pe, mm1_done[-1])])
                w1_ld.append(s_ld.n)
                dma("sync", ch_out(w2sb, D), ch_in(w2bf_d[e_]),
                    s_ld, waits=[(s_pe, mm2_done[-1])])
                w2_ld.append(s_ld.n)
            for fm in range(32):
                buf = ps_h[fm % 2]
                key = (id(buf), 0)
                for dk in range(8):
                    w = []
                    if dk == 0:
                        w = [(s_ld, w1_ld[-1])] + buf_wait(key)
                    mm(buf[:], w1sb[:, dk * DFF + fm * 128: dk * DFF + (fm + 1) * 128],
                       xt[:, dk * 512:(dk + 1) * 512], dk == 0, dk == 7,
                       waits=w, inc=s_pe if dk == 7 else None)
                act(h_sb[:, fm * 512:(fm + 1) * 512], buf[:], AF.Relu,
                    bias=b1_sb[:, e_ * 32 + fm: e_ * 32 + fm + 1],
                    waits=[(s_pe, None), (s_ld, ld_moe_consts)], inc=s_h2)
                last_ev[key] = (s_h2, s_h2.n)
                if p > 0 and e_ == 0:
                    if fm == 3:
                        e_sm_pend[0]()
                    elif fm == 5:
                        e_sm_pend[1]()
                        emit_pass_ln(p - 1)
                    elif fm == 24:
                        e_sm_pend[2]()
                    elif fm == 29:
                        e_sm_pend[3](ve_yacc_scr[0])
            mm1_done.append(s_pe.n)
            if e_ == NE - 1:
                pass_mm1_end[p] = s_pe.n
            sc_h = s_h2.n
            for half2 in range(2):
                for fk in range(32):
                    for tmi in range(2):
                        tm = half2 * 2 + tmi
                        for dc in range(2):
                            w = []
                            if fk == 0 and tmi == 0 and dc == 0:
                                w = [(s_ld, w2_ld[-1]), (s_h2, sc_h), (s_ve, None)]
                            mm(ps_ya[tmi][:, dc * 512:(dc + 1) * 512],
                               h_sb[:, fk * 512 + tm * 128: fk * 512 + (tm + 1) * 128],
                               w2sb[:, fk * D + dc * 512: fk * D + (dc + 1) * 512],
                               fk == 0, fk == 31, waits=w,
                               inc=s_pe if (fk == 31 and tmi == 1 and dc == 1) else None)
                for tmi in range(2):
                    tm = half2 * 2 + tmi
                    P.op("vector", lambda e, o=y_acc[:, tm * D:(tm + 1) * D],
                         a=ps_ya[tmi][:],
                         s=combine[p % 2][:, tm * 8 + e_: tm * 8 + e_ + 1]:
                         e.scalar_tensor_tensor(o, a, s, o, ALU.mult, ALU.add),
                         waits=[(s_pe, None), (s_sc, None)] if tmi == 0 else [],
                         inc=s_ve)
            mm2_done.append(s_pe.n)

    emit_pass_ln(NPASS - 1)

    P.op("sync", lambda e: e.nop(), waits=[(s_st, s_st.n)])

    with nc.Block() as block:
        @block.sync
        def _(eng):
            for fn in P.ops["sync"]:
                fn(eng)

        @block.gpsimd
        def _(eng):
            for fn in P.ops["gpsimd"]:
                fn(eng)

        @block.vector
        def _(eng):
            for fn in P.ops["vector"]:
                fn(eng)

        @block.scalar
        def _(eng):
            for fn in P.ops["scalar"]:
                fn(eng)

        @block.tensor
        def _(eng):
            for fn in P.ops["tensor"]:
                fn(eng)

    stm.close()
    st.close()
    return nc


def _consts_np():
    inv_freq = 10000.0 ** (-np.arange(0, DHR, 2, dtype=np.float64) / DHR)
    t = np.arange(S, dtype=np.float64)
    ang = inv_freq[:, None] * t[None, :]
    C = np.repeat(np.cos(ang), 2, axis=0).astype(np.float32)
    sin = np.sin(ang)
    Spm = np.empty((DHR, S), np.float64)
    Spm[0::2] = -sin
    Spm[1::2] = sin
    masks = np.zeros((128, 4 * 512), np.float32)
    jj = np.arange(128)[:, None]
    ii = np.arange(512)[None, :]
    for r in range(4):
        masks[:, r * 512:(r + 1) * 512] = (jj + 128 * r <= ii).astype(np.float32)
    consts = np.zeros((128, 4), np.float32)
    consts[:, 0] = 1.0 / np.sqrt(96.0)
    consts[:, 1] = 1e-5
    consts[:, 2] = -0.5
    consts[:, 3] = 1.0 / D
    bf = ml_dtypes.bfloat16
    CS = np.concatenate([C, Spm.astype(np.float32)], axis=0)
    return (CS.astype(bf), masks.astype(bf), consts)


def _cat_swap(W, blk):
    n = W.shape[0] // blk
    out = np.empty((n * 2 * blk, W.shape[1]), W.dtype)
    perm = np.arange(blk) ^ 1
    for i in range(n):
        out[i * 2 * blk: i * 2 * blk + blk] = W[i * blk:(i + 1) * blk]
        out[i * 2 * blk + blk: (i + 1) * 2 * blk] = W[i * blk:(i + 1) * blk][perm]
    return out


def kernel(L0_attn_x, L0_attn_W_DQ, L0_attn_W_UQ, L0_attn_W_QR, L0_attn_W_DKV,
           L0_attn_W_UK, L0_attn_W_UV, L0_attn_W_KR, L0_attn_W_O,
           L0_ffn_W1, L0_ffn_b1, L0_ffn_W2, L0_ffn_b2, L0_ffn_Wg, L0_ffn_bg):
    nc = bass.Bass("TRN2", target_bir_lowering=False)
    build(nc)

    CS, masks, consts = _consts_np()
    asc = np.ascontiguousarray
    wqrc = _cat_swap(L0_attn_W_QR, DHR)
    wkrc = _cat_swap(L0_attn_W_KR, DHR)
    common = {
        "wdqt": asc(L0_attn_W_DQ.T), "wdkvt": asc(L0_attn_W_DKV.T),
        "wkrct": asc(wkrc.T), "wuqt": asc(L0_attn_W_UQ.T),
        "wqrct": asc(wqrc.T), "wukt": asc(L0_attn_W_UK.T),
        "wuvt": asc(L0_attn_W_UV.T), "wot": asc(L0_attn_W_O.T),
        "w1": asc(L0_ffn_W1), "w2": asc(L0_ffn_W2),
        "b1t": asc(L0_ffn_b1.reshape(NE, 32, 128).transpose(2, 0, 1).reshape(128, NE * 32)),
        "b2": asc(L0_ffn_b2),
        "wgt": asc(L0_ffn_Wg.reshape(8, 128, 8).transpose(1, 0, 2).reshape(128, 64)),
        "bg": asc(L0_ffn_bg.reshape(NE, 1)),
        "ropecs": CS, "masks": masks,
        "id8": np.eye(8, dtype=np.float32), "id128": np.eye(128, dtype=np.float32),
        "ones64": np.ones((1, 64), np.float32), "consts": consts,
    }
    in_maps = []
    for c in range(8):
        m = dict(common)
        xc = asc(np.asarray(L0_attn_x[c], dtype=np.float32))
        m["x"] = xc
        m["xT"] = asc(xc.T)
        in_maps.append(m)
    res = run_bass_kernel_spmd(nc, in_maps, core_ids=list(range(8)))
    return np.stack([np.asarray(r["out"], dtype=np.float32) for r in res.results])


# version 10
# speedup vs baseline: 1.3432x; 1.0185x over previous
import numpy as np
import ml_dtypes
import concourse.bass as bass
import concourse.mybir as mybir
from concourse.bass_utils import run_bass_kernel_spmd
from contextlib import ExitStack

F32 = mybir.dt.float32
BF16 = mybir.dt.bfloat16
AF = mybir.ActivationFunctionType
ALU = mybir.AluOpType
AX = mybir.AxisListType

S, D, H, DH, DHR, LAT = 4096, 1024, 16, 64, 32, 512
DFF, NE = 4096, 8
TAU = float(np.sqrt(96.0))
PLO = float(np.exp(-80.0 / TAU))
PHI = float(np.exp(80.0 / TAU))
NPASS = 8
PT_ = S // NPASS
VSL = 260


class Sem:
    def __init__(self, h):
        self.h = h
        self.n = 0


class Prog:
    def __init__(self):
        self.ops = {k: [] for k in ("sync", "gpsimd", "vector", "scalar", "tensor")}

    def op(self, eng, f, waits=(), inc=None, inc_amt=1):
        wl = [(s.h, s.n if tgt is None else tgt) for (s, tgt) in waits]
        if inc is not None:
            inc.n += inc_amt
            pair = (inc.h, inc_amt)
        else:
            pair = None

        def fn(e, wl=wl, f=f, pair=pair):
            for h, v in wl:
                e.wait_ge(h, v)
            inst = f(e)
            if pair is not None:
                inst.then_inc(*pair)
        self.ops[eng].append(fn)


def build(nc):
    x_d = nc.dram_tensor("x", [S, D], F32, kind="ExternalInput").ap()
    xT_d = nc.dram_tensor("xT", [D, S], F32, kind="ExternalInput").ap()
    wdqt_d = nc.dram_tensor("wdqt", [D, LAT], F32, kind="ExternalInput").ap()
    wdkvt_d = nc.dram_tensor("wdkvt", [D, LAT], F32, kind="ExternalInput").ap()
    wkrct_d = nc.dram_tensor("wkrct", [D, 64], F32, kind="ExternalInput").ap()
    wuqt_d = nc.dram_tensor("wuqt", [LAT, D], F32, kind="ExternalInput").ap()
    wqrct_d = nc.dram_tensor("wqrct", [LAT, D], F32, kind="ExternalInput").ap()
    wukt_d = nc.dram_tensor("wukt", [LAT, D], F32, kind="ExternalInput").ap()
    wuvt_d = nc.dram_tensor("wuvt", [LAT, D], F32, kind="ExternalInput").ap()
    wot_d = nc.dram_tensor("wot", [D, D], F32, kind="ExternalInput").ap()
    w1_d = nc.dram_tensor("w1", [NE, D, DFF], F32, kind="ExternalInput").ap()
    w2_d = nc.dram_tensor("w2", [NE, DFF, D], F32, kind="ExternalInput").ap()
    b1t_d = nc.dram_tensor("b1t", [128, NE * 32], F32, kind="ExternalInput").ap()
    b2_d = nc.dram_tensor("b2", [NE, D], F32, kind="ExternalInput").ap()
    wgt_d = nc.dram_tensor("wgt", [128, 64], F32, kind="ExternalInput").ap()
    bg_d = nc.dram_tensor("bg", [NE, 1], F32, kind="ExternalInput").ap()
    ropecs_d = nc.dram_tensor("ropecs", [64, S], BF16, kind="ExternalInput").ap()
    masks_d = nc.dram_tensor("masks", [128, 4 * 512], BF16, kind="ExternalInput").ap()
    id8_d = nc.dram_tensor("id8", [8, 8], F32, kind="ExternalInput").ap()
    id128_d = nc.dram_tensor("id128", [128, 128], F32, kind="ExternalInput").ap()
    ones64_d = nc.dram_tensor("ones64", [1, 64], F32, kind="ExternalInput").ap()
    consts_d = nc.dram_tensor("consts", [128, 4], F32, kind="ExternalInput").ap()

    out_d = nc.dram_tensor("out", [S, D], F32, kind="ExternalOutput").ap()

    w1bf_d = nc.dram_tensor("w1bf", [NE, D, DFF], BF16, kind="Internal").ap()
    w2bf_d = nc.dram_tensor("w2bf", [NE, DFF, D], BF16, kind="Internal").ap()
    x1_d = nc.dram_tensor("x1f", [S, D], F32, kind="Internal").ap()
    x1bf_d = nc.dram_tensor("x1bf", [S, D], BF16, kind="Internal").ap()

    P = Prog()
    st = ExitStack()
    sb = lambda n, s_, d_: st.enter_context(nc.sbuf_tensor(n, s_, d_))
    psal = lambda n, s_: st.enter_context(nc.psum_tensor(n, s_, F32))
    sem = lambda n: Sem(st.enter_context(nc.semaphore(n)))
    s_ld = sem("s_ld")
    s_gp = sem("s_gp")
    s_pe = sem("s_pe")
    s_sc = sem("s_sc")
    s_ve = sem("s_ve")
    s_st = sem("s_st")
    s_h2 = sem("s_h2")

    consts = sb("consts_s", [128, 4], F32)
    id8 = sb("id8_s", [8, 8], F32)
    id128 = sb("id128_s", [128, 128], F32)
    ones64 = sb("ones64_s", [1, 64], F32)
    wg_sb = sb("wg_sb", [128, 64], BF16)
    bg_sb = sb("bg_sb", [NE, 1], F32)

    def dma(eng, out, in_, semc, waits=(), **kw):
        P.op(eng, lambda e, o=out, i=in_, kw=kw: e.dma_start(out=o, in_=i, **kw),
             waits=waits, inc=semc, inc_amt=16)

    def ch_in(dram_ap):
        return dram_ap.rearrange("(k p) t -> p k t", p=128)

    def ch_out(sb_t, inner):
        return sb_t[:].rearrange("p (k t) -> p k t", t=inner)

    dma("sync", consts[:], consts_d, s_ld)
    dma("sync", id8[:], id8_d, s_ld)
    dma("sync", id128[:], id128_d, s_ld)
    dma("sync", ones64[:], ones64_d, s_ld)
    dma("sync", bg_sb[:], bg_d, s_ld)
    SC_SQRT = consts[:, 0:1]
    SC_EPS = consts[:, 1:2]
    SC_NHALF = consts[:, 2:3]

    stA = ExitStack()
    sba = lambda n, sh, d_: stA.enter_context(nc.sbuf_tensor(n, sh, d_))
    cqt = sba("cqt", [128, 4 * S], BF16)
    ckvt = sba("ckvt", [128, 4 * S], BF16)
    krt = sba("krt", [DHR, S], BF16)
    tabCS = sba("tabCS", [64, S], BF16)
    masks = sba("masks_s", [128, 4 * 512], BF16)
    qrs = [sba("qrs%d" % i, [64, 512], BF16) for i in range(2)]
    rt1 = sba("rt1", [DHR, 512], BF16)
    rt2 = sba("rt2", [DHR, 512], BF16)

    dma("sync", tabCS[:], ropecs_d, s_ld)
    dma("sync", masks[:], masks_d, s_ld)
    dma("gpsimd", wg_sb[:], wgt_d, s_gp)

    psA = psal("psA", [128, 512])
    psB = psal("psB", [128, 512])
    ps_sT = [psal("ps_sT0", [128, 1024]), psal("ps_sT1", [128, 1024])]
    ps_oc = [psal("ps_o0", [96, 512]), psal("ps_o1", [96, 512])]
    ps_zb = psB

    ld_base = s_ld.n

    last_ev = {}

    def mm(out, lhsT, rhs, start, stop, waits=(), inc=None):
        P.op("tensor",
             lambda e, o=out, a=lhsT, b=rhs, s1=start, s2=stop:
             e.matmul(o, a, b, start=s1, stop=s2), waits=waits, inc=inc)

    def act(out, in_, func, bias=0.0, scale=1.0, waits=(), inc=None):
        P.op("scalar",
             lambda e, o=out, i=in_, f=func, b=bias, s=scale:
             e.activation(out=o, in_=i, func=f, bias=b, scale=s),
             waits=waits, inc=inc)

    pp4 = [(ps_sT[0], 0), (ps_sT[0], 512), (ps_sT[1], 0), (ps_sT[1], 512)]
    pi = [0]

    def nbuf():
        b = pp4[pi[0] % 4]
        pi[0] += 1
        return b

    def buf_wait(key):
        v = last_ev.get(key)
        return [v] if v is not None else []

    def evict(dst, key, src_ap, func=AF.Copy, bias=0.0, extra_waits=()):
        act(dst, src_ap, func, bias=bias,
            waits=[(s_pe, None)] + list(extra_waits), inc=s_sc)
        last_ev[key] = (s_sc, s_sc.n)

    stp = ExitStack()
    xt_sb = stp.enter_context(nc.sbuf_tensor("xt_sb", [128, 8 * S], BF16))
    wdqt = stp.enter_context(nc.sbuf_tensor("wdqt_s", [128, 8 * LAT], BF16))
    wdkvt = stp.enter_context(nc.sbuf_tensor("wdkvt_s", [128, 8 * LAT], BF16))
    wkrct = stp.enter_context(nc.sbuf_tensor("wkrct_s", [128, 8 * 64], BF16))

    dma("gpsimd", ch_out(xt_sb, S), ch_in(xT_d), s_gp)
    dma("gpsimd", ch_out(wdqt, LAT), ch_in(wdqt_d), s_gp)
    dma("gpsimd", ch_out(wdkvt, LAT), ch_in(wdkvt_d), s_gp)
    dma("gpsimd", ch_out(wkrct, 64), ch_in(wkrct_d), s_gp)
    gp_proj = s_gp.n

    w0 = [(s_gp, gp_proj), (s_ld, ld_base)]

    def proj128(dst, dst_col0, wt, wt_cols, row0, M):
        for n in range(S // 512):
            t, off = nbuf()
            key = (id(t), off)
            w = w0 + buf_wait(key)
            for dk in range(8):
                mm(t[0:M, off:off + 512],
                   wt[:, dk * wt_cols + row0: dk * wt_cols + row0 + M],
                   xt_sb[:, dk * S + n * 512: dk * S + (n + 1) * 512],
                   dk == 0, dk == 7, waits=w if dk == 0 else (),
                   inc=s_pe if dk == 7 else None)
            evict(dst[:, dst_col0 + n * 512: dst_col0 + (n + 1) * 512],
                  key, t[0:M, off:off + 512])

    for lm in range(4):
        proj128(cqt, lm * S, wdqt, LAT, lm * 128, 128)
    for lm in range(4):
        proj128(ckvt, lm * S, wdkvt, LAT, lm * 128, 128)

    def rope_emit(dst, stage, tab_c):
        P.op("vector", lambda e, o=rt1[:], a=stage[0:DHR, :],
             b=tabCS[0:DHR, tab_c:tab_c + 512]:
             e.tensor_mul(o, a, b), waits=[(s_sc, None), (s_ld, ld_base)])
        P.op("vector", lambda e, o=rt2[:], a=stage[DHR:64, :],
             b=tabCS[DHR:64, tab_c:tab_c + 512]:
             e.tensor_mul(o, a, b))
        P.op("vector", lambda e, o=dst, a=rt1[:], b=rt2[:]:
             e.tensor_add(o, a, b), inc=s_ve)

    for n in range(S // 512):
        t, off = nbuf()
        key = (id(t), off)
        w = w0 + buf_wait(key) + [(s_ve, None)]
        for dk in range(8):
            mm(t[0:64, off:off + 512], wkrct[:, dk * 64:(dk + 1) * 64],
               xt_sb[:, dk * S + n * 512: dk * S + (n + 1) * 512],
               dk == 0, dk == 7, waits=w if dk == 0 else (),
               inc=s_pe if dk == 7 else None)
        evict(qrs[n % 2][:], key, t[0:64, off:off + 512],
              extra_waits=[(s_ve, None)])
        rope_emit(krt[:, n * 512:(n + 1) * 512], qrs[n % 2], n * 512)

    pe_projdone = s_pe.n
    stp.close()

    ot_sb = sba("ot_sb", [128, 8 * 2048], BF16)

    x1st_cnt = [0]
    conv_emitted = [False]
    gp_conv = [0]

    def ln_emit(X, stt, out_f32, scratch, store_wait=None):
        sum1 = stt[:, 0:1]; ss = stt[:, 1:2]; mean = stt[:, 2:3]
        mvar = stt[:, 3:4]; m2 = stt[:, 4:5]; var = stt[:, 5:6]
        lnv = stt[:, 6:7]; rstd = stt[:, 7:8]; nb = stt[:, 8:9]
        P.op("vector", lambda e, o=scratch, a=X[:], b=ss:
             e.scalar_tensor_tensor(o, a, 1.0, a, ALU.mult, ALU.mult, accum_out=b))
        P.op("vector", lambda e, o=sum1, i=X[:]: e.reduce_sum(o, i, axis=AX.X),
             inc=s_ve)
        ve_scr = s_ve.n
        P.op("vector", lambda e: e.drain())
        P.op("vector", lambda e, o=mean, i=sum1:
             e.tensor_scalar(o, i, 1.0 / D, None, ALU.mult))
        P.op("vector", lambda e, o=mvar, i=ss:
             e.tensor_scalar(o, i, 1.0 / D, None, ALU.mult))
        P.op("vector", lambda e: e.drain())
        P.op("vector", lambda e, o=m2, a=mean, b=mean: e.tensor_mul(o, a, b))
        P.op("vector", lambda e: e.drain())
        P.op("vector", lambda e, o=var, a=mvar, b=m2: e.tensor_sub(o, a, b),
             inc=s_ve)
        act(lnv, var, AF.Ln, bias=SC_EPS, waits=[(s_ve, None)])
        act(rstd, lnv, AF.Exp, scale=SC_NHALF, inc=s_sc)
        P.op("vector", lambda e, o=nb, i=rstd, m=mean:
             e.tensor_scalar(o, i, m, -1.0, ALU.mult, ALU.mult),
             waits=[(s_sc, None)])
        P.op("vector", lambda e: e.drain(), inc=s_ve)
        aw = [(s_ve, None)]
        if store_wait is not None:
            aw.append(store_wait)
        act(out_f32[:], X[:], AF.Identity, bias=nb, scale=rstd, waits=aw,
            inc=s_sc)
        return ve_scr, s_sc.n

    def run_half(half):
        kj_end = 2048 * (half + 1)
        nkt = kj_end // 128
        qi0 = half * 2048

        sth = ExitStack()
        wuqt = sth.enter_context(nc.sbuf_tensor("wuqt_s%d" % half, [128, 4 * D], BF16))
        wqrct = sth.enter_context(nc.sbuf_tensor("wqrct_s%d" % half, [128, 4 * D], BF16))
        wukt = sth.enter_context(nc.sbuf_tensor("wukt_s%d" % half, [128, 4 * D], BF16))
        wuvt = sth.enter_context(nc.sbuf_tensor("wuvt_s%d" % half, [128, 4 * D], BF16))
        khT = sth.enter_context(nc.sbuf_tensor("khT%d" % half, [96, S], BF16))
        qhT = sth.enter_context(nc.sbuf_tensor("qhT%d" % half, [96, 2048], BF16))
        v4 = sth.enter_context(nc.sbuf_tensor("v4_%d" % half, [128, 32 * VSL + 36], BF16))
        Ebuf = [sth.enter_context(nc.sbuf_tensor("E%d_%d" % (half, i), [128, 1024], BF16))
                for i in range(3)]
        Pbuf = [sth.enter_context(nc.sbuf_tensor("P%d_%d" % (half, i), [128, 1024], BF16))
                for i in range(3)]
        o_f32 = sth.enter_context(nc.sbuf_tensor("of%d" % half, [64, 512], F32))
        zrow = sth.enter_context(nc.sbuf_tensor("zr%d" % half, [1, 512], F32))
        rz = sth.enter_context(nc.sbuf_tensor("rz%d" % half, [1, 512], F32))

        if half == 0:
            wq_wait = [(s_gp, gp_proj), (s_pe, pe_projdone)]
        else:
            wq_wait = [(s_st, None)]
        dma("gpsimd", ch_out(wuqt, D), ch_in(wuqt_d), s_gp, waits=wq_wait)
        dma("gpsimd", ch_out(wqrct, D), ch_in(wqrct_d), s_gp)
        dma("gpsimd", ch_out(wukt, D), ch_in(wukt_d), s_gp)
        dma("gpsimd", ch_out(wuvt, D), ch_in(wuvt_d), s_gp)
        gp_wu = s_gp.n

        if not conv_emitted[0]:
            for e in range(NE):
                dma("gpsimd", w1bf_d[e], w1_d[e], s_gp)
                dma("gpsimd", w2bf_d[e], w2_d[e], s_gp)
            gp_conv[0] = s_gp.n
            conv_emitted[0] = True

        vw = [(s_pe, pe_projdone)] if half == 0 else [(s_st, None)]
        for hh in range(4):
            P.op("vector", lambda e, o=v4[:, hh * 65 + 64: 32 * VSL: VSL]:
                 e.memset(o, 1.0), waits=vw if hh == 0 else ())
        P.op("vector", lambda e, o=v4[:, 32 * VSL: 32 * VSL + 36]:
             e.memset(o, 0.0), inc=s_ve)
        ve_vinit = s_ve.n
        P.op("vector", lambda e, o=khT[64:96, 0:kj_end], i=krt[:, 0:kj_end]:
             e.tensor_copy(out=o, in_=i),
             waits=(vw if half else []) + [(s_ve, None)], inc=s_ve)

        o_free = {}
        pend_z = [None]
        for hg in range(4):
            for tt in range(nkt):
                t, off = nbuf()
                key = (id(t), off)
                w = buf_wait(key) + [(s_gp, gp_wu), (s_ve, ve_vinit)]
                for lk in range(4):
                    mm(t[:, off:off + 256],
                       ckvt[:, lk * S + tt * 128: lk * S + (tt + 1) * 128],
                       wuvt[:, lk * D + hg * 256: lk * D + hg * 256 + 256],
                       lk == 0, lk == 3, waits=w if lk == 0 else (),
                       inc=s_pe if lk == 3 else None)
                act(v4[:, tt * VSL: tt * VSL + VSL]
                    .rearrange("p (b c) -> p b c", c=65)[:, :, 0:64],
                    t[:, off:off + 256].rearrange("p (b c) -> p b c", c=64),
                    AF.Copy, waits=[(s_pe, None)], inc=s_sc)
                last_ev[key] = (s_sc, s_sc.n)

            for hh in range(4):
                h = hg * 4 + hh
                for n in range(4):
                    t, off = nbuf()
                    key = (id(t), off)
                    w = buf_wait(key)
                    for lk in range(4):
                        mm(t[0:64, off:off + 512],
                           wuqt[:, lk * D + h * 64: lk * D + h * 64 + 64],
                           cqt[:, lk * S + qi0 + n * 512: lk * S + qi0 + (n + 1) * 512],
                           lk == 0, lk == 3, waits=w if lk == 0 else (),
                           inc=s_pe if lk == 3 else None)
                    evict(qhT[0:64, n * 512:(n + 1) * 512], key,
                          t[0:64, off:off + 512])
                for n in range(4):
                    t, off = nbuf()
                    key = (id(t), off)
                    w = buf_wait(key)
                    for lk in range(4):
                        mm(t[0:64, off:off + 512],
                           wqrct[:, lk * D + h * 64: lk * D + h * 64 + 64],
                           cqt[:, lk * S + qi0 + n * 512: lk * S + qi0 + (n + 1) * 512],
                           lk == 0, lk == 3, waits=w if lk == 0 else (),
                           inc=s_pe if lk == 3 else None)
                    evict(qrs[n % 2][:], key, t[0:64, off:off + 512],
                          extra_waits=[(s_ve, None)])
                    rope_emit(qhT[64:96, n * 512:(n + 1) * 512], qrs[n % 2],
                              qi0 + n * 512)
                for n in range(kj_end // 512):
                    t, off = nbuf()
                    key = (id(t), off)
                    w = buf_wait(key)
                    for lk in range(4):
                        mm(t[0:64, off:off + 512],
                           wukt[:, lk * D + h * 64: lk * D + h * 64 + 64],
                           ckvt[:, lk * S + n * 512: lk * S + (n + 1) * 512],
                           lk == 0, lk == 3, waits=w if lk == 0 else (),
                           inc=s_pe if lk == 3 else None)
                    evict(khT[0:64, n * 512:(n + 1) * 512], key,
                          t[0:64, off:off + 512])
                sc_head = s_sc.n
                ve_head = s_ve.n

                for cl in range(4):
                    gc = half * 4 + cl
                    ktiles = 4 * gc + 4
                    qsl = slice(cl * 512, (cl + 1) * 512)
                    po = ps_oc[gc % 2]
                    exp_done = {}
                    pt_done = {}
                    pv_done = {}

                    def emit_pv(g, ktiles=ktiles, po=po, gc=gc):
                        ei = g % 3
                        for sub in range(2):
                            j = 2 * g + sub
                            w = [(s_ve, pt_done[g])] if sub == 0 else []
                            if sub == 0 and g == 0:
                                w.append(o_free.get(gc % 2, (s_ve, 0)))
                                w.append((s_sc, None))
                            mm(po[:], v4[:, j * VSL + hh * 65: j * VSL + hh * 65 + 96],
                               Pbuf[ei][:, sub * 512:(sub + 1) * 512],
                               j == 0, j == ktiles - 1,
                               waits=w, inc=s_pe if j == ktiles - 1 else None)
                        pv_done[g] = s_pe.n

                    pend = []
                    for g in range(ktiles // 2):
                        bi = g % 2
                        ei = g % 3
                        stk = ps_sT[bi]
                        for sub in range(2):
                            j = 2 * g + sub
                            w = []
                            if sub == 0:
                                w = [(s_sc, sc_head), (s_ve, ve_head)]
                                if g >= 2:
                                    w.append((s_sc, exp_done[g - 2]))
                                else:
                                    w += buf_wait((id(stk), 0)) + buf_wait((id(stk), 512))
                            mm(stk[:, sub * 512:(sub + 1) * 512],
                               khT[:, j * 128:(j + 1) * 128], qhT[:, qsl],
                               True, True, waits=w,
                               inc=s_pe if sub == 1 else None)
                        if g == 1 and pend_z[0] is not None:
                            pend_z[0]()
                            pend_z[0] = None
                        we = [(s_pe, None), (s_ld, ld_base)]
                        if g >= 3:
                            we.append((s_ve, pt_done[g - 3]))
                        act(Ebuf[ei][:], stk[:], AF.Exp, scale=SC_SQRT,
                            waits=we, inc=s_sc)
                        exp_done[g] = s_sc.n
                        last_ev[(id(stk), 0)] = (s_sc, s_sc.n)
                        last_ev[(id(stk), 512)] = (s_sc, s_sc.n)
                        wv = [(s_sc, None)]
                        if g >= 3:
                            wv.append((s_pe, pv_done[g - 3]))
                        vops = [lambda e, o=Pbuf[ei][:], i=Ebuf[ei][:]:
                                e.tensor_scalar(o, i, PLO, PHI, ALU.max, ALU.min)]
                        for sub in range(2):
                            j = 2 * g + sub
                            r = j - 4 * gc
                            if r >= 0:
                                vops.append(
                                    lambda e, o=Pbuf[ei][:, sub * 512:(sub + 1) * 512],
                                    m=masks[:, r * 512:(r + 1) * 512]:
                                    e.tensor_mul(o, o, m))
                        for k, vo in enumerate(vops):
                            P.op("vector", vo, waits=wv if k == 0 else (),
                                 inc=s_ve if k == len(vops) - 1 else None)
                        pt_done[g] = s_ve.n
                        pend.append(g)
                        if len(pend) > 2:
                            emit_pv(pend.pop(0))
                    while pend:
                        emit_pv(pend.pop(0))

                    def z_chain(po=po, h=h, cl=cl, gc=gc):
                        P.op("vector", lambda e, o=zrow[:], i=po[64:65, :]:
                             e.tensor_copy(out=o, in_=i), waits=[(s_pe, None)])
                        P.op("vector", lambda e: e.drain())
                        P.op("vector", lambda e, o=rz[:], i=zrow[:]:
                             e.reciprocal(o, i), inc=s_ve)
                        mm(ps_zb[0:64, :], ones64[:], rz[:], True, True,
                           waits=[(s_ve, None), (s_ld, ld_base)], inc=s_pe)
                        act(o_f32[:], po[0:64, :], AF.Copy, waits=[(s_pe, None)],
                            inc=s_sc)
                        P.op("vector", lambda e,
                             o=ot_sb[64 * (h % 2):64 * (h % 2) + 64,
                                     (h // 2) * 2048 + cl * 512:(h // 2) * 2048 + (cl + 1) * 512],
                             a=o_f32[:], b=ps_zb[0:64, :]:
                             e.tensor_mul(o, a, b),
                             waits=[(s_sc, None), (s_pe, None)], inc=s_ve)
                        o_free[gc % 2] = (s_ve, s_ve.n)

                    pend_z[0] = z_chain

        if pend_z[0] is not None:
            pend_z[0]()
            pend_z[0] = None
        pe_heads = s_pe.n
        ve_heads = s_ve.n
        sth.close()

        stw = ExitStack()
        wot = stw.enter_context(nc.sbuf_tensor("wot_s%d" % half, [128, 8 * D], BF16))
        resid = stw.enter_context(nc.sbuf_tensor("res%d" % half, [128, 2 * D], F32))
        Xb = [stw.enter_context(nc.sbuf_tensor("X%d_%d" % (half, i), [128, D], F32))
              for i in range(2)]
        x1o = [stw.enter_context(nc.sbuf_tensor("x1o%d_%d" % (half, i), [128, D], F32))
               for i in range(2)]
        stats = [stw.enter_context(nc.sbuf_tensor("stt%d_%d" % (half, i), [128, 10], F32))
                 for i in range(2)]

        dma("gpsimd", ch_out(wot, D), ch_in(wot_d), s_gp, waits=[(s_ve, ve_heads)])
        gp_wot = s_gp.n

        add_done = [0, 0]
        scr_done = [0, 0]
        store_done = [0, 0]
        for tt in range(16):
            bi = tt % 2
            row0 = half * 2048 + tt * 128
            dma("sync", resid[:, bi * D:(bi + 1) * D], x_d[row0:row0 + 128, :],
                s_ld,
                waits=[(s_ve, scr_done[bi])] if tt >= 2 else [(s_ve, ve_heads)])
            ld_res = s_ld.n
            pw = ps_sT[bi]
            for ok in range(8):
                for dc in range(2):
                    w = []
                    if ok == 0 and dc == 0:
                        w = [(s_gp, gp_wot), (s_pe, pe_heads), (s_ve, ve_heads)]
                        if tt >= 2:
                            w.append((s_ve, add_done[bi]))
                        else:
                            w += buf_wait((id(pw), 0)) + buf_wait((id(pw), 512))
                    mm(pw[:, dc * 512:(dc + 1) * 512],
                       ot_sb[:, ok * 2048 + tt * 128: ok * 2048 + (tt + 1) * 128],
                       wot[:, ok * D + dc * 512: ok * D + (dc + 1) * 512],
                       ok == 0, ok == 7, waits=w,
                       inc=s_pe if (ok == 7 and dc == 1) else None)
            X = Xb[bi]
            P.op("vector", lambda e, o=X[:], a=pw[:], b=resid[:, bi * D:(bi + 1) * D]:
                 e.tensor_add(o, a, b), waits=[(s_pe, None), (s_ld, ld_res)],
                 inc=s_ve)
            add_done[bi] = s_ve.n
            last_ev[(id(pw), 0)] = (s_ve, s_ve.n)
            last_ev[(id(pw), 512)] = (s_ve, s_ve.n)
            scr, _ = ln_emit(X, stats[bi], x1o[bi], resid[:, bi * D:(bi + 1) * D],
                             store_wait=(s_st, store_done[bi]))
            scr_done[bi] = scr
            P.op("scalar", lambda e, o=x1_d[row0:row0 + 128, :], i=x1o[bi][:]:
                 e.dma_start(out=o, in_=i), waits=[(s_sc, None)],
                 inc=s_st, inc_amt=16)
            store_done[bi] = s_st.n
            x1st_cnt.append(s_st.n)
        stw.close()

    run_half(0)
    x1bf_cnt = [0] * NPASS
    for p in range(4):
        dma("gpsimd", x1bf_d[p * PT_:(p + 1) * PT_, :],
            x1_d[p * PT_:(p + 1) * PT_, :], s_gp,
            waits=[(s_st, x1st_cnt[4 * (p + 1)])])
        x1bf_cnt[p] = s_gp.n
    run_half(1)
    att_st_done = s_st.n
    stA.close()
    for p in range(4, NPASS):
        dma("gpsimd", x1bf_d[p * PT_:(p + 1) * PT_, :],
            x1_d[p * PT_:(p + 1) * PT_, :], s_gp,
            waits=[(s_st, x1st_cnt[4 * (p + 1)])])
        x1bf_cnt[p] = s_gp.n

    stm = ExitStack()
    w1sb = stm.enter_context(nc.sbuf_tensor("w1sb", [128, 8 * DFF], BF16))
    w2sb = stm.enter_context(nc.sbuf_tensor("w2sb", [128, 32 * D], BF16))
    h_sb = stm.enter_context(nc.sbuf_tensor("h_sb", [128, 32 * 512], BF16))
    y_acc = stm.enter_context(nc.sbuf_tensor("y_acc", [128, 4 * D], F32))
    x1t = [stm.enter_context(nc.sbuf_tensor("x1t_s%d" % i, [128, 8 * 512], BF16))
           for i in range(2)]
    lg_t = stm.enter_context(nc.sbuf_tensor("lg_t", [128, 32], F32))
    combine = [stm.enter_context(nc.sbuf_tensor("combine%d" % i, [128, 32], F32))
               for i in range(2)]
    combT = stm.enter_context(nc.sbuf_tensor("combT", [8, 512], BF16))
    logT = stm.enter_context(nc.sbuf_tensor("logT", [8, 512], F32))
    mres = stm.enter_context(nc.sbuf_tensor("mres", [128, D], F32))
    mout = stm.enter_context(nc.sbuf_tensor("mout", [128, D], F32))
    mstat = stm.enter_context(nc.sbuf_tensor("mstat", [128, 10], F32))
    rstat = stm.enter_context(nc.sbuf_tensor("rstat", [128, 32], F32))
    wexp_t = stm.enter_context(nc.sbuf_tensor("wexp_t", [128, 32], F32))
    wsm_t = stm.enter_context(nc.sbuf_tensor("wsm_t", [128, 32], F32))
    b2_sb = stm.enter_context(nc.sbuf_tensor("b2_sb", [NE, D], BF16))
    b1_sb = stm.enter_context(nc.sbuf_tensor("b1_sb", [128, NE * 32], F32))

    ps_h = [psA, psB]
    ps_ya = [ps_sT[0], ps_sT[1]]
    last_ev[(id(psA), 0)] = (s_st, att_st_done)
    last_ev[(id(psB), 0)] = (s_st, att_st_done)

    pass_mm1_end = [0] * (NPASS + 1)
    mm1_done = [s_pe.n]
    mm2_done = [s_pe.n]
    w1_ld = [0]
    w2_ld = [0]

    dma("gpsimd", b2_sb[:], b2_d, s_gp, waits=[(s_st, att_st_done)])
    gp_b2 = s_gp.n
    dma("sync", b1_sb[:], b1t_d, s_ld, waits=[(s_st, att_st_done)])
    dma("sync", ch_out(w1sb, DFF), ch_in(w1bf_d[0]), s_ld,
        waits=[(s_gp, gp_conv[0])])
    w1_ld[-1] = s_ld.n
    dma("sync", ch_out(w2sb, D), ch_in(w2bf_d[0]), s_ld)
    w2_ld[-1] = s_ld.n
    ld_moe_consts = s_ld.n

    sm_done = [0]

    def mk_router(p, ld_target=None):
        cmb = combine[p % 2]
        xt = x1t[p % 2]
        rt = ps_ya[0]

        def r_mms():
            for dk in range(8):
                w = [(s_ld, ld_target), (s_ve, None), (s_gp, gp_proj)] if dk == 0 else []
                mm(rt[0:8, 0:512], wg_sb[:, dk * 8:(dk + 1) * 8],
                   xt[:, dk * 512:(dk + 1) * 512], dk == 0, dk == 7,
                   waits=w, inc=s_pe if dk == 7 else None)
            act(logT[:], rt[0:8, 0:512], AF.Identity, bias=bg_sb[:],
                waits=[(s_pe, None), (s_ld, ld_moe_consts)], inc=s_sc)

        def r_lt(q4):
            P.op("tensor", lambda e, o=rt[:, 512:520],
                 i=logT[:, q4 * 128:(q4 + 1) * 128], idt=id8[:]:
                 e.transpose(o, i, idt), waits=[(s_sc, None), (s_ve, None)],
                 inc=s_pe)
            act(lg_t[:, q4 * 8:(q4 + 1) * 8], rt[:, 512:520], AF.Copy,
                waits=[(s_pe, None)], inc=s_sc)

        def r_sm():
            LG = [lg_t[:, q * 8:(q + 1) * 8] for q in range(4)]
            CM = [cmb[:, q * 8:(q + 1) * 8] for q in range(4)]
            WX = [wexp_t[:, q * 8:(q + 1) * 8] for q in range(4)]
            WS = [wsm_t[:, q * 8:(q + 1) * 8] for q in range(4)]
            STc = lambda j, q: rstat[:, 4 * j + q:4 * j + q + 1]
            dr = lambda: P.op("vector", lambda e: e.drain())
            for q in range(4):
                P.op("vector", lambda e, o=STc(0, q), i=LG[q]:
                     e.reduce_max(o, i, axis=AX.X, negate=True),
                     waits=[(s_sc, None)] if q == 0 else (),
                     inc=s_ve if q == 3 else None)
            for q in range(4):
                act(WX[q], LG[q], AF.Exp, bias=STc(0, q),
                    waits=[(s_ve, None)] if q == 0 else (),
                    inc=s_sc if q == 3 else None)
            for q in range(4):
                P.op("vector", lambda e, o=STc(1, q), i=WX[q]:
                     e.reduce_sum(o, i, axis=AX.X),
                     waits=[(s_sc, None)] if q == 0 else ())
            dr()
            for q in range(4):
                P.op("vector", lambda e, o=STc(2, q), i=STc(1, q): e.reciprocal(o, i))
            dr()
            for q in range(4):
                P.op("vector", lambda e, o=WS[q], i=WX[q], sc=STc(2, q):
                     e.tensor_scalar(o, i, sc, None, ALU.mult))
            dr()
            for q in range(4):
                P.op("vector", lambda e, o=STc(3, q), i=WS[q]:
                     e.reduce_max(o, i, axis=AX.X))
            dr()
            for q in range(4):
                P.op("vector", lambda e, o=CM[q], i=WS[q], sc=STc(3, q):
                     e.tensor_scalar(o, i, sc, None, ALU.is_lt))
            dr()
            for q in range(4):
                P.op("vector", lambda e, o=CM[q], a=WS[q], b=CM[q]: e.tensor_mul(o, a, b))
            dr()
            for q in range(4):
                P.op("vector", lambda e, o=STc(4, q), i=CM[q]:
                     e.reduce_max(o, i, axis=AX.X))
            dr()
            for q in range(4):
                P.op("vector", lambda e, o=STc(5, q), a=STc(3, q), b=STc(4, q):
                     e.tensor_add(o, a, b))
            for q in range(4):
                P.op("vector", lambda e, o=CM[q], i=WS[q], sc=STc(4, q):
                     e.tensor_scalar(o, i, sc, None, ALU.is_ge))
            dr()
            for q in range(4):
                P.op("vector", lambda e, o=STc(6, q), i=STc(5, q): e.reciprocal(o, i))
            for q in range(4):
                P.op("vector", lambda e, o=CM[q], a=WS[q], b=CM[q]: e.tensor_mul(o, a, b))
            dr()
            for q in range(4):
                P.op("vector", lambda e, o=CM[q], i=CM[q], sc=STc(6, q):
                     e.tensor_scalar(o, i, sc, None, ALU.mult),
                     inc=s_ve if q == 3 else None)
            sm_done[0] = s_ve.n

        def r_ct(q4):
            P.op("tensor", lambda e, o=rt[0:8, 512:640],
                 i=cmb[:, q4 * 8:(q4 + 1) * 8], idt=id128[:]:
                 e.transpose(o, i, idt),
                 waits=[(s_ve, None), (s_sc, None)], inc=s_pe)
            act(combT[:, q4 * 128:(q4 + 1) * 128], rt[0:8, 512:640], AF.Copy,
                waits=[(s_pe, None)], inc=s_sc)

        return r_mms, r_lt, r_sm, r_ct

    def emit_yinit(ct_done, ya_free, scr_gate):
        rt = ps_ya[0]
        for tm in range(4):
            for dc in range(2):
                w = ()
                if dc == 0:
                    w = [(s_sc, ct_done), (s_gp, gp_b2), (s_ve, ya_free),
                         (s_sc, None)]
                mm(rt[:, dc * 512:(dc + 1) * 512],
                   combT[:, tm * 128:(tm + 1) * 128],
                   b2_sb[:, dc * 512:(dc + 1) * 512], True, True,
                   waits=w, inc=s_pe if dc == 1 else None)
            act(y_acc[:, tm * D:(tm + 1) * D], rt[:], AF.Copy,
                waits=[(s_pe, None), (s_ve, scr_gate)], inc=s_sc)
        return s_sc.n

    ve_yacc_scr = [0]

    def emit_pass_ln(p, gate_a, gate_b):
        for tm in range(4):
            row0 = p * PT_ + tm * 128
            dma("sync", mres[:], x1_d[row0:row0 + 128, :], s_ld,
                waits=[(s_st, None)])
            g = gate_a if tm < 2 else gate_b
            P.op("vector", lambda e, o=mout[:], a=y_acc[:, tm * D:(tm + 1) * D],
                 b=mres[:]:
                 e.tensor_add(o, a, b), waits=[(s_ld, None), (s_ve, g)],
                 inc=s_ve)
            scr, _ = ln_emit(mout, mstat, mres, y_acc[:, tm * D:(tm + 1) * D])
            ve_yacc_scr[0] = scr
            P.op("scalar", lambda e, o=out_d[row0:row0 + 128, :], i=mres[:]:
                 e.dma_start(out=o, in_=i), waits=[(s_sc, None)],
                 inc=s_st, inc_amt=16)

    for k in range(8):
        w = [(s_gp, x1bf_cnt[0]), (s_st, att_st_done)] if k == 0 else []
        dma("sync", x1t[0][:, k * 512:(k + 1) * 512],
            x1bf_d[0:PT_, k * 128:(k + 1) * 128], s_ld, waits=w, transpose=True)
    r = mk_router(0, s_ld.n)
    r[0]()
    for q4 in range(4):
        r[1](q4)
    r[2]()
    for q4 in range(4):
        r[3](q4)
    ct_done = [s_sc.n]
    ya_free = [0]
    rt_guard = [(s_sc, 0)]
    nxt = [None]

    for p in range(NPASS):
        if p + 1 < NPASS:
            for k in range(8):
                w = []
                if k == 0:
                    w = [(s_gp, x1bf_cnt[p + 1]), (s_st, att_st_done)]
                    if p >= 1:
                        w.append((s_pe, pass_mm1_end[p - 1]))
                dma("sync", x1t[(p + 1) % 2][:, k * 512:(k + 1) * 512],
                    x1bf_d[(p + 1) * PT_:(p + 2) * PT_, k * 128:(k + 1) * 128],
                    s_ld, waits=w, transpose=True)
            nxt[0] = mk_router(p + 1, s_ld.n)
        else:
            nxt[0] = None

        xt_cur = x1t[p % 2]
        sc_yinit = emit_yinit(ct_done[0], ya_free[0], ve_yacc_scr[0])

        for e_ in range(NE):
            if not (p == 0 and e_ == 0):
                dma("sync", ch_out(w1sb, DFF), ch_in(w1bf_d[e_]),
                    s_ld, waits=[(s_pe, mm1_done[-1])])
                w1_ld.append(s_ld.n)
                dma("sync", ch_out(w2sb, D), ch_in(w2bf_d[e_]),
                    s_ld, waits=[(s_pe, mm2_done[-1])])
                w2_ld.append(s_ld.n)
            for fm in range(32):
                buf = ps_h[fm % 2]
                key = (id(buf), 0)
                for dk in range(8):
                    w = []
                    if dk == 0:
                        w = [(s_ld, w1_ld[-1])] + buf_wait(key)
                    mm(buf[:], w1sb[:, dk * DFF + fm * 128: dk * DFF + (fm + 1) * 128],
                       xt_cur[:, dk * 512:(dk + 1) * 512], dk == 0, dk == 7,
                       waits=w, inc=s_pe if dk == 7 else None)
                act(h_sb[:, fm * 512:(fm + 1) * 512], buf[:], AF.Relu,
                    bias=b1_sb[:, e_ * 32 + fm: e_ * 32 + fm + 1],
                    waits=[(s_pe, None), (s_ld, ld_moe_consts)], inc=s_h2)
                last_ev[key] = (s_h2, s_h2.n)
                if nxt[0] is not None:
                    if e_ == 3:
                        if fm == 2:
                            nxt[0][0]()
                        elif fm in (8, 10, 12, 14):
                            nxt[0][1]((fm - 8) // 2)
                            if fm == 14:
                                rt_guard[0] = (s_sc, s_sc.n)
                        elif fm == 16:
                            nxt[0][2]()
                    elif e_ == 4 and fm in (8, 10, 12, 14):
                        nxt[0][3]((fm - 8) // 2)
                        if fm == 14:
                            ct_done[0] = s_sc.n
                            rt_guard[0] = (s_sc, s_sc.n)
            mm1_done.append(s_pe.n)
            if e_ == NE - 1:
                pass_mm1_end[p] = s_pe.n
            sc_h = s_h2.n
            for half2 in range(2):
                for fk in range(32):
                    for tmi in range(2):
                        tm = half2 * 2 + tmi
                        for dc in range(2):
                            w = []
                            if fk == 0 and tmi == 0 and dc == 0:
                                w = [(s_ld, w2_ld[-1]), (s_h2, sc_h), (s_ve, None),
                                     rt_guard[0]]
                            mm(ps_ya[tmi][:, dc * 512:(dc + 1) * 512],
                               h_sb[:, fk * 512 + tm * 128: fk * 512 + (tm + 1) * 128],
                               w2sb[:, fk * D + dc * 512: fk * D + (dc + 1) * 512],
                               fk == 0, fk == 31, waits=w,
                               inc=s_pe if (fk == 31 and tmi == 1 and dc == 1) else None)
                for tmi in range(2):
                    tm = half2 * 2 + tmi
                    P.op("vector", lambda e, o=y_acc[:, tm * D:(tm + 1) * D],
                         a=ps_ya[tmi][:],
                         s=combine[p % 2][:, tm * 8 + e_: tm * 8 + e_ + 1]:
                         e.scalar_tensor_tensor(o, a, s, o, ALU.mult, ALU.add),
                         waits=[(s_pe, None), (s_sc, sc_yinit)] if tmi == 0 else [],
                         inc=s_ve)
                if half2 == 0:
                    gate_a = s_ve.n
                else:
                    gate_b = s_ve.n
            mm2_done.append(s_pe.n)
        ya_free[0] = s_ve.n
        emit_pass_ln(p, gate_a, gate_b)

    P.op("sync", lambda e: e.nop(), waits=[(s_st, s_st.n)])

    with nc.Block() as block:
        @block.sync
        def _(eng):
            for fn in P.ops["sync"]:
                fn(eng)

        @block.gpsimd
        def _(eng):
            for fn in P.ops["gpsimd"]:
                fn(eng)

        @block.vector
        def _(eng):
            for fn in P.ops["vector"]:
                fn(eng)

        @block.scalar
        def _(eng):
            for fn in P.ops["scalar"]:
                fn(eng)

        @block.tensor
        def _(eng):
            for fn in P.ops["tensor"]:
                fn(eng)

    stm.close()
    st.close()
    return nc


def _consts_np():
    inv_freq = 10000.0 ** (-np.arange(0, DHR, 2, dtype=np.float64) / DHR)
    t = np.arange(S, dtype=np.float64)
    ang = inv_freq[:, None] * t[None, :]
    C = np.repeat(np.cos(ang), 2, axis=0).astype(np.float32)
    sin = np.sin(ang)
    Spm = np.empty((DHR, S), np.float64)
    Spm[0::2] = -sin
    Spm[1::2] = sin
    masks = np.zeros((128, 4 * 512), np.float32)
    jj = np.arange(128)[:, None]
    ii = np.arange(512)[None, :]
    for r in range(4):
        masks[:, r * 512:(r + 1) * 512] = (jj + 128 * r <= ii).astype(np.float32)
    consts = np.zeros((128, 4), np.float32)
    consts[:, 0] = 1.0 / np.sqrt(96.0)
    consts[:, 1] = 1e-5
    consts[:, 2] = -0.5
    consts[:, 3] = 1.0 / D
    bf = ml_dtypes.bfloat16
    CS = np.concatenate([C, Spm.astype(np.float32)], axis=0)
    return (CS.astype(bf), masks.astype(bf), consts)


def _cat_swap(W, blk):
    n = W.shape[0] // blk
    out = np.empty((n * 2 * blk, W.shape[1]), W.dtype)
    perm = np.arange(blk) ^ 1
    for i in range(n):
        out[i * 2 * blk: i * 2 * blk + blk] = W[i * blk:(i + 1) * blk]
        out[i * 2 * blk + blk: (i + 1) * 2 * blk] = W[i * blk:(i + 1) * blk][perm]
    return out


def kernel(L0_attn_x, L0_attn_W_DQ, L0_attn_W_UQ, L0_attn_W_QR, L0_attn_W_DKV,
           L0_attn_W_UK, L0_attn_W_UV, L0_attn_W_KR, L0_attn_W_O,
           L0_ffn_W1, L0_ffn_b1, L0_ffn_W2, L0_ffn_b2, L0_ffn_Wg, L0_ffn_bg):
    nc = bass.Bass("TRN2", target_bir_lowering=False)
    build(nc)

    CS, masks, consts = _consts_np()
    asc = np.ascontiguousarray
    wqrc = _cat_swap(L0_attn_W_QR, DHR)
    wkrc = _cat_swap(L0_attn_W_KR, DHR)
    common = {
        "wdqt": asc(L0_attn_W_DQ.T), "wdkvt": asc(L0_attn_W_DKV.T),
        "wkrct": asc(wkrc.T), "wuqt": asc(L0_attn_W_UQ.T),
        "wqrct": asc(wqrc.T), "wukt": asc(L0_attn_W_UK.T),
        "wuvt": asc(L0_attn_W_UV.T), "wot": asc(L0_attn_W_O.T),
        "w1": asc(L0_ffn_W1), "w2": asc(L0_ffn_W2),
        "b1t": asc(L0_ffn_b1.reshape(NE, 32, 128).transpose(2, 0, 1).reshape(128, NE * 32)),
        "b2": asc(L0_ffn_b2),
        "wgt": asc(L0_ffn_Wg.reshape(8, 128, 8).transpose(1, 0, 2).reshape(128, 64)),
        "bg": asc(L0_ffn_bg.reshape(NE, 1)),
        "ropecs": CS, "masks": masks,
        "id8": np.eye(8, dtype=np.float32), "id128": np.eye(128, dtype=np.float32),
        "ones64": np.ones((1, 64), np.float32), "consts": consts,
    }
    in_maps = []
    for c in range(8):
        m = dict(common)
        xc = asc(np.asarray(L0_attn_x[c], dtype=np.float32))
        m["x"] = xc
        m["xT"] = asc(xc.T)
        in_maps.append(m)
    res = run_bass_kernel_spmd(nc, in_maps, core_ids=list(range(8)))
    return np.stack([np.asarray(r["out"], dtype=np.float32) for r in res.results])


# version 11 (final)
# speedup vs baseline: 1.4210x; 1.0579x over previous
import numpy as np
import ml_dtypes
import concourse.bass as bass
import concourse.mybir as mybir
from concourse.bass_utils import run_bass_kernel_spmd
from contextlib import ExitStack

F32 = mybir.dt.float32
BF16 = mybir.dt.bfloat16
AF = mybir.ActivationFunctionType
ALU = mybir.AluOpType
AX = mybir.AxisListType

S, D, H, DH, DHR, LAT = 4096, 1024, 16, 64, 32, 512
DFF, NE = 4096, 8
TAU = float(np.sqrt(96.0))
PLO = float(np.exp(-80.0 / TAU))
PHI = float(np.exp(80.0 / TAU))
NPASS = 8
PT_ = S // NPASS
VSL = 260


class Sem:
    def __init__(self, h):
        self.h = h
        self.n = 0


class Prog:
    def __init__(self):
        self.ops = {k: [] for k in ("sync", "gpsimd", "vector", "scalar", "tensor")}

    def op(self, eng, f, waits=(), inc=None, inc_amt=1):
        wl = [(s.h, s.n if tgt is None else tgt) for (s, tgt) in waits]
        if inc is not None:
            inc.n += inc_amt
            pair = (inc.h, inc_amt)
        else:
            pair = None

        def fn(e, wl=wl, f=f, pair=pair):
            for h, v in wl:
                e.wait_ge(h, v)
            inst = f(e)
            if pair is not None:
                inst.then_inc(*pair)
        self.ops[eng].append(fn)


def build(nc):
    x_d = nc.dram_tensor("x", [S, D], F32, kind="ExternalInput").ap()
    xT_d = nc.dram_tensor("xT", [D, S], F32, kind="ExternalInput").ap()
    wdqt_d = nc.dram_tensor("wdqt", [D, LAT], F32, kind="ExternalInput").ap()
    wdkvt_d = nc.dram_tensor("wdkvt", [D, LAT], F32, kind="ExternalInput").ap()
    wkrct_d = nc.dram_tensor("wkrct", [D, 64], F32, kind="ExternalInput").ap()
    wuqt_d = nc.dram_tensor("wuqt", [LAT, D], F32, kind="ExternalInput").ap()
    wqrct_d = nc.dram_tensor("wqrct", [LAT, D], F32, kind="ExternalInput").ap()
    wukt_d = nc.dram_tensor("wukt", [LAT, D], F32, kind="ExternalInput").ap()
    wuvt_d = nc.dram_tensor("wuvt", [LAT, D], F32, kind="ExternalInput").ap()
    wot_d = nc.dram_tensor("wot", [D, D], F32, kind="ExternalInput").ap()
    w1_d = nc.dram_tensor("w1", [NE, D, DFF], F32, kind="ExternalInput").ap()
    w2_d = nc.dram_tensor("w2", [NE, DFF, D], F32, kind="ExternalInput").ap()
    b1t_d = nc.dram_tensor("b1t", [128, NE * 32], F32, kind="ExternalInput").ap()
    b2_d = nc.dram_tensor("b2", [NE, D], F32, kind="ExternalInput").ap()
    wgt_d = nc.dram_tensor("wgt", [128, 64], F32, kind="ExternalInput").ap()
    bg_d = nc.dram_tensor("bg", [NE, 1], F32, kind="ExternalInput").ap()
    ropecs_d = nc.dram_tensor("ropecs", [64, S], BF16, kind="ExternalInput").ap()
    masks_d = nc.dram_tensor("masks", [128, 4 * 512], BF16, kind="ExternalInput").ap()
    id8_d = nc.dram_tensor("id8", [8, 8], F32, kind="ExternalInput").ap()
    id128_d = nc.dram_tensor("id128", [128, 128], F32, kind="ExternalInput").ap()
    ones64_d = nc.dram_tensor("ones64", [1, 64], F32, kind="ExternalInput").ap()
    consts_d = nc.dram_tensor("consts", [128, 4], F32, kind="ExternalInput").ap()

    out_d = nc.dram_tensor("out", [S, D], F32, kind="ExternalOutput").ap()

    w1bf_d = nc.dram_tensor("w1bf", [NE, D, DFF], BF16, kind="Internal").ap()
    w2bf_d = nc.dram_tensor("w2bf", [NE, DFF, D], BF16, kind="Internal").ap()
    x1_d = nc.dram_tensor("x1f", [S, D], F32, kind="Internal").ap()
    x1bf_d = nc.dram_tensor("x1bf", [S, D], BF16, kind="Internal").ap()

    P = Prog()
    st = ExitStack()
    sb = lambda n, s_, d_: st.enter_context(nc.sbuf_tensor(n, s_, d_))
    psal = lambda n, s_: st.enter_context(nc.psum_tensor(n, s_, F32))
    sem = lambda n: Sem(st.enter_context(nc.semaphore(n)))
    s_ld = sem("s_ld")
    s_gp = sem("s_gp")
    s_pe = sem("s_pe")
    s_sc = sem("s_sc")
    s_ve = sem("s_ve")
    s_st = sem("s_st")
    s_h2 = sem("s_h2")

    consts = sb("consts_s", [128, 4], F32)
    id8 = sb("id8_s", [8, 8], F32)
    id128 = sb("id128_s", [128, 128], F32)
    ones64 = sb("ones64_s", [1, 64], F32)
    wg_sb = sb("wg_sb", [128, 64], BF16)
    bg_sb = sb("bg_sb", [NE, 1], F32)

    def dma(eng, out, in_, semc, waits=(), **kw):
        P.op(eng, lambda e, o=out, i=in_, kw=kw: e.dma_start(out=o, in_=i, **kw),
             waits=waits, inc=semc, inc_amt=16)

    def ch_in(dram_ap):
        return dram_ap.rearrange("(k p) t -> p k t", p=128)

    def ch_out(sb_t, inner):
        return sb_t[:].rearrange("p (k t) -> p k t", t=inner)

    dma("sync", consts[:], consts_d, s_ld)
    dma("sync", id8[:], id8_d, s_ld)
    dma("sync", id128[:], id128_d, s_ld)
    dma("sync", ones64[:], ones64_d, s_ld)
    dma("sync", bg_sb[:], bg_d, s_ld)
    SC_SQRT = consts[:, 0:1]
    SC_EPS = consts[:, 1:2]
    SC_NHALF = consts[:, 2:3]

    stA = ExitStack()
    sba = lambda n, sh, d_: stA.enter_context(nc.sbuf_tensor(n, sh, d_))
    cqt = sba("cqt", [128, 4 * S], BF16)
    ckvt = sba("ckvt", [128, 4 * S], BF16)
    krt = sba("krt", [DHR, S], BF16)
    tabCS = sba("tabCS", [64, S], BF16)
    masks = sba("masks_s", [128, 4 * 512], BF16)
    qrs = [sba("qrs%d" % i, [64, 512], BF16) for i in range(2)]
    rt1 = sba("rt1", [DHR, 512], BF16)
    rt2 = sba("rt2", [DHR, 512], BF16)

    dma("sync", tabCS[:], ropecs_d, s_ld)
    dma("sync", masks[:], masks_d, s_ld)
    dma("gpsimd", wg_sb[:], wgt_d, s_gp)

    psA = psal("psA", [128, 512])
    psB = psal("psB", [128, 512])
    ps_sT = [psal("ps_sT0", [128, 1024]), psal("ps_sT1", [128, 1024])]
    ps_oc = [psal("ps_o0", [96, 512]), psal("ps_o1", [96, 512])]
    ps_zb = psB

    ld_base = s_ld.n

    last_ev = {}

    def mm(out, lhsT, rhs, start, stop, waits=(), inc=None):
        P.op("tensor",
             lambda e, o=out, a=lhsT, b=rhs, s1=start, s2=stop:
             e.matmul(o, a, b, start=s1, stop=s2), waits=waits, inc=inc)

    def act(out, in_, func, bias=0.0, scale=1.0, waits=(), inc=None):
        P.op("scalar",
             lambda e, o=out, i=in_, f=func, b=bias, s=scale:
             e.activation(out=o, in_=i, func=f, bias=b, scale=s),
             waits=waits, inc=inc)

    pp4 = [(ps_sT[0], 0), (ps_sT[0], 512), (ps_sT[1], 0), (ps_sT[1], 512)]
    pi = [0]

    def nbuf():
        b = pp4[pi[0] % 4]
        pi[0] += 1
        return b

    def buf_wait(key):
        v = last_ev.get(key)
        return [v] if v is not None else []

    def evict(dst, key, src_ap, func=AF.Copy, bias=0.0, extra_waits=()):
        act(dst, src_ap, func, bias=bias,
            waits=[(s_pe, None)] + list(extra_waits), inc=s_sc)
        last_ev[key] = (s_sc, s_sc.n)

    stp = ExitStack()
    xt_sb = stp.enter_context(nc.sbuf_tensor("xt_sb", [128, 8 * S], BF16))
    wdqt = stp.enter_context(nc.sbuf_tensor("wdqt_s", [128, 8 * LAT], BF16))
    wdkvt = stp.enter_context(nc.sbuf_tensor("wdkvt_s", [128, 8 * LAT], BF16))
    wkrct = stp.enter_context(nc.sbuf_tensor("wkrct_s", [128, 8 * 64], BF16))

    dma("gpsimd", ch_out(xt_sb, S), ch_in(xT_d), s_gp)
    dma("gpsimd", ch_out(wdqt, LAT), ch_in(wdqt_d), s_gp)
    dma("gpsimd", ch_out(wdkvt, LAT), ch_in(wdkvt_d), s_gp)
    dma("gpsimd", ch_out(wkrct, 64), ch_in(wkrct_d), s_gp)
    gp_proj = s_gp.n

    w0 = [(s_gp, gp_proj), (s_ld, ld_base)]

    def proj128(dst, dst_col0, wt, wt_cols, row0, M):
        for n in range(S // 512):
            t, off = nbuf()
            key = (id(t), off)
            w = w0 + buf_wait(key)
            for dk in range(8):
                mm(t[0:M, off:off + 512],
                   wt[:, dk * wt_cols + row0: dk * wt_cols + row0 + M],
                   xt_sb[:, dk * S + n * 512: dk * S + (n + 1) * 512],
                   dk == 0, dk == 7, waits=w if dk == 0 else (),
                   inc=s_pe if dk == 7 else None)
            evict(dst[:, dst_col0 + n * 512: dst_col0 + (n + 1) * 512],
                  key, t[0:M, off:off + 512])

    for lm in range(4):
        proj128(cqt, lm * S, wdqt, LAT, lm * 128, 128)
    for lm in range(4):
        proj128(ckvt, lm * S, wdkvt, LAT, lm * 128, 128)

    def rope_emit(dst, stage, tab_c):
        P.op("vector", lambda e, o=rt1[:], a=stage[0:DHR, :],
             b=tabCS[0:DHR, tab_c:tab_c + 512]:
             e.tensor_mul(o, a, b), waits=[(s_sc, None), (s_ld, ld_base)])
        P.op("vector", lambda e, o=rt2[:], a=stage[DHR:64, :],
             b=tabCS[DHR:64, tab_c:tab_c + 512]:
             e.tensor_mul(o, a, b))
        P.op("vector", lambda e, o=dst, a=rt1[:], b=rt2[:]:
             e.tensor_add(o, a, b), inc=s_ve)

    for n in range(S // 512):
        t, off = nbuf()
        key = (id(t), off)
        w = w0 + buf_wait(key) + [(s_ve, None)]
        for dk in range(8):
            mm(t[0:64, off:off + 512], wkrct[:, dk * 64:(dk + 1) * 64],
               xt_sb[:, dk * S + n * 512: dk * S + (n + 1) * 512],
               dk == 0, dk == 7, waits=w if dk == 0 else (),
               inc=s_pe if dk == 7 else None)
        evict(qrs[n % 2][:], key, t[0:64, off:off + 512],
              extra_waits=[(s_ve, None)])
        rope_emit(krt[:, n * 512:(n + 1) * 512], qrs[n % 2], n * 512)

    pe_projdone = s_pe.n
    stp.close()

    ot_sb = sba("ot_sb", [128, 8 * 2048], BF16)

    x1st_cnt = [0]
    conv_emitted = [False]
    gp_conv = [0]

    def ln_emit(X, stt, out_f32, scratch, store_wait=None):
        sum1 = stt[:, 0:1]; ss = stt[:, 1:2]; mean = stt[:, 2:3]
        mvar = stt[:, 3:4]; m2 = stt[:, 4:5]; var = stt[:, 5:6]
        lnv = stt[:, 6:7]; rstd = stt[:, 7:8]; nb = stt[:, 8:9]
        P.op("vector", lambda e, o=scratch, a=X[:], b=ss:
             e.scalar_tensor_tensor(o, a, 1.0, a, ALU.mult, ALU.mult, accum_out=b))
        P.op("vector", lambda e, o=sum1, i=X[:]: e.reduce_sum(o, i, axis=AX.X),
             inc=s_ve)
        ve_scr = s_ve.n
        P.op("vector", lambda e: e.drain())
        P.op("vector", lambda e, o=mean, i=sum1:
             e.tensor_scalar(o, i, 1.0 / D, None, ALU.mult))
        P.op("vector", lambda e, o=mvar, i=ss:
             e.tensor_scalar(o, i, 1.0 / D, None, ALU.mult))
        P.op("vector", lambda e: e.drain())
        P.op("vector", lambda e, o=m2, a=mean, b=mean: e.tensor_mul(o, a, b))
        P.op("vector", lambda e: e.drain())
        P.op("vector", lambda e, o=var, a=mvar, b=m2: e.tensor_sub(o, a, b),
             inc=s_ve)
        act(lnv, var, AF.Ln, bias=SC_EPS, waits=[(s_ve, None)])
        act(rstd, lnv, AF.Exp, scale=SC_NHALF, inc=s_sc)
        P.op("vector", lambda e, o=nb, i=rstd, m=mean:
             e.tensor_scalar(o, i, m, -1.0, ALU.mult, ALU.mult),
             waits=[(s_sc, None)])
        P.op("vector", lambda e: e.drain(), inc=s_ve)
        aw = [(s_ve, None)]
        if store_wait is not None:
            aw.append(store_wait)
        act(out_f32[:], X[:], AF.Identity, bias=nb, scale=rstd, waits=aw,
            inc=s_sc)
        return ve_scr, s_sc.n

    def run_half(half):
        kj_end = 2048 * (half + 1)
        nkt = kj_end // 128
        qi0 = half * 2048

        sth = ExitStack()
        wuqt = sth.enter_context(nc.sbuf_tensor("wuqt_s%d" % half, [128, 4 * D], BF16))
        wqrct = sth.enter_context(nc.sbuf_tensor("wqrct_s%d" % half, [128, 4 * D], BF16))
        wukt = sth.enter_context(nc.sbuf_tensor("wukt_s%d" % half, [128, 4 * D], BF16))
        wuvt = sth.enter_context(nc.sbuf_tensor("wuvt_s%d" % half, [128, 4 * D], BF16))
        khT = sth.enter_context(nc.sbuf_tensor("khT%d" % half, [96, S], BF16))
        qhT = sth.enter_context(nc.sbuf_tensor("qhT%d" % half, [96, 2048], BF16))
        v4 = sth.enter_context(nc.sbuf_tensor("v4_%d" % half, [128, 32 * VSL + 36], BF16))
        Ebuf = [sth.enter_context(nc.sbuf_tensor("E%d_%d" % (half, i), [128, 1024], BF16))
                for i in range(3)]
        Pbuf = [sth.enter_context(nc.sbuf_tensor("P%d_%d" % (half, i), [128, 1024], BF16))
                for i in range(3)]
        o_f32 = sth.enter_context(nc.sbuf_tensor("of%d" % half, [64, 512], F32))
        zrow = sth.enter_context(nc.sbuf_tensor("zr%d" % half, [1, 512], F32))
        rz = sth.enter_context(nc.sbuf_tensor("rz%d" % half, [1, 512], F32))

        if half == 0:
            wq_wait = [(s_gp, gp_proj), (s_pe, pe_projdone)]
        else:
            wq_wait = [(s_st, None)]
        dma("gpsimd", ch_out(wuqt, D), ch_in(wuqt_d), s_gp, waits=wq_wait)
        dma("gpsimd", ch_out(wqrct, D), ch_in(wqrct_d), s_gp)
        dma("gpsimd", ch_out(wukt, D), ch_in(wukt_d), s_gp)
        dma("gpsimd", ch_out(wuvt, D), ch_in(wuvt_d), s_gp)
        gp_wu = s_gp.n

        if not conv_emitted[0]:
            for e in range(NE):
                dma("gpsimd", w1bf_d[e], w1_d[e], s_gp)
                dma("gpsimd", w2bf_d[e], w2_d[e], s_gp)
            gp_conv[0] = s_gp.n
            conv_emitted[0] = True

        vw = [(s_pe, pe_projdone)] if half == 0 else [(s_st, None)]
        for hh in range(4):
            P.op("vector", lambda e, o=v4[:, hh * 65 + 64: 32 * VSL: VSL]:
                 e.memset(o, 1.0), waits=vw if hh == 0 else ())
        P.op("vector", lambda e, o=v4[:, 32 * VSL: 32 * VSL + 36]:
             e.memset(o, 0.0), inc=s_ve)
        ve_vinit = s_ve.n
        P.op("vector", lambda e, o=khT[64:96, 0:kj_end], i=krt[:, 0:kj_end]:
             e.tensor_copy(out=o, in_=i),
             waits=(vw if half else []) + [(s_ve, None)], inc=s_ve)

        o_free = {}
        pend_z = [None]
        for hg in range(4):
            for tt in range(nkt):
                t, off = nbuf()
                key = (id(t), off)
                w = buf_wait(key) + [(s_gp, gp_wu), (s_ve, ve_vinit)]
                for lk in range(4):
                    mm(t[:, off:off + 256],
                       ckvt[:, lk * S + tt * 128: lk * S + (tt + 1) * 128],
                       wuvt[:, lk * D + hg * 256: lk * D + hg * 256 + 256],
                       lk == 0, lk == 3, waits=w if lk == 0 else (),
                       inc=s_pe if lk == 3 else None)
                act(v4[:, tt * VSL: tt * VSL + VSL]
                    .rearrange("p (b c) -> p b c", c=65)[:, :, 0:64],
                    t[:, off:off + 256].rearrange("p (b c) -> p b c", c=64),
                    AF.Copy, waits=[(s_pe, None)], inc=s_sc)
                last_ev[key] = (s_sc, s_sc.n)

            for hh in range(4):
                h = hg * 4 + hh
                for n in range(4):
                    t, off = nbuf()
                    key = (id(t), off)
                    w = buf_wait(key)
                    for lk in range(4):
                        mm(t[0:64, off:off + 512],
                           wuqt[:, lk * D + h * 64: lk * D + h * 64 + 64],
                           cqt[:, lk * S + qi0 + n * 512: lk * S + qi0 + (n + 1) * 512],
                           lk == 0, lk == 3, waits=w if lk == 0 else (),
                           inc=s_pe if lk == 3 else None)
                    evict(qhT[0:64, n * 512:(n + 1) * 512], key,
                          t[0:64, off:off + 512])
                for n in range(4):
                    t, off = nbuf()
                    key = (id(t), off)
                    w = buf_wait(key)
                    for lk in range(4):
                        mm(t[0:64, off:off + 512],
                           wqrct[:, lk * D + h * 64: lk * D + h * 64 + 64],
                           cqt[:, lk * S + qi0 + n * 512: lk * S + qi0 + (n + 1) * 512],
                           lk == 0, lk == 3, waits=w if lk == 0 else (),
                           inc=s_pe if lk == 3 else None)
                    evict(qrs[n % 2][:], key, t[0:64, off:off + 512],
                          extra_waits=[(s_ve, None)])
                    rope_emit(qhT[64:96, n * 512:(n + 1) * 512], qrs[n % 2],
                              qi0 + n * 512)
                for n in range(kj_end // 512):
                    t, off = nbuf()
                    key = (id(t), off)
                    w = buf_wait(key)
                    for lk in range(4):
                        mm(t[0:64, off:off + 512],
                           wukt[:, lk * D + h * 64: lk * D + h * 64 + 64],
                           ckvt[:, lk * S + n * 512: lk * S + (n + 1) * 512],
                           lk == 0, lk == 3, waits=w if lk == 0 else (),
                           inc=s_pe if lk == 3 else None)
                    evict(khT[0:64, n * 512:(n + 1) * 512], key,
                          t[0:64, off:off + 512])
                sc_head = s_sc.n
                ve_head = s_ve.n

                for cl in range(4):
                    gc = half * 4 + cl
                    ktiles = 4 * gc + 4
                    qsl = slice(cl * 512, (cl + 1) * 512)
                    po = ps_oc[gc % 2]
                    exp_done = {}
                    pt_done = {}
                    pv_done = {}

                    def emit_pv(g, ktiles=ktiles, po=po, gc=gc):
                        ei = g % 3
                        for sub in range(2):
                            j = 2 * g + sub
                            w = [(s_ve, pt_done[g])] if sub == 0 else []
                            if sub == 0 and g == 0:
                                w.append(o_free.get(gc % 2, (s_ve, 0)))
                                w.append((s_sc, None))
                            mm(po[:], v4[:, j * VSL + hh * 65: j * VSL + hh * 65 + 96],
                               Pbuf[ei][:, sub * 512:(sub + 1) * 512],
                               j == 0, j == ktiles - 1,
                               waits=w, inc=s_pe if j == ktiles - 1 else None)
                        pv_done[g] = s_pe.n

                    pend = []
                    for g in range(ktiles // 2):
                        bi = g % 2
                        ei = g % 3
                        stk = ps_sT[bi]
                        for sub in range(2):
                            j = 2 * g + sub
                            w = []
                            if sub == 0:
                                w = [(s_sc, sc_head), (s_ve, ve_head)]
                                if g >= 2:
                                    w.append((s_sc, exp_done[g - 2]))
                                else:
                                    w += buf_wait((id(stk), 0)) + buf_wait((id(stk), 512))
                            mm(stk[:, sub * 512:(sub + 1) * 512],
                               khT[:, j * 128:(j + 1) * 128], qhT[:, qsl],
                               True, True, waits=w,
                               inc=s_pe if sub == 1 else None)
                        if g == 1 and pend_z[0] is not None:
                            pend_z[0]()
                            pend_z[0] = None
                        we = [(s_pe, None), (s_ld, ld_base)]
                        if g >= 3:
                            we.append((s_ve, pt_done[g - 3]))
                        act(Ebuf[ei][:], stk[:], AF.Exp, scale=SC_SQRT,
                            waits=we, inc=s_sc)
                        exp_done[g] = s_sc.n
                        last_ev[(id(stk), 0)] = (s_sc, s_sc.n)
                        last_ev[(id(stk), 512)] = (s_sc, s_sc.n)
                        wv = [(s_sc, None)]
                        if g >= 3:
                            wv.append((s_pe, pv_done[g - 3]))
                        vops = [lambda e, o=Pbuf[ei][:], i=Ebuf[ei][:]:
                                e.tensor_scalar(o, i, PLO, PHI, ALU.max, ALU.min)]
                        for sub in range(2):
                            j = 2 * g + sub
                            r = j - 4 * gc
                            if r >= 0:
                                vops.append(
                                    lambda e, o=Pbuf[ei][:, sub * 512:(sub + 1) * 512],
                                    m=masks[:, r * 512:(r + 1) * 512]:
                                    e.tensor_mul(o, o, m))
                        for k, vo in enumerate(vops):
                            P.op("vector", vo, waits=wv if k == 0 else (),
                                 inc=s_ve if k == len(vops) - 1 else None)
                        pt_done[g] = s_ve.n
                        pend.append(g)
                        if len(pend) > 2:
                            emit_pv(pend.pop(0))
                    while pend:
                        emit_pv(pend.pop(0))

                    def z_chain(po=po, h=h, cl=cl, gc=gc):
                        P.op("vector", lambda e, o=zrow[:], i=po[64:65, :]:
                             e.tensor_copy(out=o, in_=i), waits=[(s_pe, None)])
                        P.op("vector", lambda e: e.drain())
                        P.op("vector", lambda e, o=rz[:], i=zrow[:]:
                             e.reciprocal(o, i), inc=s_ve)
                        mm(ps_zb[0:64, :], ones64[:], rz[:], True, True,
                           waits=[(s_ve, None), (s_ld, ld_base)], inc=s_pe)
                        act(o_f32[:], po[0:64, :], AF.Copy, waits=[(s_pe, None)],
                            inc=s_sc)
                        P.op("vector", lambda e,
                             o=ot_sb[64 * (h % 2):64 * (h % 2) + 64,
                                     (h // 2) * 2048 + cl * 512:(h // 2) * 2048 + (cl + 1) * 512],
                             a=o_f32[:], b=ps_zb[0:64, :]:
                             e.tensor_mul(o, a, b),
                             waits=[(s_sc, None), (s_pe, None)], inc=s_ve)
                        o_free[gc % 2] = (s_ve, s_ve.n)

                    pend_z[0] = z_chain

        if pend_z[0] is not None:
            pend_z[0]()
            pend_z[0] = None
        pe_heads = s_pe.n
        ve_heads = s_ve.n
        sth.close()

        stw = ExitStack()
        wot = stw.enter_context(nc.sbuf_tensor("wot_s%d" % half, [128, 8 * D], BF16))
        resid = stw.enter_context(nc.sbuf_tensor("res%d" % half, [128, 2 * D], F32))
        Xb = [stw.enter_context(nc.sbuf_tensor("X%d_%d" % (half, i), [128, D], F32))
              for i in range(2)]
        x1o = [stw.enter_context(nc.sbuf_tensor("x1o%d_%d" % (half, i), [128, D], F32))
               for i in range(2)]
        stats = [stw.enter_context(nc.sbuf_tensor("stt%d_%d" % (half, i), [128, 10], F32))
                 for i in range(2)]

        dma("gpsimd", ch_out(wot, D), ch_in(wot_d), s_gp, waits=[(s_ve, ve_heads)])
        gp_wot = s_gp.n

        add_done = [0, 0]
        scr_done = [0, 0]
        store_done = [0, 0]
        for tt in range(16):
            bi = tt % 2
            row0 = half * 2048 + tt * 128
            dma("sync", resid[:, bi * D:(bi + 1) * D], x_d[row0:row0 + 128, :],
                s_ld,
                waits=[(s_ve, scr_done[bi])] if tt >= 2 else [(s_ve, ve_heads)])
            ld_res = s_ld.n
            pw = ps_sT[bi]
            for ok in range(8):
                for dc in range(2):
                    w = []
                    if ok == 0 and dc == 0:
                        w = [(s_gp, gp_wot), (s_pe, pe_heads), (s_ve, ve_heads)]
                        if tt >= 2:
                            w.append((s_ve, add_done[bi]))
                        else:
                            w += buf_wait((id(pw), 0)) + buf_wait((id(pw), 512))
                    mm(pw[:, dc * 512:(dc + 1) * 512],
                       ot_sb[:, ok * 2048 + tt * 128: ok * 2048 + (tt + 1) * 128],
                       wot[:, ok * D + dc * 512: ok * D + (dc + 1) * 512],
                       ok == 0, ok == 7, waits=w,
                       inc=s_pe if (ok == 7 and dc == 1) else None)
            X = Xb[bi]
            P.op("vector", lambda e, o=X[:], a=pw[:], b=resid[:, bi * D:(bi + 1) * D]:
                 e.tensor_add(o, a, b), waits=[(s_pe, None), (s_ld, ld_res)],
                 inc=s_ve)
            add_done[bi] = s_ve.n
            last_ev[(id(pw), 0)] = (s_ve, s_ve.n)
            last_ev[(id(pw), 512)] = (s_ve, s_ve.n)
            scr, _ = ln_emit(X, stats[bi], x1o[bi], resid[:, bi * D:(bi + 1) * D],
                             store_wait=(s_st, store_done[bi]))
            scr_done[bi] = scr
            P.op("scalar", lambda e, o=x1_d[row0:row0 + 128, :], i=x1o[bi][:]:
                 e.dma_start(out=o, in_=i), waits=[(s_sc, None)],
                 inc=s_st, inc_amt=16)
            store_done[bi] = s_st.n
            x1st_cnt.append(s_st.n)
        stw.close()

    run_half(0)
    x1bf_cnt = [0] * NPASS
    for p in range(4):
        dma("gpsimd", x1bf_d[p * PT_:(p + 1) * PT_, :],
            x1_d[p * PT_:(p + 1) * PT_, :], s_gp,
            waits=[(s_st, x1st_cnt[4 * (p + 1)])])
        x1bf_cnt[p] = s_gp.n
    run_half(1)
    att_st_done = s_st.n
    stA.close()
    for p in range(4, NPASS):
        dma("gpsimd", x1bf_d[p * PT_:(p + 1) * PT_, :],
            x1_d[p * PT_:(p + 1) * PT_, :], s_gp,
            waits=[(s_st, x1st_cnt[4 * (p + 1)])])
        x1bf_cnt[p] = s_gp.n

    stm = ExitStack()
    w1sb = stm.enter_context(nc.sbuf_tensor("w1sb", [128, 8 * DFF], BF16))
    w2sb = stm.enter_context(nc.sbuf_tensor("w2sb", [128, 32 * D], BF16))
    h_sb = stm.enter_context(nc.sbuf_tensor("h_sb", [128, 32 * 512], BF16))
    y_acc = stm.enter_context(nc.sbuf_tensor("y_acc", [128, 4 * D], F32))
    x1t = [stm.enter_context(nc.sbuf_tensor("x1t_s%d" % i, [128, 8 * 512], BF16))
           for i in range(2)]
    lg_t = stm.enter_context(nc.sbuf_tensor("lg_t", [128, 32], F32))
    combine = [stm.enter_context(nc.sbuf_tensor("combine%d" % i, [128, 32], F32))
               for i in range(2)]
    combT = stm.enter_context(nc.sbuf_tensor("combT", [8, 512], BF16))
    logT = stm.enter_context(nc.sbuf_tensor("logT", [8, 512], F32))
    mres = stm.enter_context(nc.sbuf_tensor("mres", [128, D], F32))
    mout = stm.enter_context(nc.sbuf_tensor("mout", [128, D], F32))
    mstat = stm.enter_context(nc.sbuf_tensor("mstat", [128, 10], F32))
    rstat = stm.enter_context(nc.sbuf_tensor("rstat", [128, 32], F32))
    wexp_t = stm.enter_context(nc.sbuf_tensor("wexp_t", [128, 32], F32))
    wsm_t = stm.enter_context(nc.sbuf_tensor("wsm_t", [128, 32], F32))
    b2_sb = stm.enter_context(nc.sbuf_tensor("b2_sb", [NE, D], BF16))
    b1_sb = stm.enter_context(nc.sbuf_tensor("b1_sb", [128, NE * 32], F32))

    ps_h = [psA, psB]
    ps_ya = [ps_sT[0], ps_sT[1]]
    last_ev[(id(psA), 0)] = (s_st, att_st_done)
    last_ev[(id(psB), 0)] = (s_st, att_st_done)

    pass_mm1_end = [0] * (NPASS + 1)
    mm1_done = [s_pe.n]
    mm2_done = [s_pe.n]
    w1_ld = [0]
    w2_ld = [0]

    dma("gpsimd", b2_sb[:], b2_d, s_gp, waits=[(s_st, att_st_done)])
    gp_b2 = s_gp.n
    dma("sync", b1_sb[:], b1t_d, s_ld, waits=[(s_st, att_st_done)])
    dma("sync", ch_out(w1sb, DFF), ch_in(w1bf_d[0]), s_ld,
        waits=[(s_gp, gp_conv[0])])
    w1_ld[-1] = s_ld.n
    dma("sync", ch_out(w2sb, D), ch_in(w2bf_d[0]), s_ld)
    w2_ld[-1] = s_ld.n
    ld_moe_consts = s_ld.n

    sm_done = [0]

    def mk_router(p, ld_target=None):
        cmb = combine[p % 2]
        xt = x1t[p % 2]
        rt = ps_ya[0]

        def r_mms():
            for dk in range(8):
                w = [(s_ld, ld_target), (s_ve, None), (s_gp, gp_proj)] if dk == 0 else []
                mm(rt[0:8, 0:512], wg_sb[:, dk * 8:(dk + 1) * 8],
                   xt[:, dk * 512:(dk + 1) * 512], dk == 0, dk == 7,
                   waits=w, inc=s_pe if dk == 7 else None)
            act(logT[:], rt[0:8, 0:512], AF.Identity, bias=bg_sb[:],
                waits=[(s_pe, None), (s_ld, ld_moe_consts)], inc=s_sc)

        def r_lt(q4):
            P.op("tensor", lambda e, o=rt[:, 512:520],
                 i=logT[:, q4 * 128:(q4 + 1) * 128], idt=id8[:]:
                 e.transpose(o, i, idt), waits=[(s_sc, None), (s_ve, None)],
                 inc=s_pe)
            act(lg_t[:, q4 * 8:(q4 + 1) * 8], rt[:, 512:520], AF.Copy,
                waits=[(s_pe, None)], inc=s_sc)

        def r_sm():
            LG = [lg_t[:, q * 8:(q + 1) * 8] for q in range(4)]
            CM = [cmb[:, q * 8:(q + 1) * 8] for q in range(4)]
            WX = [wexp_t[:, q * 8:(q + 1) * 8] for q in range(4)]
            WS = [wsm_t[:, q * 8:(q + 1) * 8] for q in range(4)]
            STc = lambda j, q: rstat[:, 4 * j + q:4 * j + q + 1]
            dr = lambda: P.op("vector", lambda e: e.drain())
            for q in range(4):
                P.op("vector", lambda e, o=STc(0, q), i=LG[q]:
                     e.reduce_max(o, i, axis=AX.X, negate=True),
                     waits=[(s_sc, None)] if q == 0 else (),
                     inc=s_ve if q == 3 else None)
            for q in range(4):
                act(WX[q], LG[q], AF.Exp, bias=STc(0, q),
                    waits=[(s_ve, None)] if q == 0 else (),
                    inc=s_sc if q == 3 else None)
            for q in range(4):
                P.op("vector", lambda e, o=STc(1, q), i=WX[q]:
                     e.reduce_sum(o, i, axis=AX.X),
                     waits=[(s_sc, None)] if q == 0 else ())
            dr()
            for q in range(4):
                P.op("vector", lambda e, o=STc(2, q), i=STc(1, q): e.reciprocal(o, i))
            dr()
            for q in range(4):
                P.op("vector", lambda e, o=WS[q], i=WX[q], sc=STc(2, q):
                     e.tensor_scalar(o, i, sc, None, ALU.mult))
            dr()
            for q in range(4):
                P.op("vector", lambda e, o=STc(3, q), i=WS[q]:
                     e.reduce_max(o, i, axis=AX.X))
            dr()
            for q in range(4):
                P.op("vector", lambda e, o=CM[q], i=WS[q], sc=STc(3, q):
                     e.tensor_scalar(o, i, sc, None, ALU.is_lt))
            dr()
            for q in range(4):
                P.op("vector", lambda e, o=CM[q], a=WS[q], b=CM[q]: e.tensor_mul(o, a, b))
            dr()
            for q in range(4):
                P.op("vector", lambda e, o=STc(4, q), i=CM[q]:
                     e.reduce_max(o, i, axis=AX.X))
            dr()
            for q in range(4):
                P.op("vector", lambda e, o=STc(5, q), a=STc(3, q), b=STc(4, q):
                     e.tensor_add(o, a, b))
            for q in range(4):
                P.op("vector", lambda e, o=CM[q], i=WS[q], sc=STc(4, q):
                     e.tensor_scalar(o, i, sc, None, ALU.is_ge))
            dr()
            for q in range(4):
                P.op("vector", lambda e, o=STc(6, q), i=STc(5, q): e.reciprocal(o, i))
            for q in range(4):
                P.op("vector", lambda e, o=CM[q], a=WS[q], b=CM[q]: e.tensor_mul(o, a, b))
            dr()
            for q in range(4):
                P.op("vector", lambda e, o=CM[q], i=CM[q], sc=STc(6, q):
                     e.tensor_scalar(o, i, sc, None, ALU.mult),
                     inc=s_ve if q == 3 else None)
            sm_done[0] = s_ve.n

        def r_ct(q4):
            P.op("tensor", lambda e, o=rt[0:8, 512:640],
                 i=cmb[:, q4 * 8:(q4 + 1) * 8], idt=id128[:]:
                 e.transpose(o, i, idt),
                 waits=[(s_ve, None), (s_sc, None)], inc=s_pe)
            act(combT[:, q4 * 128:(q4 + 1) * 128], rt[0:8, 512:640], AF.Copy,
                waits=[(s_pe, None)], inc=s_sc)

        return r_mms, r_lt, r_sm, r_ct

    def emit_yinit(ct_done, ya_free, scr_gate):
        rt = ps_ya[0]
        for tm in range(4):
            for dc in range(2):
                w = ()
                if dc == 0:
                    w = [(s_sc, ct_done), (s_gp, gp_b2), (s_ve, ya_free)]
                    if tm > 0:
                        w.append((s_sc, None))
                mm(rt[:, dc * 512:(dc + 1) * 512],
                   combT[:, tm * 128:(tm + 1) * 128],
                   b2_sb[:, dc * 512:(dc + 1) * 512], True, True,
                   waits=w, inc=s_pe if dc == 1 else None)
            act(y_acc[:, tm * D:(tm + 1) * D], rt[:], AF.Copy,
                waits=[(s_pe, None), (s_ve, scr_gate)], inc=s_sc)
        return s_sc.n

    ve_yacc_scr = [0]

    def emit_pass_ln(p, gate_a, gate_b):
        for tm in range(4):
            row0 = p * PT_ + tm * 128
            dma("gpsimd", mres[:], x1_d[row0:row0 + 128, :], s_gp,
                waits=[(s_st, None)])
            g = gate_a if tm < 2 else gate_b
            P.op("vector", lambda e, o=mout[:], a=y_acc[:, tm * D:(tm + 1) * D],
                 b=mres[:]:
                 e.tensor_add(o, a, b), waits=[(s_gp, None), (s_ve, g)],
                 inc=s_ve)
            scr, _ = ln_emit(mout, mstat, mres, y_acc[:, tm * D:(tm + 1) * D])
            ve_yacc_scr[0] = scr
            P.op("scalar", lambda e, o=out_d[row0:row0 + 128, :], i=mres[:]:
                 e.dma_start(out=o, in_=i), waits=[(s_sc, None)],
                 inc=s_st, inc_amt=16)

    for k in range(8):
        w = [(s_gp, x1bf_cnt[0]), (s_st, att_st_done)] if k == 0 else []
        dma("sync", x1t[0][:, k * 512:(k + 1) * 512],
            x1bf_d[0:PT_, k * 128:(k + 1) * 128], s_ld, waits=w, transpose=True)
    r = mk_router(0, s_ld.n)
    r[0]()
    for q4 in range(4):
        r[1](q4)
    r[2]()
    for q4 in range(4):
        r[3](q4)
    ct_done = [s_sc.n]
    ya_free = [0]
    rt_guard = [(s_sc, 0)]
    nxt = [None]

    for p in range(NPASS):
        if p > 0:
            dma("sync", ch_out(w1sb, DFF), ch_in(w1bf_d[0]),
                s_ld, waits=[(s_pe, mm1_done[-1])])
            w1_ld.append(s_ld.n)
            dma("sync", ch_out(w2sb, D), ch_in(w2bf_d[0]),
                s_ld, waits=[(s_pe, mm2_done[-1])])
            w2_ld.append(s_ld.n)
        if p + 1 < NPASS:
            for k in range(8):
                w = []
                if k == 0:
                    w = [(s_gp, x1bf_cnt[p + 1]), (s_st, att_st_done)]
                    if p >= 1:
                        w.append((s_pe, pass_mm1_end[p - 1]))
                dma("sync", x1t[(p + 1) % 2][:, k * 512:(k + 1) * 512],
                    x1bf_d[(p + 1) * PT_:(p + 2) * PT_, k * 128:(k + 1) * 128],
                    s_ld, waits=w, transpose=True)
            nxt[0] = mk_router(p + 1, s_ld.n)
        else:
            nxt[0] = None

        xt_cur = x1t[p % 2]
        sc_yinit = emit_yinit(ct_done[0], ya_free[0], ve_yacc_scr[0])

        for e_ in range(NE):
            if e_ > 0:
                dma("sync", ch_out(w1sb, DFF), ch_in(w1bf_d[e_]),
                    s_ld, waits=[(s_pe, mm1_done[-1])])
                w1_ld.append(s_ld.n)
                dma("sync", ch_out(w2sb, D), ch_in(w2bf_d[e_]),
                    s_ld, waits=[(s_pe, mm2_done[-1])])
                w2_ld.append(s_ld.n)
            for fm in range(32):
                buf = ps_h[fm % 2]
                key = (id(buf), 0)
                for dk in range(8):
                    w = []
                    if dk == 0:
                        w = [(s_ld, w1_ld[-1])] + buf_wait(key)
                    mm(buf[:], w1sb[:, dk * DFF + fm * 128: dk * DFF + (fm + 1) * 128],
                       xt_cur[:, dk * 512:(dk + 1) * 512], dk == 0, dk == 7,
                       waits=w, inc=s_pe if dk == 7 else None)
                act(h_sb[:, fm * 512:(fm + 1) * 512], buf[:], AF.Relu,
                    bias=b1_sb[:, e_ * 32 + fm: e_ * 32 + fm + 1],
                    waits=[(s_pe, None), (s_ld, ld_moe_consts)], inc=s_h2)
                last_ev[key] = (s_h2, s_h2.n)
                if nxt[0] is not None:
                    if e_ == 3:
                        if fm == 2:
                            nxt[0][0]()
                        elif fm in (8, 10, 12, 14):
                            nxt[0][1]((fm - 8) // 2)
                            if fm == 14:
                                rt_guard[0] = (s_sc, s_sc.n)
                        elif fm == 16:
                            nxt[0][2]()
                    elif e_ == 4 and fm in (8, 10, 12, 14):
                        nxt[0][3]((fm - 8) // 2)
                        if fm == 14:
                            ct_done[0] = s_sc.n
                            rt_guard[0] = (s_sc, s_sc.n)
            mm1_done.append(s_pe.n)
            if e_ == NE - 1:
                pass_mm1_end[p] = s_pe.n
            sc_h = s_h2.n
            for half2 in range(2):
                for fk in range(32):
                    for tmi in range(2):
                        tm = half2 * 2 + tmi
                        for dc in range(2):
                            w = []
                            if fk == 0 and tmi == 0 and dc == 0:
                                w = [(s_ld, w2_ld[-1]), (s_h2, sc_h), (s_ve, None),
                                     rt_guard[0]]
                            mm(ps_ya[tmi][:, dc * 512:(dc + 1) * 512],
                               h_sb[:, fk * 512 + tm * 128: fk * 512 + (tm + 1) * 128],
                               w2sb[:, fk * D + dc * 512: fk * D + (dc + 1) * 512],
                               fk == 0, fk == 31, waits=w,
                               inc=s_pe if (fk == 31 and tmi == 1 and dc == 1) else None)
                for tmi in range(2):
                    tm = half2 * 2 + tmi
                    P.op("vector", lambda e, o=y_acc[:, tm * D:(tm + 1) * D],
                         a=ps_ya[tmi][:],
                         s=combine[p % 2][:, tm * 8 + e_: tm * 8 + e_ + 1]:
                         e.scalar_tensor_tensor(o, a, s, o, ALU.mult, ALU.add),
                         waits=[(s_pe, None), (s_sc, sc_yinit)] if tmi == 0 else [],
                         inc=s_ve)
                if half2 == 0:
                    gate_a = s_ve.n
                else:
                    gate_b = s_ve.n
            mm2_done.append(s_pe.n)
        ya_free[0] = s_ve.n
        emit_pass_ln(p, gate_a, gate_b)

    P.op("sync", lambda e: e.nop(), waits=[(s_st, s_st.n)])

    with nc.Block() as block:
        @block.sync
        def _(eng):
            for fn in P.ops["sync"]:
                fn(eng)

        @block.gpsimd
        def _(eng):
            for fn in P.ops["gpsimd"]:
                fn(eng)

        @block.vector
        def _(eng):
            for fn in P.ops["vector"]:
                fn(eng)

        @block.scalar
        def _(eng):
            for fn in P.ops["scalar"]:
                fn(eng)

        @block.tensor
        def _(eng):
            for fn in P.ops["tensor"]:
                fn(eng)

    stm.close()
    st.close()
    return nc


def _consts_np():
    inv_freq = 10000.0 ** (-np.arange(0, DHR, 2, dtype=np.float64) / DHR)
    t = np.arange(S, dtype=np.float64)
    ang = inv_freq[:, None] * t[None, :]
    C = np.repeat(np.cos(ang), 2, axis=0).astype(np.float32)
    sin = np.sin(ang)
    Spm = np.empty((DHR, S), np.float64)
    Spm[0::2] = -sin
    Spm[1::2] = sin
    masks = np.zeros((128, 4 * 512), np.float32)
    jj = np.arange(128)[:, None]
    ii = np.arange(512)[None, :]
    for r in range(4):
        masks[:, r * 512:(r + 1) * 512] = (jj + 128 * r <= ii).astype(np.float32)
    consts = np.zeros((128, 4), np.float32)
    consts[:, 0] = 1.0 / np.sqrt(96.0)
    consts[:, 1] = 1e-5
    consts[:, 2] = -0.5
    consts[:, 3] = 1.0 / D
    bf = ml_dtypes.bfloat16
    CS = np.concatenate([C, Spm.astype(np.float32)], axis=0)
    return (CS.astype(bf), masks.astype(bf), consts)


def _cat_swap(W, blk):
    n = W.shape[0] // blk
    out = np.empty((n * 2 * blk, W.shape[1]), W.dtype)
    perm = np.arange(blk) ^ 1
    for i in range(n):
        out[i * 2 * blk: i * 2 * blk + blk] = W[i * blk:(i + 1) * blk]
        out[i * 2 * blk + blk: (i + 1) * 2 * blk] = W[i * blk:(i + 1) * blk][perm]
    return out


def kernel(L0_attn_x, L0_attn_W_DQ, L0_attn_W_UQ, L0_attn_W_QR, L0_attn_W_DKV,
           L0_attn_W_UK, L0_attn_W_UV, L0_attn_W_KR, L0_attn_W_O,
           L0_ffn_W1, L0_ffn_b1, L0_ffn_W2, L0_ffn_b2, L0_ffn_Wg, L0_ffn_bg):
    nc = bass.Bass("TRN2", target_bir_lowering=False)
    build(nc)

    CS, masks, consts = _consts_np()
    asc = np.ascontiguousarray
    wqrc = _cat_swap(L0_attn_W_QR, DHR)
    wkrc = _cat_swap(L0_attn_W_KR, DHR)
    common = {
        "wdqt": asc(L0_attn_W_DQ.T), "wdkvt": asc(L0_attn_W_DKV.T),
        "wkrct": asc(wkrc.T), "wuqt": asc(L0_attn_W_UQ.T),
        "wqrct": asc(wqrc.T), "wukt": asc(L0_attn_W_UK.T),
        "wuvt": asc(L0_attn_W_UV.T), "wot": asc(L0_attn_W_O.T),
        "w1": asc(L0_ffn_W1), "w2": asc(L0_ffn_W2),
        "b1t": asc(L0_ffn_b1.reshape(NE, 32, 128).transpose(2, 0, 1).reshape(128, NE * 32)),
        "b2": asc(L0_ffn_b2),
        "wgt": asc(L0_ffn_Wg.reshape(8, 128, 8).transpose(1, 0, 2).reshape(128, 64)),
        "bg": asc(L0_ffn_bg.reshape(NE, 1)),
        "ropecs": CS, "masks": masks,
        "id8": np.eye(8, dtype=np.float32), "id128": np.eye(128, dtype=np.float32),
        "ones64": np.ones((1, 64), np.float32), "consts": consts,
    }
    in_maps = []
    for c in range(8):
        m = dict(common)
        xc = asc(np.asarray(L0_attn_x[c], dtype=np.float32))
        m["x"] = xc
        m["xT"] = asc(xc.T)
        in_maps.append(m)
    res = run_bass_kernel_spmd(nc, in_maps, core_ids=list(range(8)))
    return np.stack([np.asarray(r["out"], dtype=np.float32) for r in res.results])
